# version 13
# baseline (speedup 1.0000x reference)
import sys
sys.path.insert(0, "/opt/trn_rl_repo")
import os
import numpy as np
import concourse.bass as bass
import concourse.mybir as mybir
from concourse.bass_utils import run_bass_kernel_spmd
from contextlib import ExitStack

F32 = mybir.dt.float32
BF16 = mybir.dt.bfloat16
AX = mybir.AxisListType.X
AF = mybir.ActivationFunctionType
ALU = mybir.AluOpType
AP = bass.AP

NCORE = 8
T = 1024
C = 32
D = 4096

CN = {"inv4096": 0, "eps": 1, "neg1": 2, "two": 3, "c15": 4, "big": 5, "eighth": 6,
      "eps9": 7, "half": 8, "one": 9, "neg2": 10}
CONST_VALS = [1.0 / 4096, 1e-5, -1.0, 2.0, 1.5, 1e9, 0.125, 1e-9, 0.5, 1.0, -2.0]

_PHASES = int(os.environ.get("KPHASES", "4"))
_KDBG = int(os.environ.get("KDBG", "0"))


class Sem:
    def __init__(self, h):
        self.h = h
        self.n = 0
    def inc(self, instr, k=1):
        instr.then_inc(self.h, k)
        self.n += k
        return self.n


class AutoDrain:
    _OPS = {"tensor_copy", "tensor_tensor", "tensor_scalar", "scalar_tensor_tensor",
            "reduce_max", "reduce_sum", "reciprocal", "memset", "activation", "copy",
            "tensor_scalar_add", "tensor_scalar_mul", "select"}
    _CACHE = {}
    def __new__(cls, eng):
        if isinstance(eng, AutoDrain):
            return eng
        key = id(eng)
        if key in cls._CACHE:
            return cls._CACHE[key]
        obj = super().__new__(cls)
        obj._eng = eng
        obj._last_out = ((), ())
        obj._last_out2 = ((), ())
        cls._CACHE[key] = obj
        return obj
    def __init__(self, eng):
        pass
    @staticmethod
    def _tname(ap):
        try:
            return ap.tensor.name
        except Exception:
            return None
    def _handle(self, name, kwargs):
        ins = []
        for k in ("in_", "in0", "in1", "scalar1", "scalar2", "scalar", "bias", "scale", "mask",
                  "on_true", "on_false", "ap"):
            v = kwargs.get(k)
            n = self._tname(v) if v is not None and not isinstance(v, (int, float)) else None
            if n:
                ins.append(n)
        outs = []
        for k in ("out", "accum_out", "ap"):
            v = kwargs.get(k)
            n = self._tname(v) if v is not None and not isinstance(v, (int, float)) else None
            if n:
                outs.append(n)
        hazard = False
        for (po, pi) in (self._last_out, self._last_out2):
            if any(n in po for n in ins) or any(n in po or n in pi for n in outs):
                hazard = True
        if hazard:
            self._eng.drain()
            self._last_out2 = ((), ())
        else:
            self._last_out2 = self._last_out
        self._last_out = (tuple(outs), tuple(ins))
    def __getattr__(self, name):
        attr = getattr(self._eng, name)
        if name in AutoDrain._OPS and callable(attr):
            def wrapped(*a, **kw):
                if a and name == "memset":
                    kw2 = dict(kw)
                    kw2["ap"] = a[0]
                    self._handle(name, kw2)
                    return attr(*a, **kw)
                if a:
                    raise TypeError("AutoDrain ops must use kwargs only: " + name)
                self._handle(name, kw)
                return attr(**kw)
            return wrapped
        return attr


def _host_tables():
    L = 4096
    kk = np.arange(L)
    theta32 = (np.float32(-2 * np.pi) * (kk / L).astype(np.float32)).astype(np.float32)
    th = theta32.astype(np.float64)
    om = np.cos(th) + 1j * np.sin(th)
    u_c = (1 - om) / (1 + om)
    c_c = 2.0 / (1 + om)
    f32 = np.float32
    ucat = np.concatenate([np.tile(u_c.real.astype(f32)[None, :], (64, 1)),
                           np.tile(u_c.imag.astype(f32)[None, :], (64, 1))], axis=0)
    cre = c_c.real.astype(f32).reshape(32, 128).T.copy()
    cim = c_c.imag.astype(f32).reshape(32, 128).T.copy()
    t_idx = np.arange(32)
    rho = np.arange(128)
    ang_tr = 2 * np.pi * np.outer(t_idx, rho) / 32
    M1 = np.concatenate([np.cos(ang_tr), -np.sin(ang_tr)], axis=0).astype(f32)
    M2 = np.concatenate([np.sin(ang_tr), np.cos(ang_tr)], axis=0).astype(f32)
    pp = np.arange(128)
    ang_pr = 2 * np.pi * np.outer(pp, rho) / 4096
    TW1 = np.cos(ang_pr).astype(f32)
    TW2 = np.sin(ang_pr).astype(f32)
    tau = np.arange(32)
    ang_pt = 2 * np.pi * np.outer(pp, tau) / 32
    C1 = (np.cos(ang_pt) / 4096).astype(f32)
    C2 = (-np.sin(ang_pt) / 4096).astype(f32)
    ident = np.eye(128, dtype=f32)
    mask = np.where(np.tril(np.ones((128, 128), bool)), 0.0, -1e9).astype(f32)
    return dict(t_ucat=ucat, t_cre=cre, t_cim=cim, t_M1=M1, t_M2=M2, t_TW1=TW1,
                t_TW2=TW2, t_C1=C1, t_C2=C2, t_ident=ident, t_mask=mask)


def build_nc():
    nc = bass.Bass("TRN2", target_bir_lowering=False)
    I = {}
    def di(name, shape):
        I[name] = nc.dram_tensor(name, list(shape), F32, kind="ExternalInput").ap()
    di("x0T", [D, T])
    di("aLam", [64, 2]); di("aP", [64, 2]); di("aQ", [64, 2]); di("aVc", [64, 64, 2])
    di("aCt", [1, 64, 2]); di("aBpil", [128, 2]); di("aD", [1, 1]); di("aLS", [1, 1])
    di("cwsh", [3, 512, 3072]); di("cb", [3072])
    di("fWsh", [512, D])
    di("wqsh", [128, 1024]); di("wksh", [128, 1024]); di("wvsh", [128, 1024]); di("wosh", [128, 1024])
    di("pwsh", [128, D])
    di("bq", [16, 64]); di("bk", [16, 64]); di("bv", [16, 64]); di("bo", [1024])
    di("pb", [D])
    di("fb", [D]); di("fA", [D, 32]); di("fB", [32, D]); di("fL", [8, 32])
    di("oneL", [8, 1]); di("oneR", [8, 1])
    for n, s in [("t_ucat", [128, 4096]), ("t_cre", [128, 32]), ("t_cim", [128, 32]),
                 ("t_M1", [64, 128]), ("t_M2", [64, 128]), ("t_TW1", [128, 128]),
                 ("t_TW2", [128, 128]), ("t_C1", [128, 32]), ("t_C2", [128, 32]),
                 ("t_ident", [128, 128]), ("t_mask", [128, 128])]:
        di(n, s)
    OUT = nc.dram_tensor("y", [D, T], F32, kind="ExternalOutput").ap()
    DBG = nc.dram_tensor("dbg", [128, 4096], F32, kind="ExternalOutput").ap() if _KDBG else None

    kflat_h = nc.dram_tensor("kflat_h", [4096], BF16).ap()
    scr_h = nc.dram_tensor("scr_h", [128 * 4608 + 256], BF16).ap()
    agw_in = nc.dram_tensor("agw_in", [4 * 128 * 4096], BF16).ap()
    agw_out = nc.dram_tensor("agw_out", [NCORE * 4 * 128 * 4096], BF16).ap()
    agc_in = nc.dram_tensor("agc_in", [12 * 128 * 3072], BF16).ap()
    agc_out = nc.dram_tensor("agc_out", [NCORE * 12 * 128 * 3072], BF16).ap()
    agq_in = nc.dram_tensor("agq_in", [128 * 1024], BF16).ap()
    agq_out = nc.dram_tensor("agq_out", [NCORE * 128 * 1024], BF16).ap()
    agk_in = nc.dram_tensor("agk_in", [128 * 1024], BF16).ap()
    agk_out = nc.dram_tensor("agk_out", [NCORE * 128 * 1024], BF16).ap()
    agv_in = nc.dram_tensor("agv_in", [128 * 1024], BF16).ap()
    agv_out = nc.dram_tensor("agv_out", [NCORE * 128 * 1024], BF16).ap()
    ago_in = nc.dram_tensor("ago_in", [128 * 1024], BF16).ap()
    ago_out = nc.dram_tensor("ago_out", [NCORE * 128 * 1024], BF16).ap()
    agp_in = nc.dram_tensor("agp_in", [128 * 4096], BF16).ap()
    agp_out = nc.dram_tensor("agp_out", [NCORE * 128 * 4096], BF16).ap()
    ar1_in = nc.dram_tensor("ar1_in", [1, 2], F32).ap()
    ar1_out = nc.dram_tensor("ar1_out", [1, 2], F32).ap()
    ar2_in = nc.dram_tensor("ar2_in", [1, 2], F32).ap()
    ar2_out = nc.dram_tensor("ar2_out", [1, 2], F32).ap()
    agh_in = nc.dram_tensor("agh_in", [2 * 4096], BF16).ap()
    agh_out = nc.dram_tensor("agh_out", [16 * 4096], BF16).ap()
    x2_h = nc.dram_tensor("x2_h", [D, T], F32).ap()
    hfl_h = nc.dram_tensor("hfl_h", [512], BF16).ap()

    st = ExitStack()
    e = st.enter_context
    xt_st = ExitStack()
    xt = xt_st.enter_context(nc.sbuf_tensor("xt", [128, C * T], F32, side="right"))

    consts = e(nc.sbuf_tensor("consts", [128, 16], F32))
    ones1f = e(nc.sbuf_tensor("ones1f", [1, 128], F32))
    ones1b = e(nc.sbuf_tensor("ones1b", [1, 128], BF16))
    onescb = e(nc.sbuf_tensor("onescb", [128, 1], BF16))
    identf = e(nc.sbuf_tensor("identf", [128, 128], F32))
    identb = e(nc.sbuf_tensor("identb", [128, 128], BF16))
    maskb = e(nc.sbuf_tensor("maskb", [128, 128], BF16))
    DIb = e(nc.sbuf_tensor("DIb", [128, 128], BF16))
    bcol = e(nc.sbuf_tensor("bcol", [128, 32], F32))
    cbcol = e(nc.sbuf_tensor("cbcol", [128, 24], F32))
    bqcol = e(nc.sbuf_tensor("bqcol", [128, 8], F32))
    bkcol = e(nc.sbuf_tensor("bkcol", [128, 8], F32))
    bocol = e(nc.sbuf_tensor("bocol", [128, 8], F32))
    pbcol = e(nc.sbuf_tensor("pbcol", [128, 32], F32))
    bvrow = e(nc.sbuf_tensor("bvrow", [1, 1024], BF16))
    limes_b = e(nc.sbuf_tensor("limes_b", [8, 32], BF16))
    liT = e(nc.sbuf_tensor("liT", [32, 8], F32))
    onelb = e(nc.sbuf_tensor("onelb", [8, 1], BF16))
    onerb = e(nc.sbuf_tensor("onerb", [8, 1], BF16))
    t2s_c = e(nc.sbuf_tensor("t2s_c", [128, 1], F32))
    D_c = e(nc.sbuf_tensor("D_c", [128, 1], F32))
    arin_s = e(nc.sbuf_tensor("arin_s", [1, 2], F32))
    armax_s = e(nc.sbuf_tensor("armax_s", [1, 2], F32))
    hsds = e(nc.sbuf_tensor("hsds", [128, 2], F32))
    srow = e(nc.sbuf_tensor("srow", [1, 512], F32))
    brow = e(nc.sbuf_tensor("brow", [1, 512], F32))
    s_bc = [e(nc.sbuf_tensor(f"s_bc{g}", [128, 512], F32)) for g in range(2)]
    b_bc = [e(nc.sbuf_tensor(f"b_bc{g}", [128, 512], F32)) for g in range(2)]
    xbb = e(nc.sbuf_tensor("xbb", [128, 1024], BF16))
    sqb2 = e(nc.sbuf_tensor("sqb2", [128, 1024], BF16))
    t1f = e(nc.sbuf_tensor("t1f", [128, 512], F32))

    pf = [e(nc.psum_tensor(f"pf{i}", [128, 512], F32)) for i in range(6)]
    pbx = [e(nc.psum_tensor(f"pbx{i}", [128, 1024], BF16)) for i in range(2)]

    _sems = {}
    def QQ(name):
        if name not in _sems:
            _sems[name] = Sem(e(nc.semaphore("q_" + name)))
        return _sems[name]
    ct = Sem(e(nc.semaphore("ct")))
    cv = Sem(e(nc.semaphore("cv")))
    cs = Sem(e(nc.semaphore("cs")))
    block = e(nc.Block())

    cp = {}
    def mark(name, sem):
        cp[name] = (sem, sem.n)
    def wt(eng, name):
        s, v = cp[name]
        if v > 0:
            eng.wait_ge(s.h, v)

    @block.sync
    def _(s):
        s = AutoDrain(s)
        for c in range(C):
            QQ("x0").inc(s.dma_start(out=xt[:, c * T:(c + 1) * T], in_=I["x0T"][c * 128:(c + 1) * 128, :]), 16)
    mark("x0_loaded", QQ("x0"))

    Lcat = e(nc.sbuf_tensor("Lcat", [128, 1], F32))
    g1 = ExitStack()
    ge = g1.enter_context
    Lnat = ge(nc.sbuf_tensor("Lnat", [64, 2], F32))
    rows6 = ge(nc.sbuf_tensor("rows6", [1, 384], F32))
    Bprow = ge(nc.sbuf_tensor("Bprow", [128, 2], F32))
    Vcnat = ge(nc.sbuf_tensor("Vcnat", [64, 128], F32))
    Vt = ge(nc.sbuf_tensor("Vt", [128, 64], F32))
    lsD = ge(nc.sbuf_tensor("lsD", [1, 2], F32))
    m2s = ge(nc.sbuf_tensor("m2s", [1, 2], F32))
    bnat = ge(nc.sbuf_tensor("bnat", [32, 128], F32))
    cbnat = ge(nc.sbuf_tensor("cbnat", [24, 128], F32))
    pbnat = ge(nc.sbuf_tensor("pbnat", [32, 128], F32))
    bqnat = ge(nc.sbuf_tensor("bqnat", [8, 128], F32))
    bknat = ge(nc.sbuf_tensor("bknat", [8, 128], F32))
    bonat = ge(nc.sbuf_tensor("bonat", [8, 128], F32))
    limes_f = ge(nc.sbuf_tensor("limes_f", [8, 32], F32))
    sh1 = ge(nc.sbuf_tensor("sh1", [128, 4096], BF16))
    wstg = ge(nc.sbuf_tensor("wstg", [128, 4096], BF16))

    tbl = {}
    kst = ExitStack()
    ke = kst.enter_context
    for nm, shp in [("t_cre", [128, 32]), ("t_cim", [128, 32]), ("t_M1", [64, 128]),
                    ("t_M2", [64, 128]), ("t_TW1", [128, 128]), ("t_TW2", [128, 128]),
                    ("t_C1", [128, 32]), ("t_C2", [128, 32])]:
        tbl[nm] = ke(nc.sbuf_tensor("s" + nm, shp, F32))

    @block.sync
    def _(s):
        s = AutoDrain(s)
        QQ("tbl").inc(s.dma_start(out=identf[:, :], in_=I["t_ident"]), 16)
        for nm in tbl:
            QQ("tbl").inc(s.dma_start(out=tbl[nm][:, :], in_=I[nm]), 16)
    mark("tables_loaded", QQ("tbl"))

    @block.gpsimd
    def _(g):
        g = AutoDrain(g)
        QQ("small").inc(g.dma_start(out=Lnat[:, :], in_=I["aLam"]), 16)
        for j, (nm, off) in enumerate([("aCt", 0), ("aCt", 1), ("aP", 0), ("aP", 1), ("aQ", 0), ("aQ", 1)]):
            src = AP(tensor=I[nm].tensor, offset=off, ap=[[2, 64]])
            QQ("small").inc(g.dma_start(out=rows6[0:1, j * 64:(j + 1) * 64], in_=src,
                               allow_slow_non_contiguous=True), 16)
        QQ("small").inc(g.dma_start(out=Bprow[:, :], in_=I["aBpil"]), 16)
        QQ("small").inc(g.dma_start(out=Vcnat[:, :], in_=AP(tensor=I["aVc"].tensor, offset=0,
                                                   ap=[[128, 64], [1, 128]])), 16)
        QQ("small").inc(g.dma_start(out=lsD[0:1, 0:1], in_=I["aLS"]), 16)
        QQ("small").inc(g.dma_start(out=lsD[0:1, 1:2], in_=I["aD"]), 16)
        QQ("small").inc(g.dma_start(out=bnat[:, :], in_=AP(tensor=I["fb"].tensor, offset=0, ap=[[128, 32], [1, 128]])), 16)
        QQ("small").inc(g.dma_start(out=cbnat[:, :], in_=AP(tensor=I["cb"].tensor, offset=0, ap=[[128, 24], [1, 128]])), 16)
        QQ("small").inc(g.dma_start(out=pbnat[:, :], in_=AP(tensor=I["pb"].tensor, offset=0, ap=[[128, 32], [1, 128]])), 16)
        QQ("small").inc(g.dma_start(out=bqnat[:, :], in_=AP(tensor=I["bq"].tensor, offset=0, ap=[[128, 8], [1, 128]])), 16)
        QQ("small").inc(g.dma_start(out=bknat[:, :], in_=AP(tensor=I["bk"].tensor, offset=0, ap=[[128, 8], [1, 128]])), 16)
        QQ("small").inc(g.dma_start(out=bonat[:, :], in_=AP(tensor=I["bo"].tensor, offset=0, ap=[[128, 8], [1, 128]])), 16)
        QQ("small").inc(g.dma_start(out=bvrow[0:1, :], in_=AP(tensor=I["bv"].tensor, offset=0, ap=[[1, 1024]])), 16)
        QQ("small").inc(g.dma_start(out=limes_f[:, :], in_=I["fL"]), 16)
        QQ("small").inc(g.dma_start(out=identb[:, :], in_=I["t_ident"]), 16)
        QQ("small").inc(g.dma_start(out=maskb[:, :], in_=I["t_mask"]), 16)
        QQ("small").inc(g.dma_start(out=onelb[:, :], in_=I["oneL"]), 16)
        QQ("small").inc(g.dma_start(out=onerb[:, :], in_=I["oneR"]), 16)
    mark("small_loaded", QQ("small"))

    @block.vector
    def _(v):
        v = AutoDrain(v)
        for j, val in enumerate(CONST_VALS):
            v.memset(consts[:, j:j + 1], val)
        v.memset(ones1f[:, :], 1.0)
        v.memset(ones1b[:, :], 1.0)
        cv.inc(v.memset(onescb[:, :], 1.0))
    mark("consts_ready", cv)

    @block.gpsimd
    def _(g):
        g = AutoDrain(g)
        wt(g, "small_loaded")
        QQ("lcat").inc(g.dma_start(out=Lcat[0:64, 0:1], in_=Lnat[:, 0:1]), 16)
        QQ("lcat").inc(g.dma_start(out=Lcat[64:128, 0:1], in_=Lnat[:, 1:2]), 16)
    mark("lcat_ready", QQ("lcat"))

    @block.scalar
    def _(s):
        s = AutoDrain(s)
        wt(s, "small_loaded")
        cs.inc(s.activation(out=lsD[0:1, 0:1], in_=lsD[0:1, 0:1], func=AF.Exp))
    mark("step_ready", cs)

    @block.vector
    def _(v):
        v = AutoDrain(v)
        wt(v, "step_ready")
        wt(v, "consts_ready")
        v.reciprocal(out=m2s[0:1, 0:1], in_=lsD[0:1, 0:1])
        cv.inc(v.tensor_scalar(out=m2s[0:1, 0:1], in0=m2s[0:1, 0:1],
                               scalar1=consts[0:1, CN["two"]:CN["two"] + 1], scalar2=None, op0=ALU.mult))
    mark("m2s_ready", cv)

    @block.tensor
    def _(t):
        wt(t, "m2s_ready")
        wt(t, "tables_loaded")
        t.matmul(pf[0][:, 0:1], ones1f[0:1, :], m2s[0:1, 0:1], start=True, stop=True)
        t.matmul(pf[0][:, 1:2], ones1f[0:1, :], lsD[0:1, 1:2], start=True, stop=True)
        ct.inc(t.transpose(pf[1][0:128, 0:64], Vcnat[:, :], identf[0:64, 0:64]))
    mark("bcast1_done", ct)

    @block.vector
    def _(v):
        v = AutoDrain(v)
        wt(v, "bcast1_done")
        v.tensor_copy(out=t2s_c[:, :], in_=pf[0][:, 0:1])
        v.tensor_copy(out=D_c[:, :], in_=pf[0][:, 1:2])
        v.tensor_copy(out=Vt[:, :], in_=pf[1][:, 0:64])
        cv.inc(v.tensor_scalar(out=DIb[:, :], in0=identf[:, :], scalar1=D_c[:, 0:1],
                               scalar2=None, op0=ALU.mult))
    mark("vt_ready", cv)

    @block.tensor
    def _(t):
        wt(t, "vt_ready")
        t.matmul(pf[2][0:1, 0:64], Bprow[:, 0:1], Vt[:, :], start=True, stop=True)
        ct.inc(t.matmul(pf[2][0:1, 64:128], Bprow[:, 1:2], Vt[:, :], start=True, stop=True))
    mark("B_done", ct)

    @block.tensor
    def _(t):
        wt(t, "B_done")
        t.transpose(pf[3][0:128, 0:32], bnat[:, :], identf[0:32, 0:32])
        t.transpose(pf[3][0:128, 32:56], cbnat[:, :], identf[0:24, 0:24])
        t.transpose(pf[4][0:128, 0:32], pbnat[:, :], identf[0:32, 0:32])
        t.transpose(pf[4][0:128, 32:40], bqnat[:, :], identf[0:8, 0:8])
        t.transpose(pf[4][0:128, 40:48], bknat[:, :], identf[0:8, 0:8])
        t.transpose(pf[4][0:128, 48:56], bonat[:, :], identf[0:8, 0:8])
        ct.inc(t.transpose(pf[5][0:32, 0:8], limes_f[:, :], identf[0:8, 0:8]))
    mark("biasT_done", ct)

    @block.vector
    def _(v):
        v = AutoDrain(v)
        wt(v, "biasT_done")
        v.tensor_copy(out=bcol[:, :], in_=pf[3][:, 0:32])
        v.tensor_copy(out=cbcol[:, :], in_=pf[3][:, 32:56])
        v.tensor_copy(out=pbcol[:, :], in_=pf[4][:, 0:32])
        v.tensor_copy(out=bkcol[:, :], in_=pf[4][:, 40:48])
        v.tensor_copy(out=bocol[:, :], in_=pf[4][:, 48:56])
        v.tensor_copy(out=liT[:, :], in_=pf[5][0:32, 0:8])
        v.tensor_copy(out=limes_b[:, :], in_=limes_f[:, :])
        cv.inc(v.tensor_scalar(out=bqcol[:, :], in0=pf[4][:, 32:40],
                               scalar1=consts[:, CN["eighth"]:CN["eighth"] + 1], scalar2=None, op0=ALU.mult))
    mark("bias_ready", cv)

    dcat = ke(nc.sbuf_tensor("dcat", [128, 512], F32))
    dsh = ke(nc.sbuf_tensor("dsh", [64, 512], F32))
    scr1 = ke(nc.sbuf_tensor("scr1", [64, 512], F32))
    scr2 = ke(nc.sbuf_tensor("scr2", [64, 512], F32))
    RCPT = ke(nc.sbuf_tensor("RCPT", [128, 4096], F32))
    for qq in range(8):
        ks = slice(qq * 512, (qq + 1) * 512)
        @block.sync
        def _(s, qq=qq, ks=ks):
            s = AutoDrain(s)
            if qq >= 1:
                wt(s, f"kq_done{qq - 1}")
            QQ("kqld").inc(s.dma_start(out=dcat[:, :], in_=I["t_ucat"][:, ks]), 16)
            mark(f"kq_load{qq}", QQ("kqld"))
        @block.vector
        def _(v, qq=qq):
            v = AutoDrain(v)
            wt(v, f"kq_load{qq}")
            wt(v, "lcat_ready")
            wt(v, "m2s_ready")
            cv.inc(v.tensor_scalar(out=dcat[:, :], in0=dcat[:, :], scalar1=t2s_c[:, 0:1],
                                   scalar2=Lcat[:, 0:1], op0=ALU.mult, op1=ALU.subtract))
            mark(f"kq_dcat{qq}", cv)
        @block.gpsimd
        def _(g, qq=qq):
            g = AutoDrain(g)
            wt(g, f"kq_dcat{qq}")
            QQ("kq").inc(g.dma_start(out=dsh[0:64, :], in_=dcat[64:128, :]), 16)
            mark(f"kq_dsh{qq}", QQ("kq"))
        @block.scalar
        def _(s, qq=qq, ks=ks):
            s = AutoDrain(s)
            wt(s, f"kq_dsh{qq}")
            s.activation(out=scr1[0:64, :], in_=dcat[0:64, :], func=AF.Square)
            cs.inc(s.activation(out=scr2[0:64, :], in_=dsh[0:64, :], func=AF.Square))
            mark(f"kq_sq{qq}", cs)
        @block.vector
        def _(v, qq=qq, ks=ks):
            v = AutoDrain(v)
            wt(v, f"kq_sq{qq}")
            v.tensor_tensor(out=scr1[0:64, :], in0=scr1[0:64, :], in1=scr2[0:64, :], op=ALU.add)
            v.reciprocal(out=scr1[0:64, :], in_=scr1[0:64, :])
            v.tensor_tensor(out=RCPT[0:64, ks], in0=dcat[0:64, :], in1=scr1[0:64, :], op=ALU.mult)
            cv.inc(v.tensor_tensor(out=dsh[0:64, :], in0=dsh[0:64, :], in1=scr1[0:64, :], op=ALU.mult))
            mark(f"kq_rcp{qq}", cv)
        @block.gpsimd
        def _(g, qq=qq, ks=ks):
            g = AutoDrain(g)
            wt(g, f"kq_rcp{qq}")
            QQ("kq").inc(g.dma_start(out=RCPT[64:128, ks], in_=dsh[0:64, :]), 16)
            mark(f"kq_done{qq}", QQ("kq"))

    vstage = ke(nc.sbuf_tensor("vstage", [1, 512], F32))
    Brows = ke(nc.sbuf_tensor("Brows", [1, 128], F32))
    vneg = ke(nc.sbuf_tensor("vneg", [1, 256], F32))
    Vmat = ke(nc.sbuf_tensor("Vmat", [128, 8], F32))
    @block.vector
    def _(v):
        v = AutoDrain(v)
        wt(v, "B_done")
        wt(v, "small_loaded")
        v.tensor_copy(out=Brows[0:1, :], in_=pf[2][0:1, 0:128])
        ctr, cti = rows6[0:1, 0:64], rows6[0:1, 64:128]
        pr, pi = rows6[0:1, 128:192], rows6[0:1, 192:256]
        qr, qi = rows6[0:1, 256:320], rows6[0:1, 320:384]
        Br, Bi = Brows[0:1, 0:64], Brows[0:1, 64:128]
        def cmulj(ar, ai, br, bi, o):
            re = vstage[0:1, o * 128:o * 128 + 64]
            im = vstage[0:1, o * 128 + 64:o * 128 + 128]
            v.tensor_tensor(out=re, in0=ar, in1=br, op=ALU.mult)
            v.tensor_tensor(out=t1f[0:1, 0:64], in0=ai, in1=bi, op=ALU.mult)
            v.tensor_tensor(out=re, in0=re, in1=t1f[0:1, 0:64], op=ALU.add)
            v.tensor_tensor(out=im, in0=ar, in1=bi, op=ALU.mult)
            v.tensor_tensor(out=t1f[0:1, 0:64], in0=ai, in1=br, op=ALU.mult)
            v.tensor_tensor(out=im, in0=im, in1=t1f[0:1, 0:64], op=ALU.subtract)
        cmulj(ctr, cti, Br, Bi, 0)
        cmulj(ctr, cti, pr, pi, 1)
        cmulj(qr, qi, Br, Bi, 2)
        cmulj(qr, qi, pr, pi, 3)
        for j in range(4):
            ins = v.tensor_scalar(out=vneg[0:1, j * 64:(j + 1) * 64], in0=vstage[0:1, j * 128:j * 128 + 64],
                                  scalar1=consts[0:1, CN["neg1"]:CN["neg1"] + 1], scalar2=None, op0=ALU.mult)
        cv.inc(ins)
    mark("vprod", cv)

    @block.gpsimd
    def _(g):
        g = AutoDrain(g)
        wt(g, "vprod")
        for j in range(4):
            QQ("vmat").inc(g.dma_start(out=Vmat[0:64, 2 * j:2 * j + 1], in_=vstage[0:1, j * 128:j * 128 + 64]), 16)
            QQ("vmat").inc(g.dma_start(out=Vmat[64:128, 2 * j:2 * j + 1], in_=vstage[0:1, j * 128 + 64:j * 128 + 128]), 16)
            QQ("vmat").inc(g.dma_start(out=Vmat[0:64, 2 * j + 1:2 * j + 2], in_=vstage[0:1, j * 128 + 64:j * 128 + 128]), 16)
            QQ("vmat").inc(g.dma_start(out=Vmat[64:128, 2 * j + 1:2 * j + 2], in_=vneg[0:1, j * 64:(j + 1) * 64]), 16)
    mark("vmat", QQ("vmat"))

    kxy = ke(nc.sbuf_tensor("kxy", [128, 256], F32))
    arcat = ke(nc.sbuf_tensor("arcat", [128, 64], F32))
    gsc = ke(nc.sbuf_tensor("gsc", [128, 160], F32))
    @block.tensor
    def _(t):
        wt(t, "vmat")
        wt(t, "kq_done7")
        wt(t, "vt_ready")
        for tt in range(32):
            ins = t.matmul(pf[0][:, 8 * tt:8 * tt + 8], RCPT[:, tt * 128:(tt + 1) * 128], Vmat[:, :],
                           start=True, stop=True)
        ct.inc(ins)
    mark("kxy_mm", ct)
    @block.vector
    def _(v):
        v = AutoDrain(v)
        wt(v, "kxy_mm")
        cv.inc(v.tensor_copy(out=kxy[:, :], in_=pf[0][:, 0:256]))
    mark("kxy_s", cv)

    @block.vector
    def _(v):
        v = AutoDrain(v)
        wt(v, "kxy_s")
        k00r, k00i = kxy[:, 0:256:8], kxy[:, 1:256:8]
        k01r, k01i = kxy[:, 2:256:8], kxy[:, 3:256:8]
        k10r, k10i = kxy[:, 4:256:8], kxy[:, 5:256:8]
        k11r, k11i = kxy[:, 6:256:8], kxy[:, 7:256:8]
        g32 = lambda j: gsc[:, j * 32:(j + 1) * 32]
        v.tensor_scalar(out=g32(0), in0=k11r, scalar1=consts[:, CN["one"]:CN["one"] + 1], scalar2=None, op0=ALU.add)
        v.tensor_tensor(out=g32(1), in0=g32(0), in1=g32(0), op=ALU.mult)
        v.tensor_tensor(out=g32(2), in0=k11i, in1=k11i, op=ALU.mult)
        v.tensor_tensor(out=g32(1), in0=g32(1), in1=g32(2), op=ALU.add)
        v.reciprocal(out=g32(1), in_=g32(1))
        v.tensor_tensor(out=g32(2), in0=k10r, in1=g32(0), op=ALU.mult)
        v.tensor_tensor(out=g32(3), in0=k10i, in1=k11i, op=ALU.mult)
        v.tensor_tensor(out=g32(2), in0=g32(2), in1=g32(3), op=ALU.add)
        v.tensor_tensor(out=g32(2), in0=g32(2), in1=g32(1), op=ALU.mult)
        v.tensor_tensor(out=g32(3), in0=k10i, in1=g32(0), op=ALU.mult)
        v.tensor_tensor(out=g32(4), in0=k10r, in1=k11i, op=ALU.mult)
        v.tensor_tensor(out=g32(3), in0=g32(3), in1=g32(4), op=ALU.subtract)
        v.tensor_tensor(out=g32(3), in0=g32(3), in1=g32(1), op=ALU.mult)
        v.tensor_tensor(out=g32(0), in0=k01r, in1=g32(2), op=ALU.mult)
        v.tensor_tensor(out=g32(1), in0=k01i, in1=g32(3), op=ALU.mult)
        v.tensor_tensor(out=g32(0), in0=g32(0), in1=g32(1), op=ALU.subtract)
        v.tensor_tensor(out=g32(1), in0=k01r, in1=g32(3), op=ALU.mult)
        v.tensor_tensor(out=g32(4), in0=k01i, in1=g32(2), op=ALU.mult)
        v.tensor_tensor(out=g32(1), in0=g32(1), in1=g32(4), op=ALU.add)
        v.tensor_tensor(out=g32(0), in0=k00r, in1=g32(0), op=ALU.subtract)
        v.tensor_tensor(out=g32(1), in0=k00i, in1=g32(1), op=ALU.subtract)
        v.tensor_tensor(out=g32(2), in0=tbl["t_cre"][:, :], in1=g32(0), op=ALU.mult)
        v.tensor_tensor(out=g32(3), in0=tbl["t_cim"][:, :], in1=g32(1), op=ALU.mult)
        v.tensor_tensor(out=arcat[:, 0:32], in0=g32(2), in1=g32(3), op=ALU.subtract)
        v.tensor_tensor(out=g32(2), in0=tbl["t_cre"][:, :], in1=g32(1), op=ALU.mult)
        v.tensor_tensor(out=g32(3), in0=tbl["t_cim"][:, :], in1=g32(0), op=ALU.mult)
        cv.inc(v.tensor_tensor(out=arcat[:, 32:64], in0=g32(2), in1=g32(3), op=ALU.add))
    mark("arcat", cv)

    arcT = ke(nc.sbuf_tensor("arcT", [64, 128], F32))
    Spre = ke(nc.sbuf_tensor("Spre", [128, 128], F32))
    Spim = ke(nc.sbuf_tensor("Spim", [128, 128], F32))
    Ksb = ke(nc.sbuf_tensor("Ksb", [128, 32], BF16))
    K2sb = ke(nc.sbuf_tensor("K2sb", [32, 128], BF16))
    @block.tensor
    def _(t):
        wt(t, "arcat")
        ct.inc(t.transpose(pf[1][0:64, 0:128], arcat[:, :], identf[:, :]))
    mark("arcT_mm", ct)
    @block.vector
    def _(v):
        v = AutoDrain(v)
        wt(v, "arcT_mm")
        cv.inc(v.tensor_copy(out=arcT[0:64, :], in_=pf[1][0:64, 0:128]))
    mark("arcT_s", cv)
    @block.tensor
    def _(t):
        wt(t, "arcT_s")
        wt(t, "vprod")
        t.matmul(pf[2][:, 0:128], arcT[0:64, :], tbl["t_M1"][:, :], start=True, stop=True)
        ct.inc(t.matmul(pf[3][:, 0:128], arcT[0:64, :], tbl["t_M2"][:, :], start=True, stop=True))
    mark("S_mm", ct)
    @block.vector
    def _(v):
        v = AutoDrain(v)
        wt(v, "S_mm")
        v.tensor_tensor(out=Spre[:, :], in0=pf[2][:, 0:128], in1=tbl["t_TW1"][:, :], op=ALU.mult)
        v.tensor_tensor(out=t1f[:, 0:128], in0=pf[3][:, 0:128], in1=tbl["t_TW2"][:, :], op=ALU.mult)
        v.tensor_tensor(out=Spre[:, :], in0=Spre[:, :], in1=t1f[:, 0:128], op=ALU.subtract)
        v.tensor_tensor(out=Spim[:, :], in0=pf[2][:, 0:128], in1=tbl["t_TW2"][:, :], op=ALU.mult)
        v.tensor_tensor(out=t1f[:, 0:128], in0=pf[3][:, 0:128], in1=tbl["t_TW1"][:, :], op=ALU.mult)
        cv.inc(v.tensor_tensor(out=Spim[:, :], in0=Spim[:, :], in1=t1f[:, 0:128], op=ALU.add))
    mark("Sp", cv)
    @block.tensor
    def _(t):
        wt(t, "Sp")
        wt(t, "bias_ready")
        t.matmul(pf[4][:, 0:32], Spre[:, :], tbl["t_C1"][:, :], start=True, stop=False)
        ct.inc(t.matmul(pf[4][:, 0:32], Spim[:, :], tbl["t_C2"][:, :], start=False, stop=True))
    mark("K_mm", ct)
    @block.vector
    def _(v):
        v = AutoDrain(v)
        wt(v, "K_mm")
        cv.inc(v.tensor_copy(out=Ksb[:, :], in_=pf[4][:, 0:32]))
    mark("Ksb_s", cv)
    @block.tensor
    def _(t):
        wt(t, "Ksb_s")
        ct.inc(t.transpose(pbx[0][0:32, 0:128], Ksb[:, :], identb[:, :]))
    mark("K2_mm", ct)
    @block.vector
    def _(v):
        v = AutoDrain(v)
        wt(v, "K2_mm")
        cv.inc(v.tensor_copy(out=K2sb[0:32, :], in_=pbx[0][0:32, 0:128]))
    mark("K2_s", cv)

    zsb = ke(nc.sbuf_tensor("zsb", [128, 288], BF16))
    @block.vector
    def _(v):
        v = AutoDrain(v)
        cv.inc(v.memset(zsb[:, :], 0.0))
    mark("zsb0", cv)
    @block.gpsimd
    def _(g):
        g = AutoDrain(g)
        wt(g, "zsb0")
        wt(g, "K2_s")
        QQ("scr").inc(g.dma_start(out=kflat_h, in_=K2sb[0:32, :]), 16)
        for qq in range(16):
            dst = AP(tensor=scr_h.tensor, offset=qq * 288, ap=[[4608, 128], [1, 288]])
            QQ("scr").inc(g.dma_start(out=dst, in_=zsb[:, :]), 16)
        g.wait_ge(QQ("scr").h, QQ("scr").n)
        dstk = AP(tensor=scr_h.tensor, offset=384, ap=[[4609, 128], [1, 4096]])
        srck = AP(tensor=kflat_h.tensor, offset=0, ap=[[0, 128], [1, 4096]])
        QQ("scr2").inc(g.dma_start(out=dstk, in_=srck), 16)
    mark("scr_k", QQ("scr2"))
    for ocl in range(4):
        @block.gpsimd
        def _(g, ocl=ocl):
            g = AutoDrain(g)
            if ocl == 0:
                wt(g, "small_loaded")
            else:
                wt(g, f"wc_out{ocl - 1}")
            QQ("wcld").inc(g.dma_start(out=sh1[:, :], in_=I["fWsh"][ocl * 128:(ocl + 1) * 128, :]), 16)
            mark(f"wc_ld{ocl}", QQ("wcld"))
        for dc in range(C):
            i = ocl * C + dc
            @block.tensor
            def _(t, ocl=ocl, dc=dc, i=i):
                if dc == 0:
                    wt(t, f"wc_ld{ocl}")
                    wt(t, "small_loaded")
                if i < 2:
                    wt(t, "K2_s")
                if i >= 2:
                    wt(t, f"wc_cv{i - 2}")
                ct.inc(t.transpose(pbx[i % 2][:, 0:128], sh1[:, dc * 128:(dc + 1) * 128], identb[:, :]))
                mark(f"wc_ct{i}", ct)
            @block.vector
            def _(v, dc=dc, i=i):
                v = AutoDrain(v)
                wt(v, f"wc_ct{i}")
                if dc == 0:
                    s_, v_ = cp.get(f"wc_out{(i // C) - 1}", (None, 0))
                    if s_ is not None and v_ > 0:
                        v.wait_ge(s_.h, v_)
                cv.inc(v.tensor_copy(out=wstg[:, dc * 128:(dc + 1) * 128], in_=pbx[i % 2][:, 0:128]))
                mark(f"wc_cv{i}", cv)
        @block.gpsimd
        def _(g, ocl=ocl):
            g = AutoDrain(g)
            wt(g, f"wc_cv{ocl * C + C - 1}")
            dst = AP(tensor=agw_in.tensor, offset=ocl * 128 * 4096, ap=[[4096, 128], [1, 4096]])
            QQ("wcout").inc(g.dma_start(out=dst, in_=wstg[:, :]), 16)
            mark(f"wc_out{ocl}", QQ("wcout"))
    cp["wcast_doneA"] = cp["wc_out3"]
    cp["wcast_doneB"] = cp["wc_out3"]

    castbufs = [sh1, wstg]
    for j in range(12):
        w_, dcl = j // 4, j % 4
        @block.gpsimd
        def _(g, j=j, w_=w_, dcl=dcl):
            g = AutoDrain(g)
            if j < 2:
                wt(g, "wcast_doneA")
                wt(g, "wcast_doneB")
            else:
                wt(g, f"cc_g1_{j - 2}")
            QQ(f"ccg{j % 2}").inc(g.dma_start(out=castbufs[j % 2][:, 0:3072],
                               in_=AP(tensor=I["cwsh"].tensor,
                                      offset=w_ * 512 * 3072 + dcl * 128 * 3072, ap=[[3072, 128], [1, 3072]])), 16)
            mark(f"cc_g0_{j}", QQ(f"ccg{j % 2}"))
            wt(g, f"cc_g0_{j}")
            dst = AP(tensor=agc_in.tensor, offset=(dcl * 3 + w_) * 128 * 3072, ap=[[3072, 128], [1, 3072]])
            QQ(f"ccg{j % 2}").inc(g.dma_start(out=dst, in_=castbufs[j % 2][:, 0:3072]), 16)
            mark(f"cc_g1_{j}", QQ(f"ccg{j % 2}"))
    for jq, (nm, srcI, dstH, cols) in enumerate([("q", "wqsh", agq_in, 1024), ("k", "wksh", agk_in, 1024),
                                 ("v", "wvsh", agv_in, 1024), ("o", "wosh", ago_in, 1024),
                                 ("p", "pwsh", agp_in, 4096)]):
        @block.gpsimd
        def _(g, jq=jq, nm=nm, srcI=srcI, dstH=dstH, cols=cols):
            g = AutoDrain(g)
            prevm = f"cc_g1_{10 + (jq % 2)}" if jq < 2 else f"sc_{'qkvop'[jq - 2]}1"
            wt(g, prevm)
            QQ(f"ccg{jq % 2}").inc(g.dma_start(out=castbufs[jq % 2][:, 0:cols], in_=I[srcI]), 16)
            mark(f"sc_{nm}0", QQ(f"ccg{jq % 2}"))
            wt(g, f"sc_{nm}0")
            QQ(f"ccg{jq % 2}").inc(g.dma_start(out=AP(tensor=dstH.tensor, offset=0, ap=[[cols, 128], [1, cols]]),
                               in_=castbufs[jq % 2][:, 0:cols]), 16)
            mark(f"sc_{nm}1", QQ(f"ccg{jq % 2}"))
    @block.gpsimd
    def _(g):
        g = AutoDrain(g)
        wt(g, "wcast_doneA")
        wt(g, "wcast_doneB")
        QQ("ccagw").inc(g.collective_compute("AllGather", ALU.bypass, replica_groups=[list(range(NCORE))],
                                    ins=[agw_in.opt()], outs=[agw_out.opt()]))
        mark("agw_done", QQ("ccagw"))
        for nm, i_, o_ in [("q", agq_in, agq_out), ("k", agk_in, agk_out), ("v", agv_in, agv_out),
                           ("o", ago_in, ago_out), ("p", agp_in, agp_out)]:
            wt(g, f"sc_{nm}1")
            QQ("ccag" + nm).inc(g.collective_compute("AllGather", ALU.bypass, replica_groups=[list(range(NCORE))],
                                        ins=[i_.opt()], outs=[o_.opt()]))
            mark(f"ag{nm}_done", QQ("ccag" + nm))


    kst.close()
    g1.close()
    tt_st = ExitStack()
    Tt = tt_st.enter_context(nc.sbuf_tensor("Tt", [128, 4096], BF16, side="right"))
    @block.sync
    def _(s):
        s = AutoDrain(s)
        wt(s, "scr_k")
        for j in range(32):
            src = AP(tensor=scr_h.tensor, offset=384 + 128 * j, ap=[[4608, 128], [1, 128]])
            QQ("tt").inc(s.dma_start(out=Tt[:, j * 128:(j + 1) * 128], in_=src), 16)
    mark("tt_loaded", QQ("tt"))
    if _KDBG:
        @block.vector
        def _(v):
            v = AutoDrain(v)
            wt(v, "tt_loaded")
            v.tensor_copy(out=dcat[:, 0:256], in_=kxy[:, 0:256])
            v.tensor_copy(out=dcat[:, 256:320], in_=arcat[:, 0:64])
            v.tensor_copy(out=dcat[:, 320:352], in_=Ksb[:, 0:32])
            v.tensor_copy(out=dcat[:, 352:480], in_=Spre[:, :])
            v.tensor_copy(out=dcat[:, 480:608], in_=Spim[:, :])
            v.tensor_copy(out=dcat[:, 608:616], in_=Vmat[:, 0:8])
            v.tensor_copy(out=dcat[0:64, 616:744], in_=arcT[0:64, :])
            v.tensor_copy(out=dcat[0:32, 744:872], in_=K2sb[0:32, :])
            cv.inc(v.tensor_copy(out=scr1[0:64, 0:384], in_=rows6[0:1, 0:384].partition_broadcast(64) if False else rows6[0:1, 0:384]) if False else v.tensor_copy(out=dcat[:, 872:873], in_=t2s_c[:, 0:1]))
        mark("dbgcp", cv)
        @block.sync
        def _(s):
            s = AutoDrain(s)
            wt(s, "dbgcp")
            QQ("dbg").inc(s.dma_start(out=DBG[:, 0:1024], in_=dcat[:, 0:1024]), 16)
            QQ("dbg").inc(s.dma_start(out=DBG[0:1, 1024:1408], in_=rows6[0:1, :]), 16)
            QQ("dbg").inc(s.dma_start(out=DBG[0:1, 1408:1536], in_=Brows[0:1, :]), 16)
            QQ("dbg").inc(s.dma_start(out=DBG[:, 1536:1600], in_=Vt[:, :]), 16)
            QQ("dbg").inc(s.dma_start(out=DBG[:, 1600:1602], in_=Bprow[:, :]), 16)
            QQ("dbg").inc(s.dma_start(out=DBG[0:64, 1602:1730], in_=Vcnat[:, :]), 16)
        mark("dbgout", QQ("dbg"))

    def stats_pass(xt_t, tag, first_wait=()):
        for i in range(64):
            c, h = i // 2, i % 2
            sl = slice(c * T + h * 512, c * T + h * 512 + 512)
            bs = slice((i % 2) * 512, (i % 2) * 512 + 512)
            @block.vector
            def _(v, i=i, sl=sl, bs=bs):
                v = AutoDrain(v)
                if i == 0:
                    for fwn in first_wait:
                        wt(v, fwn)
                if i >= 2:
                    wt(v, f"{tag}_stmm{i - 2}")
                v.tensor_copy(out=xbb[:, bs], in_=xt_t[:, sl])
                cv.inc(v.tensor_tensor(out=sqb2[:, bs], in0=xbb[:, bs], in1=xbb[:, bs], op=ALU.mult))
                mark(f"{tag}_sq{i}", cv)
            @block.tensor
            def _(t, i=i, c=c, h=h, bs=bs, tag=tag):
                wt(t, f"{tag}_sq{i}")
                if i == 0 and tag == "p1":
                    wt(t, "Sp")
                    wt(t, "Ksb_s")
                    wt(t, "bias_ready")
                    wt(t, "vprod")
                t.matmul(pf[2 + h][0:1, 0:512], onescb[:, 0:1], xbb[:, bs],
                         start=(c == 0), stop=(c == C - 1))
                ct.inc(t.matmul(pf[4 + h][0:1, 0:512], onescb[:, 0:1], sqb2[:, bs],
                                start=(c == 0), stop=(c == C - 1)))
                mark(f"{tag}_stmm{i}", ct)
        mark(f"{tag}_stats_done", ct)

    def rows_and_bcast(tag, h, sb=None, qb=None):
        if sb is None:
            sb, qb = 2 + h, 4 + h
        @block.vector
        def _(v, h=h, sb=sb, qb=qb):
            v = AutoDrain(v)
            wt(v, f"{tag}_stats_done")
            v.tensor_scalar(out=pf[0][0:1, 0:512], in0=pf[sb][0:1, 0:512],
                            scalar1=consts[0:1, 0:1], scalar2=None, op0=ALU.mult)
            v.tensor_scalar(out=pf[1][0:1, 0:512], in0=pf[qb][0:1, 0:512],
                            scalar1=consts[0:1, 0:1], scalar2=None, op0=ALU.mult)
            v.tensor_copy(out=t1f[0:1, 0:512], in_=pf[0][0:1, 0:512])
            v.tensor_tensor(out=t1f[0:1, 0:512], in0=pf[0][0:1, 0:512], in1=t1f[0:1, 0:512],
                            op=ALU.mult)
            cv.inc(v.tensor_tensor(out=t1f[0:1, 0:512], in0=pf[1][0:1, 0:512], in1=t1f[0:1, 0:512],
                                   op=ALU.subtract))
            mark(f"{tag}_var{h}", cv)
        @block.scalar
        def _(s, h=h):
            s = AutoDrain(s)
            wt(s, f"{tag}_var{h}")
            cs.inc(s.activation(out=t1f[0:1, 0:512], in_=t1f[0:1, 0:512], func=AF.Sqrt,
                                bias=consts[0:1, CN["eps"]:CN["eps"] + 1]))
            mark(f"{tag}_sd{h}", cs)
        @block.vector
        def _(v, h=h):
            v = AutoDrain(v)
            wt(v, f"{tag}_sd{h}")
            v.reciprocal(out=srow[0:1, :], in_=t1f[0:1, 0:512])
            v.tensor_tensor(out=brow[0:1, :], in0=pf[0][0:1, 0:512], in1=srow[0:1, :], op=ALU.mult)
            cv.inc(v.tensor_scalar(out=brow[0:1, :], in0=brow[0:1, :],
                                   scalar1=consts[0:1, CN["neg1"]:CN["neg1"] + 1], scalar2=None, op0=ALU.mult))
            mark(f"{tag}_srow{h}", cv)
        @block.tensor
        def _(t, h=h):
            wt(t, f"{tag}_srow{h}")
            t.matmul(pf[0][:, 0:512], ones1f[0:1, :], srow[0:1, :], start=True, stop=True)
            ct.inc(t.matmul(pf[1][:, 0:512], ones1f[0:1, :], brow[0:1, :], start=True, stop=True))
            mark(f"{tag}_bcmm{h}", ct)
        @block.vector
        def _(v, h=h):
            v = AutoDrain(v)
            wt(v, f"{tag}_bcmm{h}")
            v.tensor_copy(out=s_bc[h][:, :], in_=pf[0][:, 0:512])
            cv.inc(v.tensor_copy(out=b_bc[h][:, :], in_=pf[1][:, 0:512]))
            mark(f"{tag}_bc{h}", cv)

    def shadow_half(xt_t, shadow_t, tag, h, shadow_cols=512, col_off=0, extra=None, first_wait=()):
        for c in range(C):
            @block.vector
            def _(v, c=c, h=h):
                v = AutoDrain(v)
                if c == 0:
                    wt(v, f"{tag}_bc1")
                    for fwn in first_wait:
                        wt(v, fwn)
                sl = slice(c * T + h * 512, c * T + h * 512 + 512)
                osl = slice(c * shadow_cols + col_off, c * shadow_cols + col_off + 512)
                v.tensor_tensor(out=t1f[:, 0:512], in0=xt_t[:, sl], in1=s_bc[h][:, :], op=ALU.mult)
                last = v.tensor_tensor(out=shadow_t[:, osl], in0=t1f[:, 0:512], in1=b_bc[h][:, :], op=ALU.add)
                if extra is not None:
                    last = extra(v, c, h)
                cv.inc(last)
        mark(f"{tag}_shadow{h}", cv)

    sh_st = ExitStack()
    shadow = sh_st.enter_context(nc.sbuf_tensor("shadow", [128, C * 512], BF16))
    stats_pass(xt, "p1", first_wait=("x0_loaded", "consts_ready"))
    rows_and_bcast("p1", 0)
    rows_and_bcast("p1", 1)
    for g in range(2):
        if g == 0:
            fw = ("scr_k", "kxy_s", "K2_s", "arcat", "sc_p1", "sc_o1")
            shadow_half(xt, shadow, "p1", g, first_wait=fw)
        for n in range(C):
            i = g * C + n
            if g == 1:
                @block.vector
                def _(v, n=n):
                    v = AutoDrain(v)
                    if n == 0:
                        wt(v, "p1_mm31")
                        wt(v, "p1_bc1")
                    sl = slice(n * T + 512, n * T + 1024)
                    v.tensor_tensor(out=t1f[:, 0:512], in0=xt[:, sl], in1=s_bc[1][:, :], op=ALU.mult)
                    cv.inc(v.tensor_tensor(out=shadow[:, n * 512:(n + 1) * 512],
                                           in0=t1f[:, 0:512], in1=b_bc[1][:, :], op=ALU.add))
                    mark(f"p1_sh1c{n}", cv)
            @block.tensor
            def _(t, n=n, g=g, i=i):
                if n == 0:
                    wt(t, "tt_loaded")
                if g == 0:
                    if n == 0:
                        wt(t, f"p1_shadow{g}")
                else:
                    wt(t, f"p1_sh1c{n}")
                if i >= 2:
                    wt(t, f"p1_ev{i - 2}")
                bank = pf[i % 2]
                for k in range(n + 1):
                    d0 = n - k
                    t.matmul(bank[:, 0:512], Tt[:, d0 * 128:(d0 + 1) * 128],
                             shadow[:, k * 512:(k + 1) * 512], start=(k == 0), stop=False)
                ct.inc(t.matmul(bank[:, 0:512], DIb[:, :], shadow[:, n * 512:(n + 1) * 512],
                                start=False, stop=True))
                mark(f"p1_mm{i}", ct)
            @block.vector
            def _(v, n=n, g=g, i=i):
                v = AutoDrain(v)
                wt(v, f"p1_mm{i}")
                sl = slice(n * T + g * 512, n * T + g * 512 + 512)
                cv.inc(v.tensor_tensor(out=xt[:, sl], in0=xt[:, sl], in1=pf[i % 2][:, 0:512], op=ALU.add))
                mark(f"p1_ev{i}", cv)
    mark("p1_done", cv)
    tt_st.close()

    def ffn_phase(xt_t, shadow_t, tag, arb_in, arb_out, prev_done, post_abf=None, tail_hook=None):
        fst = ExitStack()
        fe = fst.enter_context
        A_bf = fe(nc.sbuf_tensor(f"Abf{tag}", [128, C * 32], BF16))
        wsl = [fe(nc.sbuf_tensor(f"wsl{tag}{i}", [128, 1024], BF16)) for i in range(2)]
        zT = fe(nc.sbuf_tensor(f"zT{tag}", [32, 1024], F32))
        zp_bf = fe(nc.sbuf_tensor(f"zp{tag}", [32, 1024], BF16))
        HslT = fe(nc.sbuf_tensor(f"Hsl{tag}", [8, 1024], F32))
        Han = fe(nc.sbuf_tensor(f"Han{tag}", [8, 512], F32))
        Dan = fe(nc.sbuf_tensor(f"Dan{tag}", [8, 512], F32))
        gt = [fe(nc.sbuf_tensor(f"gt{tag}{i}", [128, 8], F32)) for i in range(4)]
        gcol = fe(nc.sbuf_tensor(f"gcol{tag}", [128, 6], F32))
        wfT = fe(nc.sbuf_tensor(f"wfT{tag}", [8, 1024], BF16))
        Bsl = [fe(nc.sbuf_tensor(f"Bsl{tag}{i}", [32, 512], BF16)) for i in range(2)]

        if post_abf is not None:
            post_abf()
        @block.gpsimd
        def _(g):
            g = AutoDrain(g)
            wt(g, prev_done)
            for c in range(C):
                QQ(tag + "abf").inc(g.dma_start(out=A_bf[:, c * 32:(c + 1) * 32],
                                   in_=I["fA"][c * 128:(c + 1) * 128, :]), 16)
        mark(f"{tag}_abf", QQ(tag + "abf"))

        stats_pass(xt_t, tag, first_wait=(prev_done,))
        rows_and_bcast(tag, 0)
        rows_and_bcast(tag, 1)
        for g in range(2):
            for oc in range(C):
                i = g * C + oc
                for qj in range(4):
                    qidx = i * 4 + qj
                    if oc == 0:
                        @block.vector
                        def _(v, g=g, qj=qj):
                            v = AutoDrain(v)
                            if qj == 0:
                                wt(v, f"{tag}_bc{g}")
                                if g == 1:
                                    wt(v, f"{tag}_hmm{C - 1}")
                                    wt(v, f"{tag}_zmm0")
                            for c in range(qj * 8, qj * 8 + 8):
                                sl = slice(c * T + g * 512, c * T + g * 512 + 512)
                                v.tensor_tensor(out=t1f[:, 0:512], in0=xt_t[:, sl],
                                                in1=s_bc[g][:, :], op=ALU.mult)
                                last = v.tensor_tensor(out=shadow_t[:, c * 512:(c + 1) * 512],
                                                       in0=t1f[:, 0:512], in1=b_bc[g][:, :], op=ALU.add)
                            cv.inc(last)
                            mark(f"{tag}_sh{g}q{qj}", cv)
                    eng_sel = block.scalar if qidx % 2 == 1 else block.sync
                    @eng_sel
                    def _(s, oc=oc, qj=qj, qidx=qidx):
                        s = AutoDrain(s)
                        if qidx < 2:
                            wt(s, "agw_done")
                            wt(s, prev_done)
                        if qidx >= 2:
                            wt(s, f"{tag}_hq{qidx - 2}")
                        base = ((oc % 8) * 4 + (oc // 8)) * 128 * 4096 + qj * 1024
                        src = AP(tensor=agw_out.tensor, offset=base, ap=[[4096, 128], [1, 1024]])
                        QQ(f"{tag}wsl{qidx % 2}").inc(s.dma_start(out=wsl[qidx % 2][:, :], in_=src), 16)
                        mark(f"{tag}_wq{qidx}", QQ(f"{tag}wsl{qidx % 2}"))
                    @block.tensor
                    def _(t, oc=oc, g=g, i=i, qj=qj, qidx=qidx):
                        if qj == 0:
                            if i >= 2:
                                wt(t, f"{tag}_hev{i - 2}")
                        if oc == 0:
                            wt(t, f"{tag}_sh{g}q{qj}")
                        bank = pf[2 + (i % 2)]
                        wt(t, f"{tag}_wq{qidx}")
                        for dk in range(8):
                            dc = qj * 8 + dk
                            ins = t.matmul(bank[:, 0:512], wsl[qidx % 2][:, dk * 128:(dk + 1) * 128],
                                           shadow_t[:, dc * 512:(dc + 1) * 512],
                                           start=(dc == 0), stop=(dc == C - 1))
                        ct.inc(ins)
                        mark(f"{tag}_hq{qidx}", ct)
                        if qj == 3:
                            mark(f"{tag}_hmm{i}", ct)
                if oc == 0:
                    @block.tensor
                    def _(t, g=g):
                        wt(t, f"{tag}_sh{g}q3")
                        wt(t, f"{tag}_abf")
                        for c in range(C):
                            ins = t.matmul(pf[4][0:32, 0:512], A_bf[:, c * 32:(c + 1) * 32],
                                           shadow_t[:, c * 512:(c + 1) * 512], start=(c == 0), stop=(c == C - 1))
                        ct.inc(ins)
                        mark(f"{tag}_zmm{g}", ct)
                    @block.vector
                    def _(v, g=g):
                        v = AutoDrain(v)
                        wt(v, f"{tag}_zmm{g}")
                        cv.inc(v.tensor_copy(out=zT[0:32, g * 512:(g + 1) * 512], in_=pf[4][0:32, 0:512]))
                        mark(f"{tag}_zs{g}", cv)
                    @block.vector
                    def _(v, g=g, i=i):
                        v = AutoDrain(v)
                        wt(v, f"{tag}_hmm{i}")
                        cv.inc(v.tensor_scalar(out=HslT[0:8, g * 512:(g + 1) * 512],
                                               in0=pf[2 + (i % 2)][0:8, 0:512],
                                               scalar1=bcol[0:8, 0:1], scalar2=None, op0=ALU.add))
                        mark(f"{tag}_hsl{g}", cv)
                @block.vector
                def _(v, oc=oc, g=g, i=i):
                    v = AutoDrain(v)
                    wt(v, f"{tag}_hmm{i}")
                    if oc == 0:
                        wt(v, f"{tag}_hsl{g}")
                    sl = slice(oc * T + g * 512, oc * T + g * 512 + 512)
                    cv.inc(v.scalar_tensor_tensor(out=xt_t[:, sl], in0=xt_t[:, sl],
                                                  scalar=bcol[:, oc:oc + 1],
                                                  in1=pf[2 + (i % 2)][:, 0:512], op0=ALU.add, op1=ALU.add))
                    mark(f"{tag}_hev{i}", cv)
        mark(f"{tag}_H_done", cv)

        @block.tensor
        def _(t):
            wt(t, f"{tag}_zs1")
            zap = zT[0:32, :]
            zan = AP(tensor=zap.tensor, offset=zap.offset + 1, ap=[zap.ap[0], [2, 512]])
            ct.inc(t.matmul(pf[4][0:8, 0:512], liT[0:32, 0:8], zan, start=True, stop=True))
            mark(f"{tag}_dslmm", ct)
        @block.vector
        def _(v):
            v = AutoDrain(v)
            wt(v, f"{tag}_dslmm")
            wt(v, f"{tag}_hsl1")
            v.tensor_copy(out=Dan[0:8, :], in_=pf[4][0:8, 0:512])
            hap = HslT[0:8, :]
            hslan = AP(tensor=hap.tensor, offset=hap.offset + 1, ap=[hap.ap[0], [2, 512]])
            v.tensor_copy(out=Han[0:8, :], in_=hslan)
            v.reduce_max(out=gt[0][0:8, 0:1], in_=Han[0:8, :], axis=AX, apply_absolute_value=True)
            cv.inc(v.reduce_max(out=gt[0][0:8, 1:2], in_=Dan[0:8, :], axis=AX, apply_absolute_value=True))
            mark(f"{tag}_lmax", cv)
        @block.tensor
        def _(t):
            wt(t, f"{tag}_lmax")
            t.transpose(pf[5][0:1, 0:8], gt[0][0:8, 0:1], identf[0:8, 0:8])
            ct.inc(t.transpose(pf[5][0:1, 8:16], gt[0][0:8, 1:2], identf[0:8, 0:8]))
            mark(f"{tag}_lmaxT", ct)
        @block.vector
        def _(v):
            v = AutoDrain(v)
            wt(v, f"{tag}_lmaxT")
            v.reduce_max(out=arin_s[0:1, 0:1], in_=pf[5][0:1, 0:8], axis=AX)
            cv.inc(v.reduce_max(out=arin_s[0:1, 1:2], in_=pf[5][0:1, 8:16], axis=AX))
            mark(f"{tag}_lmax2", cv)
        @block.gpsimd
        def _(g):
            g = AutoDrain(g)
            wt(g, f"{tag}_lmax2")
            ars = QQ(tag + "ar")
            ars.inc(g.dma_start(out=arb_in, in_=arin_s[0:1, :]), 16)
            g.wait_ge(ars.h, ars.n)
            QQ(tag + "cc").inc(g.collective_compute("AllReduce", ALU.max, replica_groups=[list(range(NCORE))],
                                        ins=[arb_in.opt()], outs=[arb_out.opt()]))
            mark(f"{tag}_ar", QQ(tag + "cc"))
            wt(g, f"{tag}_ar")
            ars.inc(g.dma_start(out=armax_s[0:1, :], in_=arb_out), 16)
        mark(f"{tag}_armax", QQ(tag + "ar"))

        @block.vector
        def _(v):
            v = AutoDrain(v)
            wt(v, f"{tag}_armax")
            v.reciprocal(out=armax_s[0:1, :], in_=armax_s[0:1, :])
            cv.inc(v.tensor_scalar(out=armax_s[0:1, :], in0=armax_s[0:1, :],
                                   scalar1=consts[0:1, CN["half"]:CN["half"] + 1], scalar2=None, op0=ALU.mult))
            mark(f"{tag}_hsrow", cv)
        @block.tensor
        def _(t):
            wt(t, f"{tag}_hsrow")
            ct.inc(t.matmul(pf[4][:, 0:2], ones1f[0:1, :], armax_s[0:1, 0:2], start=True, stop=True))
            mark(f"{tag}_hsmm", ct)
        @block.vector
        def _(v):
            v = AutoDrain(v)
            wt(v, f"{tag}_hsmm")
            cv.inc(v.tensor_copy(out=hsds[:, :], in_=pf[4][:, 0:2]))
            mark(f"{tag}_hsds", cv)

        for a in range(4):
            @block.tensor
            def _(t, a=a):
                wt(t, f"{tag}_hsds")
                if a >= 1:
                    wt(t, f"{tag}_gw{a - 1}")
                t.transpose(pf[5][0:128, 0:8], Han[0:8, a * 128:(a + 1) * 128], identf[0:8, 0:8])
                ct.inc(t.transpose(pf[5][0:128, 8:16], Dan[0:8, a * 128:(a + 1) * 128], identf[0:8, 0:8]))
                mark(f"{tag}_gtr{a}", ct)
            @block.vector
            def _(v, a=a):
                v = AutoDrain(v)
                wt(v, f"{tag}_gtr{a}")
                v.tensor_scalar(out=gt[1][:, :], in0=pf[5][:, 8:16], scalar1=hsds[:, 1:2], scalar2=None, op0=ALU.mult)
                v.scalar_tensor_tensor(out=gt[0][:, :], in0=pf[5][:, 0:8], scalar=hsds[:, 0:1],
                                       in1=gt[1][:, :], op0=ALU.mult, op1=ALU.add)
                cv.inc(v.reduce_max(out=gcol[:, 0:1], in_=gt[0][:, :], axis=AX, negate=True))
                mark(f"{tag}_glog{a}", cv)
            @block.scalar
            def _(s, a=a):
                s = AutoDrain(s)
                wt(s, f"{tag}_glog{a}")
                cs.inc(s.activation(out=gt[1][:, :], in_=gt[0][:, :], func=AF.Exp,
                                    bias=gcol[:, 0:1], accum_out=gcol[:, 1:2]))
                mark(f"{tag}_gexp{a}", cs)
            @block.vector
            def _(v, a=a):
                v = AutoDrain(v)
                wt(v, f"{tag}_gexp{a}")
                v.reciprocal(out=gcol[:, 2:3], in_=gcol[:, 1:2])
                v.tensor_scalar(out=gt[1][:, :], in0=gt[1][:, :], scalar1=gcol[:, 2:3], scalar2=None, op0=ALU.mult)
                v.reduce_max(out=gcol[:, 0:1], in_=gt[1][:, :], axis=AX)
                v.tensor_scalar(out=gt[2][:, :], in0=gt[1][:, :], scalar1=gcol[:, 0:1], scalar2=None, op0=ALU.is_ge)
                v.reduce_sum(out=gcol[:, 1:2], in_=gt[2][:, :], axis=AX)
                v.tensor_scalar(out=gt[2][:, :], in0=gt[2][:, :], scalar1=consts[:, CN["big"]:CN["big"] + 1],
                                scalar2=None, op0=ALU.mult)
                v.tensor_tensor(out=gt[3][:, :], in0=gt[1][:, :], in1=gt[2][:, :], op=ALU.subtract)
                v.reduce_max(out=gcol[:, 2:3], in_=gt[3][:, :], axis=AX)
                v.tensor_scalar(out=gcol[:, 3:4], in0=gcol[:, 1:2], scalar1=consts[:, CN["c15"]:CN["c15"] + 1],
                                scalar2=None, op0=ALU.is_ge)
                v.tensor_tensor(out=gcol[:, 4:5], in0=gcol[:, 0:1], in1=gcol[:, 2:3], op=ALU.subtract)
                v.tensor_tensor(out=gcol[:, 5:6], in0=gcol[:, 3:4], in1=gcol[:, 4:5], op=ALU.mult)
                v.tensor_tensor(out=gcol[:, 0:1], in0=gcol[:, 2:3], in1=gcol[:, 5:6], op=ALU.add)
                cv.inc(v.tensor_scalar(out=gcol[:, 1:2], in0=gcol[:, 0:1],
                                       scalar1=consts[:, CN["neg2"]:CN["neg2"] + 1], scalar2=None, op0=ALU.mult))
                mark(f"{tag}_gth{a}", cv)
            @block.scalar
            def _(s, a=a):
                s = AutoDrain(s)
                wt(s, f"{tag}_gth{a}")
                cs.inc(s.activation(out=gt[2][:, :], in_=gt[1][:, :], func=AF.Sigmoid,
                                    scale=consts[:, CN["two"]:CN["two"] + 1], bias=gcol[:, 1:2]))
                mark(f"{tag}_gsig{a}", cs)
            @block.vector
            def _(v, a=a):
                v = AutoDrain(v)
                wt(v, f"{tag}_gsig{a}")
                v.tensor_tensor(out=gt[2][:, :], in0=gt[1][:, :], in1=gt[2][:, :], op=ALU.mult)
                v.reduce_sum(out=gcol[:, 2:3], in_=gt[2][:, :], axis=AX)
                v.tensor_scalar(out=gcol[:, 2:3], in0=gcol[:, 2:3],
                                scalar1=consts[:, CN["eps9"]:CN["eps9"] + 1], scalar2=None, op0=ALU.add)
                v.reciprocal(out=gcol[:, 3:4], in_=gcol[:, 2:3])
                cv.inc(v.tensor_scalar(out=gt[2][:, :], in0=gt[2][:, :], scalar1=gcol[:, 3:4],
                                       scalar2=None, op0=ALU.mult))
                mark(f"{tag}_gwts{a}", cv)
            @block.tensor
            def _(t, a=a):
                wt(t, f"{tag}_gwts{a}")
                ct.inc(t.transpose(pf[4][0:8, 0:128], gt[2][:, :], identf[:, :]))
                mark(f"{tag}_gwT{a}", ct)
            @block.vector
            def _(v, a=a):
                v = AutoDrain(v)
                wt(v, f"{tag}_gwT{a}")
                wap = wfT[0:8, :]
                even = AP(tensor=wap.tensor, offset=wap.offset + a * 256, ap=[wap.ap[0], [2, 128]])
                odd = AP(tensor=wap.tensor, offset=wap.offset + a * 256 + 1, ap=[wap.ap[0], [2, 128]])
                v.tensor_copy(out=even, in_=pf[4][0:8, 0:128])
                cv.inc(v.tensor_copy(out=odd, in_=pf[4][0:8, 0:128]))
                mark(f"{tag}_gw{a}", cv)
        mark(f"{tag}_wfT", cv)
        @block.tensor
        def _(t):
            wt(t, f"{tag}_wfT")
            t.matmul(pf[4][0:32, 0:512], limes_b[:, :], wfT[0:8, 0:512], start=True, stop=True)
            ct.inc(t.matmul(pf[5][0:32, 0:512], limes_b[:, :], wfT[0:8, 512:1024], start=True, stop=True))
            mark(f"{tag}_pmx", ct)
        @block.vector
        def _(v):
            v = AutoDrain(v)
            wt(v, f"{tag}_pmx")
            v.scalar_tensor_tensor(out=zp_bf[0:32, 0:512], in0=zT[0:32, 0:512],
                                   scalar=consts[0:32, CN["half"]:CN["half"] + 1],
                                   in1=pf[4][0:32, 0:512], op0=ALU.mult, op1=ALU.mult)
            cv.inc(v.scalar_tensor_tensor(out=zp_bf[0:32, 512:1024], in0=zT[0:32, 512:1024],
                                          scalar=consts[0:32, CN["half"]:CN["half"] + 1],
                                          in1=pf[5][0:32, 0:512], op0=ALU.mult, op1=ALU.mult))
            mark(f"{tag}_zp", cv)

        for ocg in range(8):
            @block.gpsimd
            def _(gg, ocg=ocg):
                gg = AutoDrain(gg)
                if ocg < 2:
                    wt(gg, prev_done)
                if ocg >= 2:
                    wt(gg, f"{tag}_dmm{(ocg - 2) * 8 + 7}")
                QQ(f"{tag}bsl{ocg % 2}").inc(gg.dma_start(out=Bsl[ocg % 2][:, :],
                                             in_=I["fB"][:, ocg * 512:(ocg + 1) * 512]), 16)
                mark(f"{tag}_dbsl{ocg}", QQ(f"{tag}bsl{ocg % 2}"))
            for g in range(2):
                for ocl in range(4):
                    oc = ocg * 4 + ocl
                    i = ocg * 8 + g * 4 + ocl
                    @block.tensor
                    def _(t, oc=oc, ocg=ocg, ocl=ocl, g=g, i=i):
                        if g == 0 and ocl == 0:
                            wt(t, f"{tag}_dbsl{ocg}")
                        if i == 0:
                            wt(t, f"{tag}_zp")
                            wt(t, f"{tag}_H_done")
                        if i >= 2:
                            wt(t, f"{tag}_dev{i - 2}")
                        ct.inc(t.matmul(pf[2 + (i % 2)][:, 0:512], Bsl[ocg % 2][:, ocl * 128:(ocl + 1) * 128],
                                        zp_bf[0:32, g * 512:(g + 1) * 512], start=True, stop=True))
                        mark(f"{tag}_dmm{i}", ct)
                    @block.vector
                    def _(v, oc=oc, g=g, i=i):
                        v = AutoDrain(v)
                        wt(v, f"{tag}_dmm{i}")
                        sl = slice(oc * T + g * 512, oc * T + g * 512 + 512)
                        cv.inc(v.tensor_tensor(out=xt_t[:, sl], in0=xt_t[:, sl],
                                               in1=pf[2 + (i % 2)][:, 0:512], op=ALU.add))
                        mark(f"{tag}_dev{i}", cv)
                    if tail_hook is not None and g == 1:
                        tail_hook(oc, i)
        mark(f"{tag}_done", cv)
        fst.close()

    def _emit_agc():
        @block.gpsimd
        def _(g):
            g = AutoDrain(g)
            wt(g, "cc_g1_11")
            QQ("ccagc").inc(g.collective_compute("AllGather", ALU.bypass, replica_groups=[list(range(NCORE))],
                                        ins=[agc_in.opt()], outs=[agc_out.opt()]))
            mark("agc_done", QQ("ccagc"))
    def p3_hook(c, devidx):
        def _stats_chunk(c):
            for h in range(2):
                i = c * 2 + h
                bs = slice(h * 512, h * 512 + 512)
                sl = slice(c * T + h * 512, c * T + h * 512 + 512)
                sb, qb = (0, 1) if h == 0 else (4, 5)
                @block.vector
                def _(v, i=i, sl=sl, bs=bs):
                    v = AutoDrain(v)
                    if i >= 2:
                        wt(v, f"p3_stmm{i - 2}")
                    v.tensor_copy(out=xbb[:, bs], in_=xt[:, sl])
                    cv.inc(v.tensor_tensor(out=sqb2[:, bs], in0=xbb[:, bs], in1=xbb[:, bs], op=ALU.mult))
                    mark(f"p3_sq{i}", cv)
                @block.tensor
                def _(t, i=i, c=c, bs=bs, sb=sb, qb=qb):
                    wt(t, f"p3_sq{i}")
                    t.matmul(pf[sb][0:1, 0:512], onescb[:, 0:1], xbb[:, bs],
                             start=(c == 0), stop=(c == C - 1))
                    ct.inc(t.matmul(pf[qb][0:1, 0:512], onescb[:, 0:1], sqb2[:, bs],
                                    start=(c == 0), stop=(c == C - 1)))
                    mark(f"p3_stmm{i}", ct)
        if c >= 1:
            _stats_chunk(c - 1)
        @block.sync
        def _(s, c=c, devidx=devidx):
            s = AutoDrain(s)
            wt(s, f"p2_dev{devidx}")
            QQ("x2sp").inc(s.dma_start(out=x2_h[c * 128:(c + 1) * 128, :], in_=xt[:, c * T:(c + 1) * T]), 16)
        if c == C - 1:
            _stats_chunk(C - 1)
            mark("p3_stats_done", ct)
            mark("x2_spilled", QQ("x2sp"))

    if _PHASES >= 2:
        ffn_phase(xt, shadow, "p2", ar1_in, ar1_out, "p1_done", post_abf=_emit_agc,
                  tail_hook=p3_hook if _PHASES >= 3 else None)
    sh_st.close()

    if _PHASES >= 3:
        xt_st.close()
        p3 = ExitStack()
        pe3 = p3.enter_context
        ln2 = pe3(nc.sbuf_tensor("ln2", [128, C * 1026], BF16))
        hstL = pe3(nc.sbuf_tensor("hstL", [128, 32], BF16))
        hstR = pe3(nc.sbuf_tensor("hstR", [128, 32], BF16))
        x2s = [pe3(nc.sbuf_tensor(f"x2s{i}", [128, 512], F32)) for i in range(4)]

        rows_and_bcast("p3", 0, sb=0, qb=1)
        rows_and_bcast("p3", 1, sb=4, qb=5)
        for h in range(2):
            for c in range(C):
                i2 = h * C + c
                eng_sel2 = block.scalar if i2 % 4 in (1, 3) else block.sync
                @eng_sel2
                def _(s, c=c, h=h, i2=i2):
                    s = AutoDrain(s)
                    if i2 < 4:
                        wt(s, "x2_spilled")
                        wt(s, "p3_stats_done")
                    if i2 >= 4:
                        wt(s, f"p3_sh{i2 - 4}")
                    QQ(f"x2s{i2 % 4}").inc(s.dma_start(out=x2s[i2 % 4][:, :],
                                       in_=x2_h[c * 128:(c + 1) * 128, h * 512:(h + 1) * 512]), 16)
                    mark(f"p3_xs{i2}", QQ(f"x2s{i2 % 4}"))
                @block.vector
                def _(v, c=c, h=h, i2=i2):
                    v = AutoDrain(v)
                    wt(v, f"p3_xs{i2}")
                    if i2 == 0:
                        wt(v, "p3_bc1")
                    col_off = 1 if h == 0 else 513
                    osl = slice(c * 1026 + col_off, c * 1026 + col_off + 512)
                    v.tensor_tensor(out=t1f[:, 0:512], in0=x2s[i2 % 4][:, :], in1=s_bc[h][:, :], op=ALU.mult)
                    v.tensor_tensor(out=ln2[:, osl], in0=t1f[:, 0:512], in1=b_bc[h][:, :], op=ALU.add)
                    if h == 0:
                        last = v.tensor_copy(out=hstL[:, c:c + 1], in_=ln2[:, c * 1026 + 1:c * 1026 + 2])
                    else:
                        last = v.tensor_copy(out=hstR[:, c:c + 1], in_=ln2[:, c * 1026 + 1024:c * 1026 + 1025])
                    cv.inc(last)
                    mark(f"p3_sh{i2}", cv)
            mark(f"p3_shadow{h}", cv)

        @block.gpsimd
        def _(g):
            g = AutoDrain(g)
            wt(g, "p3_shadow1")
            hs_ = QQ("halo")
            hs_.inc(g.dma_start(out=AP(tensor=agh_in.tensor, offset=0, ap=[[32, 128], [1, 32]]),
                               in_=hstL[:, :]), 16)
            hs_.inc(g.dma_start(out=AP(tensor=agh_in.tensor, offset=4096, ap=[[32, 128], [1, 32]]),
                               in_=hstR[:, :]), 16)
            g.wait_ge(hs_.h, hs_.n)
            QQ("ccagh").inc(g.collective_compute("AllGather", ALU.bypass, replica_groups=[list(range(NCORE))],
                                        ins=[agh_in.opt()], outs=[agh_out.opt()]))
        mark("agh_done", QQ("ccagh"))

        qk_sc = ExitStack()
        qkvT = qk_sc.enter_context(nc.sbuf_tensor("qkvT", [128, 24 * 1024], BF16, side="right"))
        c3 = ExitStack()
        ce3 = c3.enter_context
        cwb = [ce3(nc.sbuf_tensor(f"cwb{i}", [128, 512], BF16)) for i in range(3)]

        hi = ExitStack()
        he_ = hi.enter_context
        GlGf = he_(nc.sbuf_tensor("GlGf", [8, 8192], BF16, side="right"))
        hfl = he_(nc.sbuf_tensor("hfl", [1, 512], BF16, side="right"))
        haloL = he_(nc.sbuf_tensor("haloL", [128, 32], BF16, side="right"))
        haloR = he_(nc.sbuf_tensor("haloR", [128, 32], BF16, side="right"))
        @block.gpsimd
        def _(g):
            g = AutoDrain(g)
            wt(g, "agh_done")
            wt(g, "x2_spilled")
            QQ("halo").inc(g.dma_start(out=GlGf[0:8, 0:4096],
                               in_=AP(tensor=agh_out.tensor, offset=4096, ap=[[8192, 8], [1, 4096]])), 16)
            QQ("halo").inc(g.dma_start(out=GlGf[0:8, 4096:8192],
                               in_=AP(tensor=agh_out.tensor, offset=0, ap=[[8192, 8], [1, 4096]])), 16)
        mark("gl_loaded", QQ("halo"))
        for j in range(16):
            side, blk_ = j // 8, j % 8
            @block.tensor
            def _(t, j=j, side=side, blk_=blk_):
                wt(t, "gl_loaded")
                if j >= 2:
                    wt(t, f"p3_hf{j - 2}")
                lhs = onelb if side == 0 else onerb
                ct.inc(t.matmul(pf[4 + (j % 2)][0:1, 0:512], lhs[0:8, 0:1],
                                GlGf[0:8, side * 4096 + blk_ * 512: side * 4096 + (blk_ + 1) * 512],
                                start=True, stop=True))
                mark(f"p3_hfmm{j}", ct)
            @block.vector
            def _(v, j=j):
                v = AutoDrain(v)
                wt(v, f"p3_hfmm{j}")
                if j >= 1:
                    wt(v, f"p3_hf{j - 1}")
                cv.inc(v.tensor_copy(out=hfl[0:1, :], in_=pf[4 + (j % 2)][0:1, 0:512]))
                mark(f"p3_hfe{j}", cv)
            @block.gpsimd
            def _(g, j=j, side=side, blk_=blk_):
                g = AutoDrain(g)
                wt(g, f"p3_hfe{j}")
                hq_ = QQ("halo")
                hq_.inc(g.dma_start(out=hfl_h, in_=hfl[0:1, :]), 16)
                g.wait_ge(hq_.h, hq_.n)
                dstT = haloL if side == 0 else haloR
                hq_.inc(g.dma_start(out=dstT[blk_ * 16:(blk_ + 1) * 16, :],
                                   in_=AP(tensor=hfl_h.tensor, offset=0, ap=[[32, 16], [1, 32]])), 16)
                mark(f"p3_hf{j}", QQ("halo"))
        @block.vector
        def _(v):
            v = AutoDrain(v)
            wt(v, "p3_hf15")
            lap = ln2[:, :]
            v.tensor_copy(out=AP(tensor=lap.tensor, offset=lap.offset, ap=[lap.ap[0], [1026, 32]]),
                          in_=haloL[:, :])
            cv.inc(v.tensor_copy(out=AP(tensor=lap.tensor, offset=lap.offset + 1025, ap=[lap.ap[0], [1026, 32]]),
                                 in_=haloR[:, :]))
        mark("halo_in", cv)
        hi.close()

        def conv_bank(idx):
            if idx < 6:
                return pf[idx][:, 0:512]
            return pbx[idx - 6][:, :].bitcast(F32)[:, 0:512]
        for ccg in range(6):
            for dc in range(C):
                for w_ in range(3):
                    j = (ccg * C + dc) * 3 + w_
                    @block.sync
                    def _(s, j=j, ccg=ccg, dc=dc, w_=w_):
                        s = AutoDrain(s)
                        if j == 0:
                            wt(s, "agc_done")
                            wt(s, "x2_spilled")
                            wt(s, "p3_stats_done")
                        if j >= 3:
                            wt(s, f"p3_cmm{j - 3}")
                        base = ((dc // 4) * 12 + (dc % 4) * 3 + w_) * 128 * 3072 + ccg * 512
                        src = AP(tensor=agc_out.tensor, offset=base, ap=[[3072, 128], [1, 512]])
                        QQ(f"cw{j % 3}").inc(s.dma_start(out=cwb[j % 3][:, :], in_=src), 16)
                        mark(f"p3_cw{j}", QQ(f"cw{j % 3}"))
                    @block.tensor
                    def _(t, j=j, ccg=ccg, dc=dc, w_=w_):
                        wt(t, f"p3_cw{j}")
                        first = dc == 0 and w_ == 0
                        last = dc == C - 1 and w_ == 2
                        if first:
                            wt(t, "halo_in")
                            if ccg >= 1:
                                wt(t, f"p3_cev{ccg - 1}")
                        for ccl in range(4):
                            for tg in range(2):
                                ins = t.matmul(conv_bank(ccl * 2 + tg),
                                               cwb[j % 3][:, ccl * 128:(ccl + 1) * 128],
                                               ln2[:, dc * 1026 + tg * 512 + w_: dc * 1026 + tg * 512 + w_ + 512],
                                               start=first, stop=last)
                        ct.inc(ins)
                        mark(f"p3_cmm{j}", ct)
            @block.vector
            def _(v, ccg=ccg):
                v = AutoDrain(v)
                wt(v, f"p3_cmm{(ccg * C + C - 1) * 3 + 2}")
                for ccl in range(4):
                    for tg in range(2):
                        cc_ = ccg * 4 + ccl
                        ins = v.tensor_scalar(out=qkvT[:, cc_ * 1024 + tg * 512: cc_ * 1024 + (tg + 1) * 512],
                                              in0=conv_bank(ccl * 2 + tg), scalar1=cbcol[:, cc_:cc_ + 1],
                                              scalar2=None, op0=ALU.add)
                cv.inc(ins)
                mark(f"p3_cev{ccg}", cv)
        mark("conv_done", cv)
        c3.close()
        p3.close()

        q3 = ExitStack()
        qe3 = q3.enter_context
        qhT = qe3(nc.sbuf_tensor("qhT", [128, 8192], BF16, side="right"))
        khT = qe3(nc.sbuf_tensor("khT", [128, 8192], BF16, side="right"))
        vhT = qe3(nc.sbuf_tensor("vhT", [128, 8192], BF16, side="right"))
        wqb = [qe3(nc.sbuf_tensor(f"wqb{i}", [128, 1024], BF16, side="right")) for i in range(2)]
        avs = qe3(nc.sbuf_tensor("avs", [128, 8192], BF16, side="right"))
        for qk in range(2):
            src_ag = agq_out if qk == 0 else agk_out
            dst_t = qhT if qk == 0 else khT
            bias_c = bqcol if qk == 0 else bkcol
            agmark = "agq_done" if qk == 0 else "agk_done"
            for hec in range(8):
                hidx = qk * 8 + hec
                @block.sync
                def _(s, qk=qk, hec=hec, hidx=hidx, src_ag=src_ag, agmark=agmark):
                    s = AutoDrain(s)
                    if hidx % 8 == 0:
                        wt(s, agmark)
                        wt(s, "conv_done")
                    if hidx >= 2:
                        wt(s, f"p3_qmmlast{hidx - 2}")
                    srcap = AP(tensor=src_ag.tensor, offset=hec * 128,
                               ap=[[1024, 128], [131072, 8], [1, 128]])
                    QQ(f"qw{hidx % 2}").inc(s.dma_start(out=wqb[hidx % 2][:, :], in_=srcap), 16)
                    mark(f"p3_qw{hidx}", QQ(f"qw{hidx % 2}"))
                for g in range(2):
                    i = hidx * 2 + g
                    @block.tensor
                    def _(t, qk=qk, hec=hec, hidx=hidx, g=g, i=i):
                        if g == 0:
                            wt(t, f"p3_qw{hidx}")
                        if i >= 2:
                            wt(t, f"p3_qev{i - 2}")
                        for ccj in range(8):
                            ins = t.matmul(pf[2 + (i % 2)][:, 0:512], wqb[hidx % 2][:, ccj * 128:(ccj + 1) * 128],
                                           qkvT[:, (qk * 8 + ccj) * 1024 + g * 512:(qk * 8 + ccj) * 1024 + (g + 1) * 512],
                                           start=(ccj == 0), stop=(ccj == 7))
                        ct.inc(ins)
                        mark(f"p3_qmm{i}", ct)
                        if g == 1:
                            mark(f"p3_qmmlast{hidx}", ct)
                    @block.scalar
                    def _(s, qk=qk, i=i, hec=hec, g=g, dst_t=dst_t, bias_c=bias_c):
                        s = AutoDrain(s)
                        wt(s, f"p3_qmm{i}")
                        if qk == 0:
                            ins = s.activation(out=dst_t[:, hec * 1024 + g * 512:hec * 1024 + (g + 1) * 512],
                                               in_=pf[2 + (i % 2)][:, 0:512], func=AF.Identity,
                                               scale=consts[:, CN["eighth"]:CN["eighth"] + 1],
                                               bias=bias_c[:, hec:hec + 1])
                        else:
                            ins = s.activation(out=dst_t[:, hec * 1024 + g * 512:hec * 1024 + (g + 1) * 512],
                                               in_=pf[2 + (i % 2)][:, 0:512], func=AF.Identity,
                                               bias=bias_c[:, hec:hec + 1])
                        cs.inc(ins)
                        mark(f"p3_qev{i}", cs)
        mark("qk_done", cs)

        @block.sync
        def _(s):
            s = AutoDrain(s)
            wt(s, "agv_done")
            wt(s, "conv_done")
            for ccj in range(8):
                QQ("avs").inc(s.dma_start(out=avs[:, ccj * 1024:(ccj + 1) * 1024],
                                   in_=AP(tensor=agv_out.tensor, offset=ccj * 128 * 1024,
                                          ap=[[1024, 128], [1, 1024]])), 16)
        mark("avs_loaded", QQ("avs"))
        for tt in range(8):
            for hf in range(2):
                i = tt * 2 + hf
                @block.tensor
                def _(t, tt=tt, hf=hf, i=i):
                    if i == 0:
                        wt(t, "avs_loaded")
                        wt(t, "conv_done")
                    if i >= 2:
                        wt(t, f"p3_vev{i - 2}")
                    bank = pf[4 + (i % 2)]
                    for ccj in range(8):
                        t.matmul(bank[:, 0:512],
                                 qkvT[:, (16 + ccj) * 1024 + tt * 128:(16 + ccj) * 1024 + (tt + 1) * 128],
                                 avs[:, ccj * 1024 + hf * 512:ccj * 1024 + (hf + 1) * 512],
                                 start=(ccj == 0), stop=False)
                    ct.inc(t.matmul(bank[:, 0:512], ones1b[0:1, :], bvrow[0:1, hf * 512:(hf + 1) * 512],
                                    start=False, stop=True))
                    mark(f"p3_vmm{i}", ct)
                @block.vector
                def _(v, tt=tt, hf=hf, i=i):
                    v = AutoDrain(v)
                    wt(v, f"p3_vmm{i}")
                    cv.inc(v.tensor_copy(out=vhT[:, tt * 1024 + hf * 512:tt * 1024 + (hf + 1) * 512],
                                         in_=pf[4 + (i % 2)][:, 0:512]))
                    mark(f"p3_vev{i}", cv)
        mark("vh_done", cv)

        a3 = ExitStack()
        ae3 = a3.enter_context
        oT = ae3(nc.sbuf_tensor("oT", [128, 8192], BF16))
        pnb = ae3(nc.sbuf_tensor("pnb", [128, 4096], BF16))
        pne = ae3(nc.sbuf_tensor("pne", [128, 4096], BF16))
        pTb = ae3(nc.sbuf_tensor("pTb", [128, 512], BF16))
        prow = ae3(nc.sbuf_tensor("prow", [128, 8], F32))
        def att_post(sg):
            for b in range(16):
                i = sg * 16 + b
                tt, h = i // 16, i % 16
                hec, hp = h // 2, h % 2
                sl2 = i % 4
                g4 = i // 4
                po = (sg % 2) * 2048
                @block.vector
                def _(v, i=i, b=b, sl2=sl2, g4=g4, po=po):
                    v = AutoDrain(v)
                    wt(v, f"p3_exp{g4}")
                    v.reduce_sum(out=prow[:, sl2 * 2 + 1:sl2 * 2 + 2],
                                 in_=pnb[:, po + b * 128:po + (b + 1) * 128], axis=AX)
                    v.reciprocal(out=prow[:, sl2 * 2 + 1:sl2 * 2 + 2], in_=prow[:, sl2 * 2 + 1:sl2 * 2 + 2])
                    cv.inc(v.tensor_scalar(out=pnb[:, po + b * 128:po + (b + 1) * 128],
                                           in0=pnb[:, po + b * 128:po + (b + 1) * 128],
                                           scalar1=prow[:, sl2 * 2 + 1:sl2 * 2 + 2],
                                           scalar2=None, op0=ALU.mult))
                    mark(f"p3_nrm{i}", cv)
                @block.tensor
                def _(t, i=i, b=b, sl2=sl2, po=po):
                    wt(t, f"p3_nrm{i}")
                    if i >= 2:
                        wt(t, f"p3_pT{i - 2}")
                    ct.inc(t.transpose(pbx[i % 2][:, 0:128], pnb[:, po + b * 128:po + (b + 1) * 128], identb[:, :]))
                    mark(f"p3_ptr{i}", ct)
                @block.vector
                def _(v, i=i, sl2=sl2):
                    v = AutoDrain(v)
                    wt(v, f"p3_ptr{i}")
                    if i >= 4:
                        wt(v, f"p3_omm{i - 4}")
                    cv.inc(v.tensor_copy(out=pTb[:, sl2 * 128:(sl2 + 1) * 128], in_=pbx[i % 2][:, 0:128]))
                    mark(f"p3_pT{i}", cv)
                @block.tensor
                def _(t, tt=tt, h=h, i=i, hp=hp, sl2=sl2):
                    wt(t, f"p3_pT{i}")
                    if hp == 0 and i >= 3:
                        wt(t, f"p3_oev{i - 3}")
                    ct.inc(t.matmul(pf[4 + ((i // 2) % 2)][hp * 64:hp * 64 + 64, 0:128],
                                    vhT[:, tt * 1024 + h * 64:tt * 1024 + h * 64 + 64],
                                    pTb[:, sl2 * 128:(sl2 + 1) * 128], start=True, stop=True))
                    mark(f"p3_omm{i}", ct)
                if hp == 1:
                    @block.vector
                    def _(v, tt=tt, hec=hec, i=i):
                        v = AutoDrain(v)
                        wt(v, f"p3_omm{i}")
                        cv.inc(v.tensor_copy(out=oT[:, hec * 1024 + tt * 128:hec * 1024 + (tt + 1) * 128],
                                             in_=pf[4 + ((i // 2) % 2)][:, 0:128]))
                        mark(f"p3_oev{i}", cv)

        for sg in range(8):
            if sg >= 1:
                att_post(sg - 1)
            for p16 in range(16):
                i = sg * 16 + p16
                tt, h = i // 16, i % 16
                hec, hp = h // 2, h % 2
                sl2 = i % 4
                @block.tensor
                def _(t, tt=tt, h=h, i=i, hec=hec, hp=hp):
                    if i == 0:
                        wt(t, "qk_done")
                        wt(t, "vh_done")
                    if i >= 4:
                        wt(t, f"p3_sub{i - 4}")
                    bank = pf[i % 4]
                    t.matmul(bank[:, 0:128],
                             qhT[hp * 64:hp * 64 + 64, hec * 1024 + tt * 128:hec * 1024 + (tt + 1) * 128],
                             khT[hp * 64:hp * 64 + 64, hec * 1024 + tt * 128:hec * 1024 + (tt + 1) * 128],
                             start=True, stop=False)
                    ct.inc(t.matmul(bank[:, 0:128], identb[:, :], maskb[:, :],
                                    start=False, stop=True))
                    mark(f"p3_smm{i}", ct)
                @block.vector
                def _(v, i=i, p16=p16, sl2=sl2, sg=sg):
                    v = AutoDrain(v)
                    wt(v, f"p3_smm{i}")
                    if p16 == 0 and sg >= 2:
                        wt(v, f"p3_exp{sg * 4 - 5}")
                    bank = pf[i % 4]
                    v.reduce_max(out=prow[:, sl2 * 2:sl2 * 2 + 1], in_=bank[:, 0:128],
                                 axis=AX, negate=True)
                    cv.inc(v.tensor_scalar(out=pne[:, (sg % 2) * 2048 + p16 * 128:(sg % 2) * 2048 + (p16 + 1) * 128],
                                           in0=bank[:, 0:128],
                                           scalar1=prow[:, sl2 * 2:sl2 * 2 + 1],
                                           scalar2=None, op0=ALU.add))
                    mark(f"p3_sub{i}", cv)
            @block.scalar
            def _(s, sg=sg):
                s = AutoDrain(s)
                if sg >= 2:
                    wt(s, f"p3_ptr{sg * 16 - 17}")
                for gg in range(4):
                    g4 = sg * 4 + gg
                    wt(s, f"p3_sub{g4 * 4 + 3}")
                    cs.inc(s.activation(out=pnb[:, (sg % 2) * 2048 + gg * 512:(sg % 2) * 2048 + (gg + 1) * 512],
                                        in_=pne[:, (sg % 2) * 2048 + gg * 512:(sg % 2) * 2048 + (gg + 1) * 512],
                                        func=AF.Exp))
                    mark(f"p3_exp{g4}", cs)
        att_post(7)
        mark("att_done", cv)
        q3.close()
        qk_sc.close()

        o3 = ExitStack()
        oe3 = o3.enter_context
        o2T = oe3(nc.sbuf_tensor("o2T", [128, 8192], BF16, side="right"))
        wob = [oe3(nc.sbuf_tensor(f"wob{i}", [128, 1024], BF16, side="right")) for i in range(2)]
        for wc in range(8):
            @block.sync
            def _(s, wc=wc):
                s = AutoDrain(s)
                if wc == 0:
                    wt(s, "ago_done")
                    wt(s, "att_done")
                if wc >= 2:
                    wt(s, f"p3_o2mmlast{wc - 2}")
                srcap = AP(tensor=ago_out.tensor, offset=wc * 128,
                           ap=[[1024, 128], [131072, 8], [1, 128]])
                QQ(f"ow{wc % 2}").inc(s.dma_start(out=wob[wc % 2][:, :], in_=srcap), 16)
                mark(f"p3_o2w{wc}", QQ(f"ow{wc % 2}"))
            for g in range(2):
                i = wc * 2 + g
                @block.tensor
                def _(t, wc=wc, g=g, i=i):
                    if g == 0:
                        wt(t, f"p3_o2w{wc}")
                    if i >= 2:
                        wt(t, f"p3_o2ev{i - 2}")
                    for hec in range(8):
                        ins = t.matmul(pf[2 + (i % 2)][:, 0:512], wob[wc % 2][:, hec * 128:(hec + 1) * 128],
                                       oT[:, hec * 1024 + g * 512:hec * 1024 + (g + 1) * 512],
                                       start=(hec == 0), stop=(hec == 7))
                    ct.inc(ins)
                    mark(f"p3_o2mm{i}", ct)
                    if g == 1:
                        mark(f"p3_o2mmlast{wc}", ct)
                @block.vector
                def _(v, i=i, wc=wc, g=g):
                    v = AutoDrain(v)
                    wt(v, f"p3_o2mm{i}")
                    cv.inc(v.tensor_scalar(out=o2T[:, wc * 1024 + g * 512:wc * 1024 + (g + 1) * 512],
                                           in0=pf[2 + (i % 2)][:, 0:512],
                                           scalar1=bocol[:, wc:wc + 1], scalar2=None, op0=ALU.add))
                    mark(f"p3_o2ev{i}", cv)
        mark("o2_done", cv)

        a3.close()
        xt3_st = ExitStack()
        xt3 = xt3_st.enter_context(nc.sbuf_tensor("xt3", [128, C * T], F32))
        pr3 = ExitStack()
        pre3 = pr3.enter_context
        pwb = [pre3(nc.sbuf_tensor(f"pwb{i}", [128, 1024], BF16)) for i in range(2)]
        x2b = [pre3(nc.sbuf_tensor(f"x2b{i}", [128, 512], F32)) for i in range(2)]
        for dmc in range(C):
            @block.sync
            def _(s, dmc=dmc):
                s = AutoDrain(s)
                if dmc == 0:
                    wt(s, "agp_done")
                    wt(s, "o2_done")
                if dmc >= 2:
                    wt(s, f"p3_pmmlast{dmc - 2}")
                srcap = AP(tensor=agp_out.tensor, offset=dmc * 128,
                           ap=[[4096, 128], [524288, 8], [1, 128]])
                QQ(f"pw{dmc % 2}").inc(s.dma_start(out=pwb[dmc % 2][:, :], in_=srcap), 16)
                mark(f"p3_pw{dmc}", QQ(f"pw{dmc % 2}"))
            for g in range(2):
                i = dmc * 2 + g
                @block.sync
                def _(s, dmc=dmc, g=g, i=i):
                    s = AutoDrain(s)
                    if i < 2:
                        wt(s, "o2_done")
                    if i >= 2:
                        wt(s, f"p3_pev{i - 2}")
                    QQ(f"x2b{i % 2}").inc(s.dma_start(out=x2b[i % 2][:, :],
                                       in_=x2_h[dmc * 128:(dmc + 1) * 128, g * 512:(g + 1) * 512]), 16)
                    mark(f"p3_x2b{i}", QQ(f"x2b{i % 2}"))
                @block.tensor
                def _(t, dmc=dmc, g=g, i=i):
                    if g == 0:
                        wt(t, f"p3_pw{dmc}")
                    if i >= 2:
                        wt(t, f"p3_pev{i - 2}")
                    for wc in range(8):
                        ins = t.matmul(pf[2 + (i % 2)][:, 0:512], pwb[dmc % 2][:, wc * 128:(wc + 1) * 128],
                                       o2T[:, wc * 1024 + g * 512:wc * 1024 + (g + 1) * 512],
                                       start=(wc == 0), stop=(wc == 7))
                    ct.inc(ins)
                    mark(f"p3_pmm{i}", ct)
                    if g == 1:
                        mark(f"p3_pmmlast{dmc}", ct)
                @block.vector
                def _(v, i=i, dmc=dmc, g=g):
                    v = AutoDrain(v)
                    wt(v, f"p3_pmm{i}")
                    wt(v, f"p3_x2b{i}")
                    cv.inc(v.scalar_tensor_tensor(out=xt3[:, dmc * T + g * 512:dmc * T + (g + 1) * 512],
                                                  in0=x2b[i % 2][:, :], scalar=pbcol[:, dmc:dmc + 1],
                                                  in1=pf[2 + (i % 2)][:, 0:512], op0=ALU.add, op1=ALU.add))
                    mark(f"p3_pev{i}", cv)
        mark("p3_done", cv)
        pr3.close()
        o3.close()
        xt_final = xt3
    else:
        xt_final = xt

    if _PHASES >= 4:
        sh2_st = ExitStack()
        shadow2 = sh2_st.enter_context(nc.sbuf_tensor("shadow2", [128, C * 512], BF16))
        ffn_phase(xt_final, shadow2, "p4", ar2_in, ar2_out, "p3_done")
        final_mark = "p4_done"
        sh2_st.close()
    else:
        final_mark = {1: "p1_done", 2: "p2_done", 3: "p3_done"}[_PHASES]

    @block.sync
    def _(s):
        s = AutoDrain(s)
        wt(s, final_mark)
        s.wait_ge(cv.h, cv.n)
        for c in range(C):
            QQ("out").inc(s.dma_start(out=OUT[c * 128:(c + 1) * 128, :], in_=xt_final[:, c * T:(c + 1) * T]), 16)
        for nm_, sm_ in sorted(_sems.items()):
            s.wait_ge(sm_.h, sm_.n)

    if _PHASES >= 3:
        xt3_st.close()
    else:
        xt_st.close()
    st.close()
    return nc


_tables_cache = None


def kernel(**ins):
    global _tables_cache
    if _tables_cache is None:
        _tables_cache = _host_tables()
    tables = _tables_cache
    x = np.asarray(ins["L0_attn_a_u"])
    fW = np.ascontiguousarray(ins["L0_ffn_W"], dtype=np.float32)
    cw = np.ascontiguousarray(ins["L0_attn_b_conv_w"], dtype=np.float32)
    wq = np.ascontiguousarray(ins["L0_attn_b_wq"], dtype=np.float32).reshape(1024, 1024)
    wk = np.ascontiguousarray(ins["L0_attn_b_wk"], dtype=np.float32).reshape(1024, 1024)
    wv = np.ascontiguousarray(ins["L0_attn_b_wv"], dtype=np.float32).reshape(1024, 1024)
    wo = np.ascontiguousarray(ins["L0_attn_b_wo"], dtype=np.float32).reshape(1024, 1024)
    pw = np.ascontiguousarray(ins["L0_attn_b_proj_w"], dtype=np.float32)
    Bp = np.asarray(ins["L0_attn_a_Bp"], dtype=np.float32)
    bpil = np.zeros((128, 2), np.float32)
    bpil[0::2, 0] = Bp[:, 0]
    bpil[1::2, 1] = Bp[:, 0]

    nc = build_nc()
    in_maps = []
    for cid in range(NCORE):
        m = {}
        m["x0T"] = np.ascontiguousarray(x[0, cid * T:(cid + 1) * T, :].T)
        m["aLam"] = np.asarray(ins["L0_attn_a_Lambda_ri"], np.float32)
        m["aP"] = np.asarray(ins["L0_attn_a_p_ri"], np.float32)
        m["aQ"] = np.asarray(ins["L0_attn_a_q_ri"], np.float32)
        m["aVc"] = np.asarray(ins["L0_attn_a_Vc_ri"], np.float32)
        m["aCt"] = np.asarray(ins["L0_attn_a_Ct_ri"], np.float32)
        m["aBpil"] = bpil
        m["aD"] = np.asarray(ins["L0_attn_a_D"], np.float32).reshape(1, 1)
        m["aLS"] = np.asarray(ins["L0_attn_a_log_step"], np.float32).reshape(1, 1)
        m["cwsh"] = np.ascontiguousarray(cw[:, cid * 512:(cid + 1) * 512, :])
        m["cb"] = np.asarray(ins["L0_attn_b_conv_b"], np.float32)
        m["fWsh"] = np.ascontiguousarray(
            np.concatenate([fW[oc * 128:(oc + 1) * 128, :] for oc in (cid, cid + 8, cid + 16, cid + 24)], axis=0))
        m["wqsh"] = np.ascontiguousarray(wq[cid * 128:(cid + 1) * 128, :])
        m["wksh"] = np.ascontiguousarray(wk[cid * 128:(cid + 1) * 128, :])
        m["wvsh"] = np.ascontiguousarray(wv[cid * 128:(cid + 1) * 128, :])
        m["wosh"] = np.ascontiguousarray(wo[cid * 128:(cid + 1) * 128, :])
        m["pwsh"] = np.ascontiguousarray(pw[cid * 128:(cid + 1) * 128, :])
        m["bq"] = np.asarray(ins["L0_attn_b_bq"], np.float32)
        m["bk"] = np.asarray(ins["L0_attn_b_bk"], np.float32)
        m["bv"] = np.asarray(ins["L0_attn_b_bv"], np.float32)
        m["bo"] = np.asarray(ins["L0_attn_b_bo"], np.float32)
        m["pb"] = np.asarray(ins["L0_attn_b_proj_b"], np.float32)
        m["fb"] = np.asarray(ins["L0_ffn_b"], np.float32)
        m["fA"] = np.asarray(ins["L0_ffn_A"], np.float32)
        m["fB"] = np.asarray(ins["L0_ffn_Bmat"], np.float32)
        m["fL"] = np.asarray(ins["L0_ffn_LiMEs"], np.float32)
        oneL = np.zeros((8, 1), np.float32)
        oneR = np.zeros((8, 1), np.float32)
        if cid > 0:
            oneL[cid - 1, 0] = 1.0
        if cid < NCORE - 1:
            oneR[cid + 1, 0] = 1.0
        m["oneL"] = oneL
        m["oneR"] = oneR
        for k, vv in tables.items():
            m[k] = vv
        in_maps.append(m)

    res = run_bass_kernel_spmd(nc, in_maps, core_ids=list(range(NCORE)))
    parts = [np.asarray(res.results[cid]["y"]).T for cid in range(NCORE)]
    out = np.concatenate(parts, axis=0).reshape(1, NCORE * T, D).astype(np.float32)
    return out


# version 14 (final)
# speedup vs baseline: 1.0056x; 1.0056x over previous
import sys
sys.path.insert(0, "/opt/trn_rl_repo")
import os
import numpy as np
import concourse.bass as bass
import concourse.mybir as mybir
from concourse.bass_utils import run_bass_kernel_spmd
from contextlib import ExitStack

F32 = mybir.dt.float32
BF16 = mybir.dt.bfloat16
AX = mybir.AxisListType.X
AF = mybir.ActivationFunctionType
ALU = mybir.AluOpType
AP = bass.AP

NCORE = 8
T = 1024
C = 32
D = 4096

CN = {"inv4096": 0, "eps": 1, "neg1": 2, "two": 3, "c15": 4, "big": 5, "eighth": 6,
      "eps9": 7, "half": 8, "one": 9, "neg2": 10}
CONST_VALS = [1.0 / 4096, 1e-5, -1.0, 2.0, 1.5, 1e9, 0.125, 1e-9, 0.5, 1.0, -2.0]

_PHASES = int(os.environ.get("KPHASES", "4"))
_KDBG = int(os.environ.get("KDBG", "0"))


class Sem:
    def __init__(self, h):
        self.h = h
        self.n = 0
    def inc(self, instr, k=1):
        instr.then_inc(self.h, k)
        self.n += k
        return self.n


class AutoDrain:
    _OPS = {"tensor_copy", "tensor_tensor", "tensor_scalar", "scalar_tensor_tensor",
            "reduce_max", "reduce_sum", "reciprocal", "memset", "activation", "copy",
            "tensor_scalar_add", "tensor_scalar_mul", "select"}
    _CACHE = {}
    def __new__(cls, eng):
        if isinstance(eng, AutoDrain):
            return eng
        key = id(eng)
        if key in cls._CACHE:
            return cls._CACHE[key]
        obj = super().__new__(cls)
        obj._eng = eng
        obj._last_out = ((), ())
        obj._last_out2 = ((), ())
        cls._CACHE[key] = obj
        return obj
    def __init__(self, eng):
        pass
    @staticmethod
    def _tname(ap):
        try:
            return ap.tensor.name
        except Exception:
            return None
    def _handle(self, name, kwargs):
        ins = []
        for k in ("in_", "in0", "in1", "scalar1", "scalar2", "scalar", "bias", "scale", "mask",
                  "on_true", "on_false", "ap"):
            v = kwargs.get(k)
            n = self._tname(v) if v is not None and not isinstance(v, (int, float)) else None
            if n:
                ins.append(n)
        outs = []
        for k in ("out", "accum_out", "ap"):
            v = kwargs.get(k)
            n = self._tname(v) if v is not None and not isinstance(v, (int, float)) else None
            if n:
                outs.append(n)
        hazard = False
        for (po, pi) in (self._last_out, self._last_out2):
            if any(n in po for n in ins) or any(n in po or n in pi for n in outs):
                hazard = True
        if hazard:
            self._eng.drain()
            self._last_out2 = ((), ())
        else:
            self._last_out2 = self._last_out
        self._last_out = (tuple(outs), tuple(ins))
    def __getattr__(self, name):
        attr = getattr(self._eng, name)
        if name in AutoDrain._OPS and callable(attr):
            def wrapped(*a, **kw):
                if a and name == "memset":
                    kw2 = dict(kw)
                    kw2["ap"] = a[0]
                    self._handle(name, kw2)
                    return attr(*a, **kw)
                if a:
                    raise TypeError("AutoDrain ops must use kwargs only: " + name)
                self._handle(name, kw)
                return attr(**kw)
            return wrapped
        return attr


def _host_tables():
    L = 4096
    kk = np.arange(L)
    theta32 = (np.float32(-2 * np.pi) * (kk / L).astype(np.float32)).astype(np.float32)
    th = theta32.astype(np.float64)
    om = np.cos(th) + 1j * np.sin(th)
    u_c = (1 - om) / (1 + om)
    c_c = 2.0 / (1 + om)
    f32 = np.float32
    ucat = np.concatenate([np.tile(u_c.real.astype(f32)[None, :], (64, 1)),
                           np.tile(u_c.imag.astype(f32)[None, :], (64, 1))], axis=0)
    cre = c_c.real.astype(f32).reshape(32, 128).T.copy()
    cim = c_c.imag.astype(f32).reshape(32, 128).T.copy()
    t_idx = np.arange(32)
    rho = np.arange(128)
    ang_tr = 2 * np.pi * np.outer(t_idx, rho) / 32
    M1 = np.concatenate([np.cos(ang_tr), -np.sin(ang_tr)], axis=0).astype(f32)
    M2 = np.concatenate([np.sin(ang_tr), np.cos(ang_tr)], axis=0).astype(f32)
    pp = np.arange(128)
    ang_pr = 2 * np.pi * np.outer(pp, rho) / 4096
    TW1 = np.cos(ang_pr).astype(f32)
    TW2 = np.sin(ang_pr).astype(f32)
    tau = np.arange(32)
    ang_pt = 2 * np.pi * np.outer(pp, tau) / 32
    C1 = (np.cos(ang_pt) / 4096).astype(f32)
    C2 = (-np.sin(ang_pt) / 4096).astype(f32)
    ident = np.eye(128, dtype=f32)
    mask = np.where(np.tril(np.ones((128, 128), bool)), 0.0, -1e9).astype(f32)
    return dict(t_ucat=ucat, t_cre=cre, t_cim=cim, t_M1=M1, t_M2=M2, t_TW1=TW1,
                t_TW2=TW2, t_C1=C1, t_C2=C2, t_ident=ident, t_mask=mask)


def build_nc():
    nc = bass.Bass("TRN2", target_bir_lowering=False)
    I = {}
    def di(name, shape):
        I[name] = nc.dram_tensor(name, list(shape), F32, kind="ExternalInput").ap()
    di("x0T", [D, T])
    di("aLam", [64, 2]); di("aP", [64, 2]); di("aQ", [64, 2]); di("aVc", [64, 64, 2])
    di("aCt", [1, 64, 2]); di("aBpil", [128, 2]); di("aD", [1, 1]); di("aLS", [1, 1])
    di("cwsh", [3, 512, 3072]); di("cb", [3072])
    di("fWsh", [512, D])
    di("wqsh", [128, 1024]); di("wksh", [128, 1024]); di("wvsh", [128, 1024]); di("wosh", [128, 1024])
    di("pwsh", [128, D])
    di("bq", [16, 64]); di("bk", [16, 64]); di("bv", [16, 64]); di("bo", [1024])
    di("pb", [D])
    di("fb", [D]); di("fA", [D, 32]); di("fB", [32, D]); di("fL", [8, 32])
    di("oneL", [8, 1]); di("oneR", [8, 1])
    for n, s in [("t_ucat", [128, 4096]), ("t_cre", [128, 32]), ("t_cim", [128, 32]),
                 ("t_M1", [64, 128]), ("t_M2", [64, 128]), ("t_TW1", [128, 128]),
                 ("t_TW2", [128, 128]), ("t_C1", [128, 32]), ("t_C2", [128, 32]),
                 ("t_ident", [128, 128]), ("t_mask", [128, 128])]:
        di(n, s)
    OUT = nc.dram_tensor("y", [D, T], F32, kind="ExternalOutput").ap()
    DBG = nc.dram_tensor("dbg", [128, 4096], F32, kind="ExternalOutput").ap() if _KDBG else None

    kflat_h = nc.dram_tensor("kflat_h", [4096], BF16).ap()
    scr_h = nc.dram_tensor("scr_h", [128 * 4608 + 256], BF16).ap()
    agw_in = nc.dram_tensor("agw_in", [4 * 128 * 4096], BF16).ap()
    agw_out = nc.dram_tensor("agw_out", [NCORE * 4 * 128 * 4096], BF16).ap()
    agc_in = nc.dram_tensor("agc_in", [12 * 128 * 3072], BF16).ap()
    agc_out = nc.dram_tensor("agc_out", [NCORE * 12 * 128 * 3072], BF16).ap()
    agq_in = nc.dram_tensor("agq_in", [128 * 1024], BF16).ap()
    agq_out = nc.dram_tensor("agq_out", [NCORE * 128 * 1024], BF16).ap()
    agk_in = nc.dram_tensor("agk_in", [128 * 1024], BF16).ap()
    agk_out = nc.dram_tensor("agk_out", [NCORE * 128 * 1024], BF16).ap()
    agv_in = nc.dram_tensor("agv_in", [128 * 1024], BF16).ap()
    agv_out = nc.dram_tensor("agv_out", [NCORE * 128 * 1024], BF16).ap()
    ago_in = nc.dram_tensor("ago_in", [128 * 1024], BF16).ap()
    ago_out = nc.dram_tensor("ago_out", [NCORE * 128 * 1024], BF16).ap()
    agp_in = nc.dram_tensor("agp_in", [128 * 4096], BF16).ap()
    agp_out = nc.dram_tensor("agp_out", [NCORE * 128 * 4096], BF16).ap()
    ar1_in = nc.dram_tensor("ar1_in", [1, 2], F32).ap()
    ar1_out = nc.dram_tensor("ar1_out", [1, 2], F32).ap()
    ar2_in = nc.dram_tensor("ar2_in", [1, 2], F32).ap()
    ar2_out = nc.dram_tensor("ar2_out", [1, 2], F32).ap()
    agh_in = nc.dram_tensor("agh_in", [2 * 4096], BF16).ap()
    agh_out = nc.dram_tensor("agh_out", [16 * 4096], BF16).ap()
    x2_h = nc.dram_tensor("x2_h", [D, T], F32).ap()
    hfl_h = nc.dram_tensor("hfl_h", [512], BF16).ap()

    st = ExitStack()
    e = st.enter_context
    xt_st = ExitStack()
    xt = xt_st.enter_context(nc.sbuf_tensor("xt", [128, C * T], F32, side="right"))

    consts = e(nc.sbuf_tensor("consts", [128, 16], F32))
    ones1f = e(nc.sbuf_tensor("ones1f", [1, 128], F32))
    ones1b = e(nc.sbuf_tensor("ones1b", [1, 128], BF16))
    onescb = e(nc.sbuf_tensor("onescb", [128, 1], BF16))
    identf = e(nc.sbuf_tensor("identf", [128, 128], F32))
    identb = e(nc.sbuf_tensor("identb", [128, 128], BF16))
    maskb = e(nc.sbuf_tensor("maskb", [128, 128], BF16))
    DIb = e(nc.sbuf_tensor("DIb", [128, 128], BF16))
    bcol = e(nc.sbuf_tensor("bcol", [128, 32], F32))
    cbcol = e(nc.sbuf_tensor("cbcol", [128, 24], F32))
    bqcol = e(nc.sbuf_tensor("bqcol", [128, 8], F32))
    bkcol = e(nc.sbuf_tensor("bkcol", [128, 8], F32))
    bocol = e(nc.sbuf_tensor("bocol", [128, 8], F32))
    pbcol = e(nc.sbuf_tensor("pbcol", [128, 32], F32))
    bvrow = e(nc.sbuf_tensor("bvrow", [1, 1024], BF16))
    limes_b = e(nc.sbuf_tensor("limes_b", [8, 32], BF16))
    liT = e(nc.sbuf_tensor("liT", [32, 8], F32))
    onelb = e(nc.sbuf_tensor("onelb", [8, 1], BF16))
    onerb = e(nc.sbuf_tensor("onerb", [8, 1], BF16))
    t2s_c = e(nc.sbuf_tensor("t2s_c", [128, 1], F32))
    D_c = e(nc.sbuf_tensor("D_c", [128, 1], F32))
    arin_s = e(nc.sbuf_tensor("arin_s", [1, 2], F32))
    armax_s = e(nc.sbuf_tensor("armax_s", [1, 2], F32))
    hsds = e(nc.sbuf_tensor("hsds", [128, 2], F32))
    srow = e(nc.sbuf_tensor("srow", [1, 512], F32))
    brow = e(nc.sbuf_tensor("brow", [1, 512], F32))
    s_bc = [e(nc.sbuf_tensor(f"s_bc{g}", [128, 512], F32)) for g in range(2)]
    b_bc = [e(nc.sbuf_tensor(f"b_bc{g}", [128, 512], F32)) for g in range(2)]
    xbb = e(nc.sbuf_tensor("xbb", [128, 1024], BF16))
    sqb2 = e(nc.sbuf_tensor("sqb2", [128, 1024], BF16))
    t1f = e(nc.sbuf_tensor("t1f", [128, 512], F32))

    pf = [e(nc.psum_tensor(f"pf{i}", [128, 512], F32)) for i in range(6)]
    pbx = [e(nc.psum_tensor(f"pbx{i}", [128, 1024], BF16)) for i in range(2)]

    _sems = {}
    def QQ(name):
        if name not in _sems:
            _sems[name] = Sem(e(nc.semaphore("q_" + name)))
        return _sems[name]
    ct = Sem(e(nc.semaphore("ct")))
    cv = Sem(e(nc.semaphore("cv")))
    cs = Sem(e(nc.semaphore("cs")))
    block = e(nc.Block())

    cp = {}
    def mark(name, sem):
        cp[name] = (sem, sem.n)
    def wt(eng, name):
        s, v = cp[name]
        if v > 0:
            eng.wait_ge(s.h, v)

    @block.sync
    def _(s):
        s = AutoDrain(s)
        for c in range(C):
            QQ("x0").inc(s.dma_start(out=xt[:, c * T:(c + 1) * T], in_=I["x0T"][c * 128:(c + 1) * 128, :]), 16)
    mark("x0_loaded", QQ("x0"))

    Lcat = e(nc.sbuf_tensor("Lcat", [128, 1], F32))
    g1 = ExitStack()
    ge = g1.enter_context
    Lnat = ge(nc.sbuf_tensor("Lnat", [64, 2], F32))
    rows6 = ge(nc.sbuf_tensor("rows6", [1, 384], F32))
    Bprow = ge(nc.sbuf_tensor("Bprow", [128, 2], F32))
    Vcnat = ge(nc.sbuf_tensor("Vcnat", [64, 128], F32))
    Vt = ge(nc.sbuf_tensor("Vt", [128, 64], F32))
    lsD = ge(nc.sbuf_tensor("lsD", [1, 2], F32))
    m2s = ge(nc.sbuf_tensor("m2s", [1, 2], F32))
    bnat = ge(nc.sbuf_tensor("bnat", [32, 128], F32))
    cbnat = ge(nc.sbuf_tensor("cbnat", [24, 128], F32))
    pbnat = ge(nc.sbuf_tensor("pbnat", [32, 128], F32))
    bqnat = ge(nc.sbuf_tensor("bqnat", [8, 128], F32))
    bknat = ge(nc.sbuf_tensor("bknat", [8, 128], F32))
    bonat = ge(nc.sbuf_tensor("bonat", [8, 128], F32))
    limes_f = ge(nc.sbuf_tensor("limes_f", [8, 32], F32))
    sh1 = ge(nc.sbuf_tensor("sh1", [128, 4096], BF16))
    wstg = ge(nc.sbuf_tensor("wstg", [128, 4096], BF16))

    tbl = {}
    kst = ExitStack()
    ke = kst.enter_context
    for nm, shp in [("t_cre", [128, 32]), ("t_cim", [128, 32]), ("t_M1", [64, 128]),
                    ("t_M2", [64, 128]), ("t_TW1", [128, 128]), ("t_TW2", [128, 128]),
                    ("t_C1", [128, 32]), ("t_C2", [128, 32])]:
        tbl[nm] = ke(nc.sbuf_tensor("s" + nm, shp, F32))

    @block.sync
    def _(s):
        s = AutoDrain(s)
        QQ("tbl").inc(s.dma_start(out=identf[:, :], in_=I["t_ident"]), 16)
        for nm in tbl:
            QQ("tbl").inc(s.dma_start(out=tbl[nm][:, :], in_=I[nm]), 16)
    mark("tables_loaded", QQ("tbl"))

    @block.gpsimd
    def _(g):
        g = AutoDrain(g)
        QQ("small").inc(g.dma_start(out=Lnat[:, :], in_=I["aLam"]), 16)
        for j, (nm, off) in enumerate([("aCt", 0), ("aCt", 1), ("aP", 0), ("aP", 1), ("aQ", 0), ("aQ", 1)]):
            src = AP(tensor=I[nm].tensor, offset=off, ap=[[2, 64]])
            QQ("small").inc(g.dma_start(out=rows6[0:1, j * 64:(j + 1) * 64], in_=src,
                               allow_slow_non_contiguous=True), 16)
        QQ("small").inc(g.dma_start(out=Bprow[:, :], in_=I["aBpil"]), 16)
        QQ("small").inc(g.dma_start(out=Vcnat[:, :], in_=AP(tensor=I["aVc"].tensor, offset=0,
                                                   ap=[[128, 64], [1, 128]])), 16)
        QQ("small").inc(g.dma_start(out=lsD[0:1, 0:1], in_=I["aLS"]), 16)
        QQ("small").inc(g.dma_start(out=lsD[0:1, 1:2], in_=I["aD"]), 16)
        QQ("small").inc(g.dma_start(out=bnat[:, :], in_=AP(tensor=I["fb"].tensor, offset=0, ap=[[128, 32], [1, 128]])), 16)
        QQ("small").inc(g.dma_start(out=cbnat[:, :], in_=AP(tensor=I["cb"].tensor, offset=0, ap=[[128, 24], [1, 128]])), 16)
        QQ("small").inc(g.dma_start(out=pbnat[:, :], in_=AP(tensor=I["pb"].tensor, offset=0, ap=[[128, 32], [1, 128]])), 16)
        QQ("small").inc(g.dma_start(out=bqnat[:, :], in_=AP(tensor=I["bq"].tensor, offset=0, ap=[[128, 8], [1, 128]])), 16)
        QQ("small").inc(g.dma_start(out=bknat[:, :], in_=AP(tensor=I["bk"].tensor, offset=0, ap=[[128, 8], [1, 128]])), 16)
        QQ("small").inc(g.dma_start(out=bonat[:, :], in_=AP(tensor=I["bo"].tensor, offset=0, ap=[[128, 8], [1, 128]])), 16)
        QQ("small").inc(g.dma_start(out=bvrow[0:1, :], in_=AP(tensor=I["bv"].tensor, offset=0, ap=[[1, 1024]])), 16)
        QQ("small").inc(g.dma_start(out=limes_f[:, :], in_=I["fL"]), 16)
        QQ("small").inc(g.dma_start(out=identb[:, :], in_=I["t_ident"]), 16)
        QQ("small").inc(g.dma_start(out=maskb[:, :], in_=I["t_mask"]), 16)
        QQ("small").inc(g.dma_start(out=onelb[:, :], in_=I["oneL"]), 16)
        QQ("small").inc(g.dma_start(out=onerb[:, :], in_=I["oneR"]), 16)
    mark("small_loaded", QQ("small"))

    @block.vector
    def _(v):
        v = AutoDrain(v)
        for j, val in enumerate(CONST_VALS):
            v.memset(consts[:, j:j + 1], val)
        v.memset(ones1f[:, :], 1.0)
        v.memset(ones1b[:, :], 1.0)
        cv.inc(v.memset(onescb[:, :], 1.0))
    mark("consts_ready", cv)

    @block.gpsimd
    def _(g):
        g = AutoDrain(g)
        wt(g, "small_loaded")
        QQ("lcat").inc(g.dma_start(out=Lcat[0:64, 0:1], in_=Lnat[:, 0:1]), 16)
        QQ("lcat").inc(g.dma_start(out=Lcat[64:128, 0:1], in_=Lnat[:, 1:2]), 16)
    mark("lcat_ready", QQ("lcat"))

    @block.scalar
    def _(s):
        s = AutoDrain(s)
        wt(s, "small_loaded")
        cs.inc(s.activation(out=lsD[0:1, 0:1], in_=lsD[0:1, 0:1], func=AF.Exp))
    mark("step_ready", cs)

    @block.vector
    def _(v):
        v = AutoDrain(v)
        wt(v, "step_ready")
        wt(v, "consts_ready")
        v.reciprocal(out=m2s[0:1, 0:1], in_=lsD[0:1, 0:1])
        cv.inc(v.tensor_scalar(out=m2s[0:1, 0:1], in0=m2s[0:1, 0:1],
                               scalar1=consts[0:1, CN["two"]:CN["two"] + 1], scalar2=None, op0=ALU.mult))
    mark("m2s_ready", cv)

    @block.tensor
    def _(t):
        wt(t, "m2s_ready")
        wt(t, "tables_loaded")
        t.matmul(pf[0][:, 0:1], ones1f[0:1, :], m2s[0:1, 0:1], start=True, stop=True)
        t.matmul(pf[0][:, 1:2], ones1f[0:1, :], lsD[0:1, 1:2], start=True, stop=True)
        ct.inc(t.transpose(pf[1][0:128, 0:64], Vcnat[:, :], identf[0:64, 0:64]))
    mark("bcast1_done", ct)

    @block.vector
    def _(v):
        v = AutoDrain(v)
        wt(v, "bcast1_done")
        v.tensor_copy(out=t2s_c[:, :], in_=pf[0][:, 0:1])
        v.tensor_copy(out=D_c[:, :], in_=pf[0][:, 1:2])
        v.tensor_copy(out=Vt[:, :], in_=pf[1][:, 0:64])
        cv.inc(v.tensor_scalar(out=DIb[:, :], in0=identf[:, :], scalar1=D_c[:, 0:1],
                               scalar2=None, op0=ALU.mult))
    mark("vt_ready", cv)

    @block.tensor
    def _(t):
        wt(t, "vt_ready")
        t.matmul(pf[2][0:1, 0:64], Bprow[:, 0:1], Vt[:, :], start=True, stop=True)
        ct.inc(t.matmul(pf[2][0:1, 64:128], Bprow[:, 1:2], Vt[:, :], start=True, stop=True))
    mark("B_done", ct)

    @block.tensor
    def _(t):
        wt(t, "B_done")
        t.transpose(pf[3][0:128, 0:32], bnat[:, :], identf[0:32, 0:32])
        t.transpose(pf[3][0:128, 32:56], cbnat[:, :], identf[0:24, 0:24])
        t.transpose(pf[4][0:128, 0:32], pbnat[:, :], identf[0:32, 0:32])
        t.transpose(pf[4][0:128, 32:40], bqnat[:, :], identf[0:8, 0:8])
        t.transpose(pf[4][0:128, 40:48], bknat[:, :], identf[0:8, 0:8])
        t.transpose(pf[4][0:128, 48:56], bonat[:, :], identf[0:8, 0:8])
        ct.inc(t.transpose(pf[5][0:32, 0:8], limes_f[:, :], identf[0:8, 0:8]))
    mark("biasT_done", ct)

    @block.vector
    def _(v):
        v = AutoDrain(v)
        wt(v, "biasT_done")
        v.tensor_copy(out=bcol[:, :], in_=pf[3][:, 0:32])
        v.tensor_copy(out=cbcol[:, :], in_=pf[3][:, 32:56])
        v.tensor_copy(out=pbcol[:, :], in_=pf[4][:, 0:32])
        v.tensor_copy(out=bkcol[:, :], in_=pf[4][:, 40:48])
        v.tensor_copy(out=bocol[:, :], in_=pf[4][:, 48:56])
        v.tensor_copy(out=liT[:, :], in_=pf[5][0:32, 0:8])
        v.tensor_copy(out=limes_b[:, :], in_=limes_f[:, :])
        cv.inc(v.tensor_scalar(out=bqcol[:, :], in0=pf[4][:, 32:40],
                               scalar1=consts[:, CN["eighth"]:CN["eighth"] + 1], scalar2=None, op0=ALU.mult))
    mark("bias_ready", cv)

    dcat = ke(nc.sbuf_tensor("dcat", [128, 512], F32))
    dsh = ke(nc.sbuf_tensor("dsh", [64, 512], F32))
    scr1 = ke(nc.sbuf_tensor("scr1", [64, 512], F32))
    scr2 = ke(nc.sbuf_tensor("scr2", [64, 512], F32))
    RCPT = ke(nc.sbuf_tensor("RCPT", [128, 4096], F32))
    for qq in range(8):
        ks = slice(qq * 512, (qq + 1) * 512)
        @block.sync
        def _(s, qq=qq, ks=ks):
            s = AutoDrain(s)
            if qq >= 1:
                wt(s, f"kq_done{qq - 1}")
            QQ("kqld").inc(s.dma_start(out=dcat[:, :], in_=I["t_ucat"][:, ks]), 16)
            mark(f"kq_load{qq}", QQ("kqld"))
        @block.vector
        def _(v, qq=qq):
            v = AutoDrain(v)
            wt(v, f"kq_load{qq}")
            wt(v, "lcat_ready")
            wt(v, "m2s_ready")
            cv.inc(v.tensor_scalar(out=dcat[:, :], in0=dcat[:, :], scalar1=t2s_c[:, 0:1],
                                   scalar2=Lcat[:, 0:1], op0=ALU.mult, op1=ALU.subtract))
            mark(f"kq_dcat{qq}", cv)
        @block.gpsimd
        def _(g, qq=qq):
            g = AutoDrain(g)
            wt(g, f"kq_dcat{qq}")
            QQ("kq").inc(g.dma_start(out=dsh[0:64, :], in_=dcat[64:128, :]), 16)
            mark(f"kq_dsh{qq}", QQ("kq"))
        @block.scalar
        def _(s, qq=qq, ks=ks):
            s = AutoDrain(s)
            wt(s, f"kq_dsh{qq}")
            s.activation(out=scr1[0:64, :], in_=dcat[0:64, :], func=AF.Square)
            cs.inc(s.activation(out=scr2[0:64, :], in_=dsh[0:64, :], func=AF.Square))
            mark(f"kq_sq{qq}", cs)
        @block.vector
        def _(v, qq=qq, ks=ks):
            v = AutoDrain(v)
            wt(v, f"kq_sq{qq}")
            v.tensor_tensor(out=scr1[0:64, :], in0=scr1[0:64, :], in1=scr2[0:64, :], op=ALU.add)
            v.reciprocal(out=scr1[0:64, :], in_=scr1[0:64, :])
            v.tensor_tensor(out=RCPT[0:64, ks], in0=dcat[0:64, :], in1=scr1[0:64, :], op=ALU.mult)
            cv.inc(v.tensor_tensor(out=dsh[0:64, :], in0=dsh[0:64, :], in1=scr1[0:64, :], op=ALU.mult))
            mark(f"kq_rcp{qq}", cv)
        @block.gpsimd
        def _(g, qq=qq, ks=ks):
            g = AutoDrain(g)
            wt(g, f"kq_rcp{qq}")
            QQ("kq").inc(g.dma_start(out=RCPT[64:128, ks], in_=dsh[0:64, :]), 16)
            mark(f"kq_done{qq}", QQ("kq"))

    vstage = ke(nc.sbuf_tensor("vstage", [1, 512], F32))
    Brows = ke(nc.sbuf_tensor("Brows", [1, 128], F32))
    vneg = ke(nc.sbuf_tensor("vneg", [1, 256], F32))
    Vmat = ke(nc.sbuf_tensor("Vmat", [128, 8], F32))
    @block.vector
    def _(v):
        v = AutoDrain(v)
        wt(v, "B_done")
        wt(v, "small_loaded")
        v.tensor_copy(out=Brows[0:1, :], in_=pf[2][0:1, 0:128])
        ctr, cti = rows6[0:1, 0:64], rows6[0:1, 64:128]
        pr, pi = rows6[0:1, 128:192], rows6[0:1, 192:256]
        qr, qi = rows6[0:1, 256:320], rows6[0:1, 320:384]
        Br, Bi = Brows[0:1, 0:64], Brows[0:1, 64:128]
        def cmulj(ar, ai, br, bi, o):
            re = vstage[0:1, o * 128:o * 128 + 64]
            im = vstage[0:1, o * 128 + 64:o * 128 + 128]
            v.tensor_tensor(out=re, in0=ar, in1=br, op=ALU.mult)
            v.tensor_tensor(out=t1f[0:1, 0:64], in0=ai, in1=bi, op=ALU.mult)
            v.tensor_tensor(out=re, in0=re, in1=t1f[0:1, 0:64], op=ALU.add)
            v.tensor_tensor(out=im, in0=ar, in1=bi, op=ALU.mult)
            v.tensor_tensor(out=t1f[0:1, 0:64], in0=ai, in1=br, op=ALU.mult)
            v.tensor_tensor(out=im, in0=im, in1=t1f[0:1, 0:64], op=ALU.subtract)
        cmulj(ctr, cti, Br, Bi, 0)
        cmulj(ctr, cti, pr, pi, 1)
        cmulj(qr, qi, Br, Bi, 2)
        cmulj(qr, qi, pr, pi, 3)
        for j in range(4):
            ins = v.tensor_scalar(out=vneg[0:1, j * 64:(j + 1) * 64], in0=vstage[0:1, j * 128:j * 128 + 64],
                                  scalar1=consts[0:1, CN["neg1"]:CN["neg1"] + 1], scalar2=None, op0=ALU.mult)
        cv.inc(ins)
    mark("vprod", cv)

    @block.gpsimd
    def _(g):
        g = AutoDrain(g)
        wt(g, "vprod")
        for j in range(4):
            QQ("vmat").inc(g.dma_start(out=Vmat[0:64, 2 * j:2 * j + 1], in_=vstage[0:1, j * 128:j * 128 + 64]), 16)
            QQ("vmat").inc(g.dma_start(out=Vmat[64:128, 2 * j:2 * j + 1], in_=vstage[0:1, j * 128 + 64:j * 128 + 128]), 16)
            QQ("vmat").inc(g.dma_start(out=Vmat[0:64, 2 * j + 1:2 * j + 2], in_=vstage[0:1, j * 128 + 64:j * 128 + 128]), 16)
            QQ("vmat").inc(g.dma_start(out=Vmat[64:128, 2 * j + 1:2 * j + 2], in_=vneg[0:1, j * 64:(j + 1) * 64]), 16)
    mark("vmat", QQ("vmat"))

    kxy = ke(nc.sbuf_tensor("kxy", [128, 256], F32))
    arcat = ke(nc.sbuf_tensor("arcat", [128, 64], F32))
    gsc = ke(nc.sbuf_tensor("gsc", [128, 160], F32))
    @block.tensor
    def _(t):
        wt(t, "vmat")
        wt(t, "kq_done7")
        wt(t, "vt_ready")
        for tt in range(32):
            ins = t.matmul(pf[0][:, 8 * tt:8 * tt + 8], RCPT[:, tt * 128:(tt + 1) * 128], Vmat[:, :],
                           start=True, stop=True)
        ct.inc(ins)
    mark("kxy_mm", ct)
    @block.vector
    def _(v):
        v = AutoDrain(v)
        wt(v, "kxy_mm")
        cv.inc(v.tensor_copy(out=kxy[:, :], in_=pf[0][:, 0:256]))
    mark("kxy_s", cv)

    @block.vector
    def _(v):
        v = AutoDrain(v)
        wt(v, "kxy_s")
        k00r, k00i = kxy[:, 0:256:8], kxy[:, 1:256:8]
        k01r, k01i = kxy[:, 2:256:8], kxy[:, 3:256:8]
        k10r, k10i = kxy[:, 4:256:8], kxy[:, 5:256:8]
        k11r, k11i = kxy[:, 6:256:8], kxy[:, 7:256:8]
        g32 = lambda j: gsc[:, j * 32:(j + 1) * 32]
        v.tensor_scalar(out=g32(0), in0=k11r, scalar1=consts[:, CN["one"]:CN["one"] + 1], scalar2=None, op0=ALU.add)
        v.tensor_tensor(out=g32(1), in0=g32(0), in1=g32(0), op=ALU.mult)
        v.tensor_tensor(out=g32(2), in0=k11i, in1=k11i, op=ALU.mult)
        v.tensor_tensor(out=g32(1), in0=g32(1), in1=g32(2), op=ALU.add)
        v.reciprocal(out=g32(1), in_=g32(1))
        v.tensor_tensor(out=g32(2), in0=k10r, in1=g32(0), op=ALU.mult)
        v.tensor_tensor(out=g32(3), in0=k10i, in1=k11i, op=ALU.mult)
        v.tensor_tensor(out=g32(2), in0=g32(2), in1=g32(3), op=ALU.add)
        v.tensor_tensor(out=g32(2), in0=g32(2), in1=g32(1), op=ALU.mult)
        v.tensor_tensor(out=g32(3), in0=k10i, in1=g32(0), op=ALU.mult)
        v.tensor_tensor(out=g32(4), in0=k10r, in1=k11i, op=ALU.mult)
        v.tensor_tensor(out=g32(3), in0=g32(3), in1=g32(4), op=ALU.subtract)
        v.tensor_tensor(out=g32(3), in0=g32(3), in1=g32(1), op=ALU.mult)
        v.tensor_tensor(out=g32(0), in0=k01r, in1=g32(2), op=ALU.mult)
        v.tensor_tensor(out=g32(1), in0=k01i, in1=g32(3), op=ALU.mult)
        v.tensor_tensor(out=g32(0), in0=g32(0), in1=g32(1), op=ALU.subtract)
        v.tensor_tensor(out=g32(1), in0=k01r, in1=g32(3), op=ALU.mult)
        v.tensor_tensor(out=g32(4), in0=k01i, in1=g32(2), op=ALU.mult)
        v.tensor_tensor(out=g32(1), in0=g32(1), in1=g32(4), op=ALU.add)
        v.tensor_tensor(out=g32(0), in0=k00r, in1=g32(0), op=ALU.subtract)
        v.tensor_tensor(out=g32(1), in0=k00i, in1=g32(1), op=ALU.subtract)
        v.tensor_tensor(out=g32(2), in0=tbl["t_cre"][:, :], in1=g32(0), op=ALU.mult)
        v.tensor_tensor(out=g32(3), in0=tbl["t_cim"][:, :], in1=g32(1), op=ALU.mult)
        v.tensor_tensor(out=arcat[:, 0:32], in0=g32(2), in1=g32(3), op=ALU.subtract)
        v.tensor_tensor(out=g32(2), in0=tbl["t_cre"][:, :], in1=g32(1), op=ALU.mult)
        v.tensor_tensor(out=g32(3), in0=tbl["t_cim"][:, :], in1=g32(0), op=ALU.mult)
        cv.inc(v.tensor_tensor(out=arcat[:, 32:64], in0=g32(2), in1=g32(3), op=ALU.add))
    mark("arcat", cv)

    arcT = ke(nc.sbuf_tensor("arcT", [64, 128], F32))
    Spre = ke(nc.sbuf_tensor("Spre", [128, 128], F32))
    Spim = ke(nc.sbuf_tensor("Spim", [128, 128], F32))
    Ksb = ke(nc.sbuf_tensor("Ksb", [128, 32], BF16))
    K2sb = ke(nc.sbuf_tensor("K2sb", [32, 128], BF16))
    @block.tensor
    def _(t):
        wt(t, "arcat")
        ct.inc(t.transpose(pf[1][0:64, 0:128], arcat[:, :], identf[:, :]))
    mark("arcT_mm", ct)
    @block.vector
    def _(v):
        v = AutoDrain(v)
        wt(v, "arcT_mm")
        cv.inc(v.tensor_copy(out=arcT[0:64, :], in_=pf[1][0:64, 0:128]))
    mark("arcT_s", cv)
    @block.tensor
    def _(t):
        wt(t, "arcT_s")
        wt(t, "vprod")
        t.matmul(pf[2][:, 0:128], arcT[0:64, :], tbl["t_M1"][:, :], start=True, stop=True)
        ct.inc(t.matmul(pf[3][:, 0:128], arcT[0:64, :], tbl["t_M2"][:, :], start=True, stop=True))
    mark("S_mm", ct)
    @block.vector
    def _(v):
        v = AutoDrain(v)
        wt(v, "S_mm")
        v.tensor_tensor(out=Spre[:, :], in0=pf[2][:, 0:128], in1=tbl["t_TW1"][:, :], op=ALU.mult)
        v.tensor_tensor(out=t1f[:, 0:128], in0=pf[3][:, 0:128], in1=tbl["t_TW2"][:, :], op=ALU.mult)
        v.tensor_tensor(out=Spre[:, :], in0=Spre[:, :], in1=t1f[:, 0:128], op=ALU.subtract)
        v.tensor_tensor(out=Spim[:, :], in0=pf[2][:, 0:128], in1=tbl["t_TW2"][:, :], op=ALU.mult)
        v.tensor_tensor(out=t1f[:, 0:128], in0=pf[3][:, 0:128], in1=tbl["t_TW1"][:, :], op=ALU.mult)
        cv.inc(v.tensor_tensor(out=Spim[:, :], in0=Spim[:, :], in1=t1f[:, 0:128], op=ALU.add))
    mark("Sp", cv)
    @block.tensor
    def _(t):
        wt(t, "Sp")
        wt(t, "bias_ready")
        t.matmul(pf[4][:, 0:32], Spre[:, :], tbl["t_C1"][:, :], start=True, stop=False)
        ct.inc(t.matmul(pf[4][:, 0:32], Spim[:, :], tbl["t_C2"][:, :], start=False, stop=True))
    mark("K_mm", ct)
    @block.vector
    def _(v):
        v = AutoDrain(v)
        wt(v, "K_mm")
        cv.inc(v.tensor_copy(out=Ksb[:, :], in_=pf[4][:, 0:32]))
    mark("Ksb_s", cv)
    @block.tensor
    def _(t):
        wt(t, "Ksb_s")
        ct.inc(t.transpose(pbx[0][0:32, 0:128], Ksb[:, :], identb[:, :]))
    mark("K2_mm", ct)
    @block.vector
    def _(v):
        v = AutoDrain(v)
        wt(v, "K2_mm")
        cv.inc(v.tensor_copy(out=K2sb[0:32, :], in_=pbx[0][0:32, 0:128]))
    mark("K2_s", cv)

    zsb = ke(nc.sbuf_tensor("zsb", [128, 288], BF16))
    @block.vector
    def _(v):
        v = AutoDrain(v)
        cv.inc(v.memset(zsb[:, :], 0.0))
    mark("zsb0", cv)
    @block.gpsimd
    def _(g):
        g = AutoDrain(g)
        wt(g, "zsb0")
        wt(g, "K2_s")
        QQ("scr").inc(g.dma_start(out=kflat_h, in_=K2sb[0:32, :]), 16)
        for qq in range(16):
            dst = AP(tensor=scr_h.tensor, offset=qq * 288, ap=[[4608, 128], [1, 288]])
            QQ("scr").inc(g.dma_start(out=dst, in_=zsb[:, :]), 16)
        g.wait_ge(QQ("scr").h, QQ("scr").n)
        dstk = AP(tensor=scr_h.tensor, offset=384, ap=[[4609, 128], [1, 4096]])
        srck = AP(tensor=kflat_h.tensor, offset=0, ap=[[0, 128], [1, 4096]])
        QQ("scr2").inc(g.dma_start(out=dstk, in_=srck), 16)
    mark("scr_k", QQ("scr2"))
    for ocl in range(4):
        @block.gpsimd
        def _(g, ocl=ocl):
            g = AutoDrain(g)
            if ocl == 0:
                wt(g, "small_loaded")
            else:
                wt(g, f"wc_out{ocl - 1}")
            QQ("wcld").inc(g.dma_start(out=sh1[:, :], in_=I["fWsh"][ocl * 128:(ocl + 1) * 128, :]), 16)
            mark(f"wc_ld{ocl}", QQ("wcld"))
        for dc in range(C):
            i = ocl * C + dc
            @block.tensor
            def _(t, ocl=ocl, dc=dc, i=i):
                if dc == 0:
                    wt(t, f"wc_ld{ocl}")
                    wt(t, "small_loaded")
                if i < 2:
                    wt(t, "K2_s")
                if i >= 2:
                    wt(t, f"wc_cv{i - 2}")
                ct.inc(t.transpose(pbx[i % 2][:, 0:128], sh1[:, dc * 128:(dc + 1) * 128], identb[:, :]))
                mark(f"wc_ct{i}", ct)
            @block.vector
            def _(v, dc=dc, i=i):
                v = AutoDrain(v)
                wt(v, f"wc_ct{i}")
                if dc == 0:
                    s_, v_ = cp.get(f"wc_out{(i // C) - 1}", (None, 0))
                    if s_ is not None and v_ > 0:
                        v.wait_ge(s_.h, v_)
                cv.inc(v.tensor_copy(out=wstg[:, dc * 128:(dc + 1) * 128], in_=pbx[i % 2][:, 0:128]))
                mark(f"wc_cv{i}", cv)
        @block.gpsimd
        def _(g, ocl=ocl):
            g = AutoDrain(g)
            wt(g, f"wc_cv{ocl * C + C - 1}")
            dst = AP(tensor=agw_in.tensor, offset=ocl * 128 * 4096, ap=[[4096, 128], [1, 4096]])
            QQ("wcout").inc(g.dma_start(out=dst, in_=wstg[:, :]), 16)
            mark(f"wc_out{ocl}", QQ("wcout"))
    cp["wcast_doneA"] = cp["wc_out3"]
    cp["wcast_doneB"] = cp["wc_out3"]

    castbufs = [sh1, wstg]
    for j in range(12):
        w_, dcl = j // 4, j % 4
        @block.gpsimd
        def _(g, j=j, w_=w_, dcl=dcl):
            g = AutoDrain(g)
            if j < 2:
                wt(g, "wcast_doneA")
                wt(g, "wcast_doneB")
            else:
                wt(g, f"cc_g1_{j - 2}")
            QQ(f"ccg{j % 2}").inc(g.dma_start(out=castbufs[j % 2][:, 0:3072],
                               in_=AP(tensor=I["cwsh"].tensor,
                                      offset=w_ * 512 * 3072 + dcl * 128 * 3072, ap=[[3072, 128], [1, 3072]])), 16)
            mark(f"cc_g0_{j}", QQ(f"ccg{j % 2}"))
            wt(g, f"cc_g0_{j}")
            dst = AP(tensor=agc_in.tensor, offset=(dcl * 3 + w_) * 128 * 3072, ap=[[3072, 128], [1, 3072]])
            QQ(f"ccg{j % 2}").inc(g.dma_start(out=dst, in_=castbufs[j % 2][:, 0:3072]), 16)
            mark(f"cc_g1_{j}", QQ(f"ccg{j % 2}"))
    for jq, (nm, srcI, dstH, cols) in enumerate([("q", "wqsh", agq_in, 1024), ("k", "wksh", agk_in, 1024),
                                 ("v", "wvsh", agv_in, 1024), ("o", "wosh", ago_in, 1024),
                                 ("p", "pwsh", agp_in, 4096)]):
        @block.gpsimd
        def _(g, jq=jq, nm=nm, srcI=srcI, dstH=dstH, cols=cols):
            g = AutoDrain(g)
            prevm = f"cc_g1_{10 + (jq % 2)}" if jq < 2 else f"sc_{'qkvop'[jq - 2]}1"
            wt(g, prevm)
            QQ(f"ccg{jq % 2}").inc(g.dma_start(out=castbufs[jq % 2][:, 0:cols], in_=I[srcI]), 16)
            mark(f"sc_{nm}0", QQ(f"ccg{jq % 2}"))
            wt(g, f"sc_{nm}0")
            QQ(f"ccg{jq % 2}").inc(g.dma_start(out=AP(tensor=dstH.tensor, offset=0, ap=[[cols, 128], [1, cols]]),
                               in_=castbufs[jq % 2][:, 0:cols]), 16)
            mark(f"sc_{nm}1", QQ(f"ccg{jq % 2}"))
    @block.gpsimd
    def _(g):
        g = AutoDrain(g)
        wt(g, "wcast_doneA")
        wt(g, "wcast_doneB")
        QQ("ccagw").inc(g.collective_compute("AllGather", ALU.bypass, replica_groups=[list(range(NCORE))],
                                    ins=[agw_in.opt()], outs=[agw_out.opt()]))
        mark("agw_done", QQ("ccagw"))
        for nm, i_, o_ in [("q", agq_in, agq_out), ("k", agk_in, agk_out), ("v", agv_in, agv_out),
                           ("o", ago_in, ago_out), ("p", agp_in, agp_out)]:
            wt(g, f"sc_{nm}1")
            QQ("ccag" + nm).inc(g.collective_compute("AllGather", ALU.bypass, replica_groups=[list(range(NCORE))],
                                        ins=[i_.opt()], outs=[o_.opt()]))
            mark(f"ag{nm}_done", QQ("ccag" + nm))


    kst.close()
    g1.close()
    tt_st = ExitStack()
    Tt = tt_st.enter_context(nc.sbuf_tensor("Tt", [128, 4096], BF16, side="right"))
    @block.sync
    def _(s):
        s = AutoDrain(s)
        wt(s, "scr_k")
        for j in range(32):
            src = AP(tensor=scr_h.tensor, offset=384 + 128 * j, ap=[[4608, 128], [1, 128]])
            QQ("tt").inc(s.dma_start(out=Tt[:, j * 128:(j + 1) * 128], in_=src), 16)
    mark("tt_loaded", QQ("tt"))
    if _KDBG:
        @block.vector
        def _(v):
            v = AutoDrain(v)
            wt(v, "tt_loaded")
            v.tensor_copy(out=dcat[:, 0:256], in_=kxy[:, 0:256])
            v.tensor_copy(out=dcat[:, 256:320], in_=arcat[:, 0:64])
            v.tensor_copy(out=dcat[:, 320:352], in_=Ksb[:, 0:32])
            v.tensor_copy(out=dcat[:, 352:480], in_=Spre[:, :])
            v.tensor_copy(out=dcat[:, 480:608], in_=Spim[:, :])
            v.tensor_copy(out=dcat[:, 608:616], in_=Vmat[:, 0:8])
            v.tensor_copy(out=dcat[0:64, 616:744], in_=arcT[0:64, :])
            v.tensor_copy(out=dcat[0:32, 744:872], in_=K2sb[0:32, :])
            cv.inc(v.tensor_copy(out=scr1[0:64, 0:384], in_=rows6[0:1, 0:384].partition_broadcast(64) if False else rows6[0:1, 0:384]) if False else v.tensor_copy(out=dcat[:, 872:873], in_=t2s_c[:, 0:1]))
        mark("dbgcp", cv)
        @block.sync
        def _(s):
            s = AutoDrain(s)
            wt(s, "dbgcp")
            QQ("dbg").inc(s.dma_start(out=DBG[:, 0:1024], in_=dcat[:, 0:1024]), 16)
            QQ("dbg").inc(s.dma_start(out=DBG[0:1, 1024:1408], in_=rows6[0:1, :]), 16)
            QQ("dbg").inc(s.dma_start(out=DBG[0:1, 1408:1536], in_=Brows[0:1, :]), 16)
            QQ("dbg").inc(s.dma_start(out=DBG[:, 1536:1600], in_=Vt[:, :]), 16)
            QQ("dbg").inc(s.dma_start(out=DBG[:, 1600:1602], in_=Bprow[:, :]), 16)
            QQ("dbg").inc(s.dma_start(out=DBG[0:64, 1602:1730], in_=Vcnat[:, :]), 16)
        mark("dbgout", QQ("dbg"))

    def stats_pass(xt_t, tag, first_wait=()):
        for i in range(64):
            c, h = i // 2, i % 2
            sl = slice(c * T + h * 512, c * T + h * 512 + 512)
            bs = slice((i % 2) * 512, (i % 2) * 512 + 512)
            @block.vector
            def _(v, i=i, sl=sl, bs=bs):
                v = AutoDrain(v)
                if i == 0:
                    for fwn in first_wait:
                        wt(v, fwn)
                if i >= 2:
                    wt(v, f"{tag}_stmm{i - 2}")
                v.tensor_copy(out=xbb[:, bs], in_=xt_t[:, sl])
                cv.inc(v.tensor_tensor(out=sqb2[:, bs], in0=xbb[:, bs], in1=xbb[:, bs], op=ALU.mult))
                mark(f"{tag}_sq{i}", cv)
            @block.tensor
            def _(t, i=i, c=c, h=h, bs=bs, tag=tag):
                wt(t, f"{tag}_sq{i}")
                if i == 0 and tag == "p1":
                    wt(t, "Sp")
                    wt(t, "Ksb_s")
                    wt(t, "bias_ready")
                    wt(t, "vprod")
                t.matmul(pf[2 + h][0:1, 0:512], onescb[:, 0:1], xbb[:, bs],
                         start=(c == 0), stop=(c == C - 1))
                ct.inc(t.matmul(pf[4 + h][0:1, 0:512], onescb[:, 0:1], sqb2[:, bs],
                                start=(c == 0), stop=(c == C - 1)))
                mark(f"{tag}_stmm{i}", ct)
        mark(f"{tag}_stats_done", ct)

    def rows_and_bcast(tag, h, sb=None, qb=None):
        if sb is None:
            sb, qb = 2 + h, 4 + h
        @block.vector
        def _(v, h=h, sb=sb, qb=qb):
            v = AutoDrain(v)
            wt(v, f"{tag}_stats_done")
            v.tensor_scalar(out=pf[0][0:1, 0:512], in0=pf[sb][0:1, 0:512],
                            scalar1=consts[0:1, 0:1], scalar2=None, op0=ALU.mult)
            v.tensor_scalar(out=pf[1][0:1, 0:512], in0=pf[qb][0:1, 0:512],
                            scalar1=consts[0:1, 0:1], scalar2=None, op0=ALU.mult)
            v.tensor_copy(out=t1f[0:1, 0:512], in_=pf[0][0:1, 0:512])
            v.tensor_tensor(out=t1f[0:1, 0:512], in0=pf[0][0:1, 0:512], in1=t1f[0:1, 0:512],
                            op=ALU.mult)
            cv.inc(v.tensor_tensor(out=t1f[0:1, 0:512], in0=pf[1][0:1, 0:512], in1=t1f[0:1, 0:512],
                                   op=ALU.subtract))
            mark(f"{tag}_var{h}", cv)
        @block.scalar
        def _(s, h=h):
            s = AutoDrain(s)
            wt(s, f"{tag}_var{h}")
            cs.inc(s.activation(out=t1f[0:1, 0:512], in_=t1f[0:1, 0:512], func=AF.Sqrt,
                                bias=consts[0:1, CN["eps"]:CN["eps"] + 1]))
            mark(f"{tag}_sd{h}", cs)
        @block.vector
        def _(v, h=h):
            v = AutoDrain(v)
            wt(v, f"{tag}_sd{h}")
            v.reciprocal(out=srow[0:1, :], in_=t1f[0:1, 0:512])
            v.tensor_tensor(out=brow[0:1, :], in0=pf[0][0:1, 0:512], in1=srow[0:1, :], op=ALU.mult)
            cv.inc(v.tensor_scalar(out=brow[0:1, :], in0=brow[0:1, :],
                                   scalar1=consts[0:1, CN["neg1"]:CN["neg1"] + 1], scalar2=None, op0=ALU.mult))
            mark(f"{tag}_srow{h}", cv)
        @block.tensor
        def _(t, h=h):
            wt(t, f"{tag}_srow{h}")
            t.matmul(pf[0][:, 0:512], ones1f[0:1, :], srow[0:1, :], start=True, stop=True)
            ct.inc(t.matmul(pf[1][:, 0:512], ones1f[0:1, :], brow[0:1, :], start=True, stop=True))
            mark(f"{tag}_bcmm{h}", ct)
        @block.vector
        def _(v, h=h):
            v = AutoDrain(v)
            wt(v, f"{tag}_bcmm{h}")
            v.tensor_copy(out=s_bc[h][:, :], in_=pf[0][:, 0:512])
            cv.inc(v.tensor_copy(out=b_bc[h][:, :], in_=pf[1][:, 0:512]))
            mark(f"{tag}_bc{h}", cv)

    def shadow_half(xt_t, shadow_t, tag, h, shadow_cols=512, col_off=0, extra=None, first_wait=()):
        for c in range(C):
            @block.vector
            def _(v, c=c, h=h):
                v = AutoDrain(v)
                if c == 0:
                    wt(v, f"{tag}_bc1")
                    for fwn in first_wait:
                        wt(v, fwn)
                sl = slice(c * T + h * 512, c * T + h * 512 + 512)
                osl = slice(c * shadow_cols + col_off, c * shadow_cols + col_off + 512)
                v.tensor_tensor(out=t1f[:, 0:512], in0=xt_t[:, sl], in1=s_bc[h][:, :], op=ALU.mult)
                last = v.tensor_tensor(out=shadow_t[:, osl], in0=t1f[:, 0:512], in1=b_bc[h][:, :], op=ALU.add)
                if extra is not None:
                    last = extra(v, c, h)
                cv.inc(last)
        mark(f"{tag}_shadow{h}", cv)

    sh_st = ExitStack()
    shadow = sh_st.enter_context(nc.sbuf_tensor("shadow", [128, C * 512], BF16))
    stats_pass(xt, "p1", first_wait=("x0_loaded", "consts_ready"))
    rows_and_bcast("p1", 0)
    rows_and_bcast("p1", 1)
    for g in range(2):
        if g == 0:
            fw = ("scr_k", "kxy_s", "K2_s", "arcat", "sc_p1", "sc_o1")
            shadow_half(xt, shadow, "p1", g, first_wait=fw)
        for n in range(C):
            i = g * C + n
            if g == 1:
                @block.vector
                def _(v, n=n):
                    v = AutoDrain(v)
                    if n == 0:
                        wt(v, "p1_mm31")
                        wt(v, "p1_bc1")
                    sl = slice(n * T + 512, n * T + 1024)
                    v.tensor_tensor(out=t1f[:, 0:512], in0=xt[:, sl], in1=s_bc[1][:, :], op=ALU.mult)
                    cv.inc(v.tensor_tensor(out=shadow[:, n * 512:(n + 1) * 512],
                                           in0=t1f[:, 0:512], in1=b_bc[1][:, :], op=ALU.add))
                    mark(f"p1_sh1c{n}", cv)
            @block.tensor
            def _(t, n=n, g=g, i=i):
                if n == 0:
                    wt(t, "tt_loaded")
                if g == 0:
                    if n == 0:
                        wt(t, f"p1_shadow{g}")
                else:
                    wt(t, f"p1_sh1c{n}")
                if i >= 2:
                    wt(t, f"p1_ev{i - 2}")
                bank = pf[i % 2]
                for k in range(n + 1):
                    d0 = n - k
                    t.matmul(bank[:, 0:512], Tt[:, d0 * 128:(d0 + 1) * 128],
                             shadow[:, k * 512:(k + 1) * 512], start=(k == 0), stop=False)
                ct.inc(t.matmul(bank[:, 0:512], DIb[:, :], shadow[:, n * 512:(n + 1) * 512],
                                start=False, stop=True))
                mark(f"p1_mm{i}", ct)
            @block.vector
            def _(v, n=n, g=g, i=i):
                v = AutoDrain(v)
                wt(v, f"p1_mm{i}")
                sl = slice(n * T + g * 512, n * T + g * 512 + 512)
                cv.inc(v.tensor_tensor(out=xt[:, sl], in0=xt[:, sl], in1=pf[i % 2][:, 0:512], op=ALU.add))
                mark(f"p1_ev{i}", cv)
    mark("p1_done", cv)
    tt_st.close()

    def ffn_phase(xt_t, shadow_t, tag, arb_in, arb_out, prev_done, post_abf=None, tail_hook=None):
        fst = ExitStack()
        fe = fst.enter_context
        A_bf = fe(nc.sbuf_tensor(f"Abf{tag}", [128, C * 32], BF16))
        wsl = [fe(nc.sbuf_tensor(f"wsl{tag}{i}", [128, 1024], BF16)) for i in range(2)]
        zT = fe(nc.sbuf_tensor(f"zT{tag}", [32, 1024], F32))
        zp_bf = fe(nc.sbuf_tensor(f"zp{tag}", [32, 1024], BF16))
        HslT = fe(nc.sbuf_tensor(f"Hsl{tag}", [8, 1024], F32))
        Han = fe(nc.sbuf_tensor(f"Han{tag}", [8, 512], F32))
        Dan = fe(nc.sbuf_tensor(f"Dan{tag}", [8, 512], F32))
        gt = [fe(nc.sbuf_tensor(f"gt{tag}{i}", [128, 8], F32)) for i in range(4)]
        gcol = fe(nc.sbuf_tensor(f"gcol{tag}", [128, 6], F32))
        wfT = fe(nc.sbuf_tensor(f"wfT{tag}", [8, 1024], BF16))
        Bsl = [fe(nc.sbuf_tensor(f"Bsl{tag}{i}", [32, 512], BF16)) for i in range(2)]

        if post_abf is not None:
            post_abf()
        @block.gpsimd
        def _(g):
            g = AutoDrain(g)
            wt(g, prev_done)
            for c in range(C):
                QQ(tag + "abf").inc(g.dma_start(out=A_bf[:, c * 32:(c + 1) * 32],
                                   in_=I["fA"][c * 128:(c + 1) * 128, :]), 16)
        mark(f"{tag}_abf", QQ(tag + "abf"))

        stats_pass(xt_t, tag, first_wait=(prev_done,))
        rows_and_bcast(tag, 0)
        rows_and_bcast(tag, 1)
        for g in range(2):
            for oc in range(C):
                i = g * C + oc
                for qj in range(4):
                    qidx = i * 4 + qj
                    if oc == 0:
                        @block.vector
                        def _(v, g=g, qj=qj):
                            v = AutoDrain(v)
                            if qj == 0:
                                wt(v, f"{tag}_bc{g}")
                                if g == 1:
                                    wt(v, f"{tag}_hmm{C - 1}")
                                    wt(v, f"{tag}_zmm0")
                            for c in range(qj * 8, qj * 8 + 8):
                                sl = slice(c * T + g * 512, c * T + g * 512 + 512)
                                v.tensor_tensor(out=t1f[:, 0:512], in0=xt_t[:, sl],
                                                in1=s_bc[g][:, :], op=ALU.mult)
                                last = v.tensor_tensor(out=shadow_t[:, c * 512:(c + 1) * 512],
                                                       in0=t1f[:, 0:512], in1=b_bc[g][:, :], op=ALU.add)
                            cv.inc(last)
                            mark(f"{tag}_sh{g}q{qj}", cv)
                    eng_sel = block.scalar if qidx % 2 == 1 else block.sync
                    @eng_sel
                    def _(s, oc=oc, qj=qj, qidx=qidx):
                        s = AutoDrain(s)
                        if qidx < 2:
                            wt(s, "agw_done")
                            wt(s, prev_done)
                        if qidx >= 2:
                            wt(s, f"{tag}_hq{qidx - 2}")
                        base = ((oc % 8) * 4 + (oc // 8)) * 128 * 4096 + qj * 1024
                        src = AP(tensor=agw_out.tensor, offset=base, ap=[[4096, 128], [1, 1024]])
                        QQ(f"{tag}wsl{qidx % 2}").inc(s.dma_start(out=wsl[qidx % 2][:, :], in_=src), 16)
                        mark(f"{tag}_wq{qidx}", QQ(f"{tag}wsl{qidx % 2}"))
                    @block.tensor
                    def _(t, oc=oc, g=g, i=i, qj=qj, qidx=qidx):
                        if qj == 0:
                            if i >= 2:
                                wt(t, f"{tag}_hev{i - 2}")
                        if oc == 0:
                            wt(t, f"{tag}_sh{g}q{qj}")
                        bank = pf[2 + (i % 2)]
                        wt(t, f"{tag}_wq{qidx}")
                        for dk in range(8):
                            dc = qj * 8 + dk
                            ins = t.matmul(bank[:, 0:512], wsl[qidx % 2][:, dk * 128:(dk + 1) * 128],
                                           shadow_t[:, dc * 512:(dc + 1) * 512],
                                           start=(dc == 0), stop=(dc == C - 1))
                        ct.inc(ins)
                        mark(f"{tag}_hq{qidx}", ct)
                        if qj == 3:
                            mark(f"{tag}_hmm{i}", ct)
                if oc == 0:
                    @block.tensor
                    def _(t, g=g):
                        wt(t, f"{tag}_sh{g}q3")
                        wt(t, f"{tag}_abf")
                        for c in range(C):
                            ins = t.matmul(pf[4][0:32, 0:512], A_bf[:, c * 32:(c + 1) * 32],
                                           shadow_t[:, c * 512:(c + 1) * 512], start=(c == 0), stop=(c == C - 1))
                        ct.inc(ins)
                        mark(f"{tag}_zmm{g}", ct)
                    @block.vector
                    def _(v, g=g):
                        v = AutoDrain(v)
                        wt(v, f"{tag}_zmm{g}")
                        cv.inc(v.tensor_copy(out=zT[0:32, g * 512:(g + 1) * 512], in_=pf[4][0:32, 0:512]))
                        mark(f"{tag}_zs{g}", cv)
                    @block.vector
                    def _(v, g=g, i=i):
                        v = AutoDrain(v)
                        wt(v, f"{tag}_hmm{i}")
                        cv.inc(v.tensor_scalar(out=HslT[0:8, g * 512:(g + 1) * 512],
                                               in0=pf[2 + (i % 2)][0:8, 0:512],
                                               scalar1=bcol[0:8, 0:1], scalar2=None, op0=ALU.add))
                        mark(f"{tag}_hsl{g}", cv)
                @block.vector
                def _(v, oc=oc, g=g, i=i):
                    v = AutoDrain(v)
                    wt(v, f"{tag}_hmm{i}")
                    if oc == 0:
                        wt(v, f"{tag}_hsl{g}")
                    sl = slice(oc * T + g * 512, oc * T + g * 512 + 512)
                    cv.inc(v.scalar_tensor_tensor(out=xt_t[:, sl], in0=xt_t[:, sl],
                                                  scalar=bcol[:, oc:oc + 1],
                                                  in1=pf[2 + (i % 2)][:, 0:512], op0=ALU.add, op1=ALU.add))
                    mark(f"{tag}_hev{i}", cv)
        mark(f"{tag}_H_done", cv)

        @block.tensor
        def _(t):
            wt(t, f"{tag}_zs1")
            zap = zT[0:32, :]
            zan = AP(tensor=zap.tensor, offset=zap.offset + 1, ap=[zap.ap[0], [2, 512]])
            ct.inc(t.matmul(pf[4][0:8, 0:512], liT[0:32, 0:8], zan, start=True, stop=True))
            mark(f"{tag}_dslmm", ct)
        @block.vector
        def _(v):
            v = AutoDrain(v)
            wt(v, f"{tag}_dslmm")
            wt(v, f"{tag}_hsl1")
            v.tensor_copy(out=Dan[0:8, :], in_=pf[4][0:8, 0:512])
            hap = HslT[0:8, :]
            hslan = AP(tensor=hap.tensor, offset=hap.offset + 1, ap=[hap.ap[0], [2, 512]])
            v.tensor_copy(out=Han[0:8, :], in_=hslan)
            v.reduce_max(out=gt[0][0:8, 0:1], in_=Han[0:8, :], axis=AX, apply_absolute_value=True)
            cv.inc(v.reduce_max(out=gt[0][0:8, 1:2], in_=Dan[0:8, :], axis=AX, apply_absolute_value=True))
            mark(f"{tag}_lmax", cv)
        @block.tensor
        def _(t):
            wt(t, f"{tag}_lmax")
            t.transpose(pf[5][0:1, 0:8], gt[0][0:8, 0:1], identf[0:8, 0:8])
            ct.inc(t.transpose(pf[5][0:1, 8:16], gt[0][0:8, 1:2], identf[0:8, 0:8]))
            mark(f"{tag}_lmaxT", ct)
        @block.vector
        def _(v):
            v = AutoDrain(v)
            wt(v, f"{tag}_lmaxT")
            v.reduce_max(out=arin_s[0:1, 0:1], in_=pf[5][0:1, 0:8], axis=AX)
            cv.inc(v.reduce_max(out=arin_s[0:1, 1:2], in_=pf[5][0:1, 8:16], axis=AX))
            mark(f"{tag}_lmax2", cv)
        @block.gpsimd
        def _(g):
            g = AutoDrain(g)
            wt(g, f"{tag}_lmax2")
            ars = QQ(tag + "ar")
            ars.inc(g.dma_start(out=arb_in, in_=arin_s[0:1, :]), 16)
            g.wait_ge(ars.h, ars.n)
            QQ(tag + "cc").inc(g.collective_compute("AllReduce", ALU.max, replica_groups=[list(range(NCORE))],
                                        ins=[arb_in.opt()], outs=[arb_out.opt()]))
            mark(f"{tag}_ar", QQ(tag + "cc"))
            wt(g, f"{tag}_ar")
            ars.inc(g.dma_start(out=armax_s[0:1, :], in_=arb_out), 16)
        mark(f"{tag}_armax", QQ(tag + "ar"))

        @block.vector
        def _(v):
            v = AutoDrain(v)
            wt(v, f"{tag}_armax")
            v.reciprocal(out=armax_s[0:1, :], in_=armax_s[0:1, :])
            cv.inc(v.tensor_scalar(out=armax_s[0:1, :], in0=armax_s[0:1, :],
                                   scalar1=consts[0:1, CN["half"]:CN["half"] + 1], scalar2=None, op0=ALU.mult))
            mark(f"{tag}_hsrow", cv)
        @block.tensor
        def _(t):
            wt(t, f"{tag}_hsrow")
            ct.inc(t.matmul(pf[4][:, 0:2], ones1f[0:1, :], armax_s[0:1, 0:2], start=True, stop=True))
            mark(f"{tag}_hsmm", ct)
        @block.vector
        def _(v):
            v = AutoDrain(v)
            wt(v, f"{tag}_hsmm")
            cv.inc(v.tensor_copy(out=hsds[:, :], in_=pf[4][:, 0:2]))
            mark(f"{tag}_hsds", cv)

        for a in range(4):
            @block.tensor
            def _(t, a=a):
                wt(t, f"{tag}_hsds")
                if a >= 1:
                    wt(t, f"{tag}_gw{a - 1}")
                t.transpose(pf[5][0:128, 0:8], Han[0:8, a * 128:(a + 1) * 128], identf[0:8, 0:8])
                ct.inc(t.transpose(pf[5][0:128, 8:16], Dan[0:8, a * 128:(a + 1) * 128], identf[0:8, 0:8]))
                mark(f"{tag}_gtr{a}", ct)
            @block.vector
            def _(v, a=a):
                v = AutoDrain(v)
                wt(v, f"{tag}_gtr{a}")
                v.tensor_scalar(out=gt[1][:, :], in0=pf[5][:, 8:16], scalar1=hsds[:, 1:2], scalar2=None, op0=ALU.mult)
                v.scalar_tensor_tensor(out=gt[0][:, :], in0=pf[5][:, 0:8], scalar=hsds[:, 0:1],
                                       in1=gt[1][:, :], op0=ALU.mult, op1=ALU.add)
                cv.inc(v.reduce_max(out=gcol[:, 0:1], in_=gt[0][:, :], axis=AX, negate=True))
                mark(f"{tag}_glog{a}", cv)
            @block.scalar
            def _(s, a=a):
                s = AutoDrain(s)
                wt(s, f"{tag}_glog{a}")
                cs.inc(s.activation(out=gt[1][:, :], in_=gt[0][:, :], func=AF.Exp,
                                    bias=gcol[:, 0:1], accum_out=gcol[:, 1:2]))
                mark(f"{tag}_gexp{a}", cs)
            @block.vector
            def _(v, a=a):
                v = AutoDrain(v)
                wt(v, f"{tag}_gexp{a}")
                v.reciprocal(out=gcol[:, 2:3], in_=gcol[:, 1:2])
                v.tensor_scalar(out=gt[1][:, :], in0=gt[1][:, :], scalar1=gcol[:, 2:3], scalar2=None, op0=ALU.mult)
                v.reduce_max(out=gcol[:, 0:1], in_=gt[1][:, :], axis=AX)
                v.tensor_scalar(out=gt[2][:, :], in0=gt[1][:, :], scalar1=gcol[:, 0:1], scalar2=None, op0=ALU.is_ge)
                v.reduce_sum(out=gcol[:, 1:2], in_=gt[2][:, :], axis=AX)
                v.tensor_scalar(out=gt[2][:, :], in0=gt[2][:, :], scalar1=consts[:, CN["big"]:CN["big"] + 1],
                                scalar2=None, op0=ALU.mult)
                v.tensor_tensor(out=gt[3][:, :], in0=gt[1][:, :], in1=gt[2][:, :], op=ALU.subtract)
                v.reduce_max(out=gcol[:, 2:3], in_=gt[3][:, :], axis=AX)
                v.tensor_scalar(out=gcol[:, 3:4], in0=gcol[:, 1:2], scalar1=consts[:, CN["c15"]:CN["c15"] + 1],
                                scalar2=None, op0=ALU.is_ge)
                v.tensor_tensor(out=gcol[:, 4:5], in0=gcol[:, 0:1], in1=gcol[:, 2:3], op=ALU.subtract)
                v.tensor_tensor(out=gcol[:, 5:6], in0=gcol[:, 3:4], in1=gcol[:, 4:5], op=ALU.mult)
                v.tensor_tensor(out=gcol[:, 0:1], in0=gcol[:, 2:3], in1=gcol[:, 5:6], op=ALU.add)
                cv.inc(v.tensor_scalar(out=gcol[:, 1:2], in0=gcol[:, 0:1],
                                       scalar1=consts[:, CN["neg2"]:CN["neg2"] + 1], scalar2=None, op0=ALU.mult))
                mark(f"{tag}_gth{a}", cv)
            @block.scalar
            def _(s, a=a):
                s = AutoDrain(s)
                wt(s, f"{tag}_gth{a}")
                cs.inc(s.activation(out=gt[2][:, :], in_=gt[1][:, :], func=AF.Sigmoid,
                                    scale=consts[:, CN["two"]:CN["two"] + 1], bias=gcol[:, 1:2]))
                mark(f"{tag}_gsig{a}", cs)
            @block.vector
            def _(v, a=a):
                v = AutoDrain(v)
                wt(v, f"{tag}_gsig{a}")
                v.tensor_tensor(out=gt[2][:, :], in0=gt[1][:, :], in1=gt[2][:, :], op=ALU.mult)
                v.reduce_sum(out=gcol[:, 2:3], in_=gt[2][:, :], axis=AX)
                v.tensor_scalar(out=gcol[:, 2:3], in0=gcol[:, 2:3],
                                scalar1=consts[:, CN["eps9"]:CN["eps9"] + 1], scalar2=None, op0=ALU.add)
                v.reciprocal(out=gcol[:, 3:4], in_=gcol[:, 2:3])
                cv.inc(v.tensor_scalar(out=gt[2][:, :], in0=gt[2][:, :], scalar1=gcol[:, 3:4],
                                       scalar2=None, op0=ALU.mult))
                mark(f"{tag}_gwts{a}", cv)
            @block.tensor
            def _(t, a=a):
                wt(t, f"{tag}_gwts{a}")
                ct.inc(t.transpose(pf[4][0:8, 0:128], gt[2][:, :], identf[:, :]))
                mark(f"{tag}_gwT{a}", ct)
            @block.vector
            def _(v, a=a):
                v = AutoDrain(v)
                wt(v, f"{tag}_gwT{a}")
                wap = wfT[0:8, :]
                even = AP(tensor=wap.tensor, offset=wap.offset + a * 256, ap=[wap.ap[0], [2, 128]])
                odd = AP(tensor=wap.tensor, offset=wap.offset + a * 256 + 1, ap=[wap.ap[0], [2, 128]])
                v.tensor_copy(out=even, in_=pf[4][0:8, 0:128])
                cv.inc(v.tensor_copy(out=odd, in_=pf[4][0:8, 0:128]))
                mark(f"{tag}_gw{a}", cv)
        mark(f"{tag}_wfT", cv)
        @block.tensor
        def _(t):
            wt(t, f"{tag}_wfT")
            t.matmul(pf[4][0:32, 0:512], limes_b[:, :], wfT[0:8, 0:512], start=True, stop=True)
            ct.inc(t.matmul(pf[5][0:32, 0:512], limes_b[:, :], wfT[0:8, 512:1024], start=True, stop=True))
            mark(f"{tag}_pmx", ct)
        @block.vector
        def _(v):
            v = AutoDrain(v)
            wt(v, f"{tag}_pmx")
            v.scalar_tensor_tensor(out=zp_bf[0:32, 0:512], in0=zT[0:32, 0:512],
                                   scalar=consts[0:32, CN["half"]:CN["half"] + 1],
                                   in1=pf[4][0:32, 0:512], op0=ALU.mult, op1=ALU.mult)
            cv.inc(v.scalar_tensor_tensor(out=zp_bf[0:32, 512:1024], in0=zT[0:32, 512:1024],
                                          scalar=consts[0:32, CN["half"]:CN["half"] + 1],
                                          in1=pf[5][0:32, 0:512], op0=ALU.mult, op1=ALU.mult))
            mark(f"{tag}_zp", cv)

        for ocg in range(8):
            @block.gpsimd
            def _(gg, ocg=ocg):
                gg = AutoDrain(gg)
                if ocg < 2:
                    wt(gg, prev_done)
                if ocg >= 2:
                    wt(gg, f"{tag}_dmm{(ocg - 2) * 8 + 7}")
                QQ(f"{tag}bsl{ocg % 2}").inc(gg.dma_start(out=Bsl[ocg % 2][:, :],
                                             in_=I["fB"][:, ocg * 512:(ocg + 1) * 512]), 16)
                mark(f"{tag}_dbsl{ocg}", QQ(f"{tag}bsl{ocg % 2}"))
            for g in range(2):
                for ocl in range(4):
                    oc = ocg * 4 + ocl
                    i = ocg * 8 + g * 4 + ocl
                    @block.tensor
                    def _(t, oc=oc, ocg=ocg, ocl=ocl, g=g, i=i):
                        if g == 0 and ocl == 0:
                            wt(t, f"{tag}_dbsl{ocg}")
                        if i == 0:
                            wt(t, f"{tag}_zp")
                            wt(t, f"{tag}_H_done")
                        if i >= 2:
                            wt(t, f"{tag}_dev{i - 2}")
                        ct.inc(t.matmul(pf[2 + (i % 2)][:, 0:512], Bsl[ocg % 2][:, ocl * 128:(ocl + 1) * 128],
                                        zp_bf[0:32, g * 512:(g + 1) * 512], start=True, stop=True))
                        mark(f"{tag}_dmm{i}", ct)
                    @block.vector
                    def _(v, oc=oc, g=g, i=i):
                        v = AutoDrain(v)
                        wt(v, f"{tag}_dmm{i}")
                        sl = slice(oc * T + g * 512, oc * T + g * 512 + 512)
                        cv.inc(v.tensor_tensor(out=xt_t[:, sl], in0=xt_t[:, sl],
                                               in1=pf[2 + (i % 2)][:, 0:512], op=ALU.add))
                        mark(f"{tag}_dev{i}", cv)
                    if tail_hook is not None and g == 1:
                        tail_hook(oc, i)
        mark(f"{tag}_done", cv)
        fst.close()

    def _emit_agc():
        @block.gpsimd
        def _(g):
            g = AutoDrain(g)
            wt(g, "cc_g1_11")
            QQ("ccagc").inc(g.collective_compute("AllGather", ALU.bypass, replica_groups=[list(range(NCORE))],
                                        ins=[agc_in.opt()], outs=[agc_out.opt()]))
            mark("agc_done", QQ("ccagc"))
    def p3_hook(c, devidx):
        def _stats_chunk(c):
            for h in range(2):
                i = c * 2 + h
                bs = slice(h * 512, h * 512 + 512)
                sl = slice(c * T + h * 512, c * T + h * 512 + 512)
                sb, qb = (0, 1) if h == 0 else (4, 5)
                @block.vector
                def _(v, i=i, sl=sl, bs=bs):
                    v = AutoDrain(v)
                    if i >= 2:
                        wt(v, f"p3_stmm{i - 2}")
                    v.tensor_copy(out=xbb[:, bs], in_=xt[:, sl])
                    cv.inc(v.tensor_tensor(out=sqb2[:, bs], in0=xbb[:, bs], in1=xbb[:, bs], op=ALU.mult))
                    mark(f"p3_sq{i}", cv)
                @block.tensor
                def _(t, i=i, c=c, bs=bs, sb=sb, qb=qb):
                    wt(t, f"p3_sq{i}")
                    t.matmul(pf[sb][0:1, 0:512], onescb[:, 0:1], xbb[:, bs],
                             start=(c == 0), stop=(c == C - 1))
                    ct.inc(t.matmul(pf[qb][0:1, 0:512], onescb[:, 0:1], sqb2[:, bs],
                                    start=(c == 0), stop=(c == C - 1)))
                    mark(f"p3_stmm{i}", ct)
        if c >= 1:
            _stats_chunk(c - 1)
        @block.sync
        def _(s, c=c, devidx=devidx):
            s = AutoDrain(s)
            wt(s, f"p2_dev{devidx}")
            QQ("x2sp").inc(s.dma_start(out=x2_h[c * 128:(c + 1) * 128, :], in_=xt[:, c * T:(c + 1) * T]), 16)
        if c == C - 1:
            _stats_chunk(C - 1)
            mark("p3_stats_done", ct)
            mark("x2_spilled", QQ("x2sp"))

    if _PHASES >= 2:
        ffn_phase(xt, shadow, "p2", ar1_in, ar1_out, "p1_done", post_abf=_emit_agc,
                  tail_hook=p3_hook if _PHASES >= 3 else None)
    sh_st.close()

    if _PHASES >= 3:
        xt_st.close()
        p3 = ExitStack()
        pe3 = p3.enter_context
        ln2 = pe3(nc.sbuf_tensor("ln2", [128, C * 1026], BF16))
        hstL = pe3(nc.sbuf_tensor("hstL", [128, 32], BF16))
        hstR = pe3(nc.sbuf_tensor("hstR", [128, 32], BF16))
        x2s = [pe3(nc.sbuf_tensor(f"x2s{i}", [128, 512], F32)) for i in range(4)]

        rows_and_bcast("p3", 0, sb=0, qb=1)
        rows_and_bcast("p3", 1, sb=4, qb=5)
        for h in range(2):
            for c in range(C):
                i2 = h * C + c
                eng_sel2 = block.scalar if i2 % 4 in (1, 3) else block.sync
                @eng_sel2
                def _(s, c=c, h=h, i2=i2):
                    s = AutoDrain(s)
                    if i2 < 4:
                        wt(s, "x2_spilled")
                        wt(s, "p3_stats_done")
                    if i2 >= 4:
                        wt(s, f"p3_sh{i2 - 4}")
                    QQ(f"x2s{i2 % 4}").inc(s.dma_start(out=x2s[i2 % 4][:, :],
                                       in_=x2_h[c * 128:(c + 1) * 128, h * 512:(h + 1) * 512]), 16)
                    mark(f"p3_xs{i2}", QQ(f"x2s{i2 % 4}"))
                @block.vector
                def _(v, c=c, h=h, i2=i2):
                    v = AutoDrain(v)
                    wt(v, f"p3_xs{i2}")
                    if i2 == 0:
                        wt(v, "p3_bc1")
                    col_off = 1 if h == 0 else 513
                    osl = slice(c * 1026 + col_off, c * 1026 + col_off + 512)
                    v.tensor_tensor(out=t1f[:, 0:512], in0=x2s[i2 % 4][:, :], in1=s_bc[h][:, :], op=ALU.mult)
                    v.tensor_tensor(out=ln2[:, osl], in0=t1f[:, 0:512], in1=b_bc[h][:, :], op=ALU.add)
                    if h == 0:
                        last = v.tensor_copy(out=hstL[:, c:c + 1], in_=ln2[:, c * 1026 + 1:c * 1026 + 2])
                    else:
                        last = v.tensor_copy(out=hstR[:, c:c + 1], in_=ln2[:, c * 1026 + 1024:c * 1026 + 1025])
                    cv.inc(last)
                    mark(f"p3_sh{i2}", cv)
            mark(f"p3_shadow{h}", cv)

        @block.gpsimd
        def _(g):
            g = AutoDrain(g)
            wt(g, "p3_shadow1")
            hs_ = QQ("halo")
            hs_.inc(g.dma_start(out=AP(tensor=agh_in.tensor, offset=0, ap=[[32, 128], [1, 32]]),
                               in_=hstL[:, :]), 16)
            hs_.inc(g.dma_start(out=AP(tensor=agh_in.tensor, offset=4096, ap=[[32, 128], [1, 32]]),
                               in_=hstR[:, :]), 16)
            g.wait_ge(hs_.h, hs_.n)
            QQ("ccagh").inc(g.collective_compute("AllGather", ALU.bypass, replica_groups=[list(range(NCORE))],
                                        ins=[agh_in.opt()], outs=[agh_out.opt()]))
        mark("agh_done", QQ("ccagh"))

        qk_sc = ExitStack()
        qkvT = qk_sc.enter_context(nc.sbuf_tensor("qkvT", [128, 24 * 1024], BF16, side="right"))
        c3 = ExitStack()
        ce3 = c3.enter_context
        cwb = [ce3(nc.sbuf_tensor(f"cwb{i}", [128, 512], BF16)) for i in range(3)]

        hi = ExitStack()
        he_ = hi.enter_context
        GlGf = he_(nc.sbuf_tensor("GlGf", [8, 8192], BF16, side="right"))
        hfl = he_(nc.sbuf_tensor("hfl", [1, 512], BF16, side="right"))
        haloL = he_(nc.sbuf_tensor("haloL", [128, 32], BF16, side="right"))
        haloR = he_(nc.sbuf_tensor("haloR", [128, 32], BF16, side="right"))
        @block.gpsimd
        def _(g):
            g = AutoDrain(g)
            wt(g, "agh_done")
            wt(g, "x2_spilled")
            QQ("halo").inc(g.dma_start(out=GlGf[0:8, 0:4096],
                               in_=AP(tensor=agh_out.tensor, offset=4096, ap=[[8192, 8], [1, 4096]])), 16)
            QQ("halo").inc(g.dma_start(out=GlGf[0:8, 4096:8192],
                               in_=AP(tensor=agh_out.tensor, offset=0, ap=[[8192, 8], [1, 4096]])), 16)
        mark("gl_loaded", QQ("halo"))
        for j in range(16):
            side, blk_ = j // 8, j % 8
            @block.tensor
            def _(t, j=j, side=side, blk_=blk_):
                wt(t, "gl_loaded")
                if j >= 2:
                    wt(t, f"p3_hf{j - 2}")
                lhs = onelb if side == 0 else onerb
                ct.inc(t.matmul(pf[4 + (j % 2)][0:1, 0:512], lhs[0:8, 0:1],
                                GlGf[0:8, side * 4096 + blk_ * 512: side * 4096 + (blk_ + 1) * 512],
                                start=True, stop=True))
                mark(f"p3_hfmm{j}", ct)
            @block.vector
            def _(v, j=j):
                v = AutoDrain(v)
                wt(v, f"p3_hfmm{j}")
                if j >= 1:
                    wt(v, f"p3_hf{j - 1}")
                cv.inc(v.tensor_copy(out=hfl[0:1, :], in_=pf[4 + (j % 2)][0:1, 0:512]))
                mark(f"p3_hfe{j}", cv)
            @block.gpsimd
            def _(g, j=j, side=side, blk_=blk_):
                g = AutoDrain(g)
                wt(g, f"p3_hfe{j}")
                hq_ = QQ("halo")
                hq_.inc(g.dma_start(out=hfl_h, in_=hfl[0:1, :]), 16)
                g.wait_ge(hq_.h, hq_.n)
                dstT = haloL if side == 0 else haloR
                hq_.inc(g.dma_start(out=dstT[blk_ * 16:(blk_ + 1) * 16, :],
                                   in_=AP(tensor=hfl_h.tensor, offset=0, ap=[[32, 16], [1, 32]])), 16)
                mark(f"p3_hf{j}", QQ("halo"))
        @block.vector
        def _(v):
            v = AutoDrain(v)
            wt(v, "p3_hf15")
            lap = ln2[:, :]
            v.tensor_copy(out=AP(tensor=lap.tensor, offset=lap.offset, ap=[lap.ap[0], [1026, 32]]),
                          in_=haloL[:, :])
            cv.inc(v.tensor_copy(out=AP(tensor=lap.tensor, offset=lap.offset + 1025, ap=[lap.ap[0], [1026, 32]]),
                                 in_=haloR[:, :]))
        mark("halo_in", cv)
        hi.close()

        def conv_bank(idx):
            if idx < 6:
                return pf[idx][:, 0:512]
            return pbx[idx - 6][:, :].bitcast(F32)[:, 0:512]
        for ccg in range(6):
            for dc in range(C):
                for w_ in range(3):
                    j = (ccg * C + dc) * 3 + w_
                    @block.sync
                    def _(s, j=j, ccg=ccg, dc=dc, w_=w_):
                        s = AutoDrain(s)
                        if j == 0:
                            wt(s, "agc_done")
                            wt(s, "x2_spilled")
                            wt(s, "p3_stats_done")
                        if j >= 3:
                            wt(s, f"p3_cmm{j - 3}")
                        base = ((dc // 4) * 12 + (dc % 4) * 3 + w_) * 128 * 3072 + ccg * 512
                        src = AP(tensor=agc_out.tensor, offset=base, ap=[[3072, 128], [1, 512]])
                        QQ(f"cw{j % 3}").inc(s.dma_start(out=cwb[j % 3][:, :], in_=src), 16)
                        mark(f"p3_cw{j}", QQ(f"cw{j % 3}"))
                    @block.tensor
                    def _(t, j=j, ccg=ccg, dc=dc, w_=w_):
                        wt(t, f"p3_cw{j}")
                        first = dc == 0 and w_ == 0
                        last = dc == C - 1 and w_ == 2
                        if first:
                            wt(t, "halo_in")
                            if ccg >= 1:
                                wt(t, f"p3_cev{ccg - 1}")
                        for ccl in range(4):
                            for tg in range(2):
                                ins = t.matmul(conv_bank(ccl * 2 + tg),
                                               cwb[j % 3][:, ccl * 128:(ccl + 1) * 128],
                                               ln2[:, dc * 1026 + tg * 512 + w_: dc * 1026 + tg * 512 + w_ + 512],
                                               start=first, stop=last)
                        ct.inc(ins)
                        mark(f"p3_cmm{j}", ct)
            @block.vector
            def _(v, ccg=ccg):
                v = AutoDrain(v)
                wt(v, f"p3_cmm{(ccg * C + C - 1) * 3 + 2}")
                for ccl in range(4):
                    for tg in range(2):
                        cc_ = ccg * 4 + ccl
                        ins = v.tensor_scalar(out=qkvT[:, cc_ * 1024 + tg * 512: cc_ * 1024 + (tg + 1) * 512],
                                              in0=conv_bank(ccl * 2 + tg), scalar1=cbcol[:, cc_:cc_ + 1],
                                              scalar2=None, op0=ALU.add)
                cv.inc(ins)
                mark(f"p3_cev{ccg}", cv)
        mark("conv_done", cv)
        c3.close()
        p3.close()

        q3 = ExitStack()
        qe3 = q3.enter_context
        qhT = qe3(nc.sbuf_tensor("qhT", [128, 8192], BF16, side="right"))
        khT = qe3(nc.sbuf_tensor("khT", [128, 8192], BF16, side="right"))
        vhT = qe3(nc.sbuf_tensor("vhT", [128, 8192], BF16, side="right"))
        wqb = [qe3(nc.sbuf_tensor(f"wqb{i}", [128, 1024], BF16, side="right")) for i in range(2)]
        avs = qe3(nc.sbuf_tensor("avs", [128, 8192], BF16, side="right"))
        for qk in range(2):
            src_ag = agq_out if qk == 0 else agk_out
            dst_t = qhT if qk == 0 else khT
            bias_c = bqcol if qk == 0 else bkcol
            agmark = "agq_done" if qk == 0 else "agk_done"
            for hec in range(8):
                hidx = qk * 8 + hec
                @block.sync
                def _(s, qk=qk, hec=hec, hidx=hidx, src_ag=src_ag, agmark=agmark):
                    s = AutoDrain(s)
                    if hidx % 8 == 0:
                        wt(s, agmark)
                        wt(s, "conv_done")
                    if hidx >= 2:
                        wt(s, f"p3_qmmlast{hidx - 2}")
                    srcap = AP(tensor=src_ag.tensor, offset=hec * 128,
                               ap=[[1024, 128], [131072, 8], [1, 128]])
                    QQ(f"qw{hidx % 2}").inc(s.dma_start(out=wqb[hidx % 2][:, :], in_=srcap), 16)
                    mark(f"p3_qw{hidx}", QQ(f"qw{hidx % 2}"))
                for g in range(2):
                    i = hidx * 2 + g
                    @block.tensor
                    def _(t, qk=qk, hec=hec, hidx=hidx, g=g, i=i):
                        if g == 0:
                            wt(t, f"p3_qw{hidx}")
                        if i >= 2:
                            wt(t, f"p3_qev{i - 2}")
                        for ccj in range(8):
                            ins = t.matmul(pf[2 + (i % 2)][:, 0:512], wqb[hidx % 2][:, ccj * 128:(ccj + 1) * 128],
                                           qkvT[:, (qk * 8 + ccj) * 1024 + g * 512:(qk * 8 + ccj) * 1024 + (g + 1) * 512],
                                           start=(ccj == 0), stop=(ccj == 7))
                        ct.inc(ins)
                        mark(f"p3_qmm{i}", ct)
                        if g == 1:
                            mark(f"p3_qmmlast{hidx}", ct)
                    @block.scalar
                    def _(s, qk=qk, i=i, hec=hec, g=g, dst_t=dst_t, bias_c=bias_c):
                        s = AutoDrain(s)
                        wt(s, f"p3_qmm{i}")
                        if qk == 0:
                            ins = s.activation(out=dst_t[:, hec * 1024 + g * 512:hec * 1024 + (g + 1) * 512],
                                               in_=pf[2 + (i % 2)][:, 0:512], func=AF.Identity,
                                               scale=consts[:, CN["eighth"]:CN["eighth"] + 1],
                                               bias=bias_c[:, hec:hec + 1])
                        else:
                            ins = s.activation(out=dst_t[:, hec * 1024 + g * 512:hec * 1024 + (g + 1) * 512],
                                               in_=pf[2 + (i % 2)][:, 0:512], func=AF.Identity,
                                               bias=bias_c[:, hec:hec + 1])
                        cs.inc(ins)
                        mark(f"p3_qev{i}", cs)
        mark("qk_done", cs)

        @block.sync
        def _(s):
            s = AutoDrain(s)
            wt(s, "agv_done")
            wt(s, "conv_done")
            for ccj in range(8):
                QQ("avs").inc(s.dma_start(out=avs[:, ccj * 1024:(ccj + 1) * 1024],
                                   in_=AP(tensor=agv_out.tensor, offset=ccj * 128 * 1024,
                                          ap=[[1024, 128], [1, 1024]])), 16)
        mark("avs_loaded", QQ("avs"))
        for tt in range(8):
            for hf in range(2):
                i = tt * 2 + hf
                @block.tensor
                def _(t, tt=tt, hf=hf, i=i):
                    if i == 0:
                        wt(t, "avs_loaded")
                        wt(t, "conv_done")
                    if i >= 2:
                        wt(t, f"p3_vev{i - 2}")
                    bank = pf[4 + (i % 2)]
                    for ccj in range(8):
                        t.matmul(bank[:, 0:512],
                                 qkvT[:, (16 + ccj) * 1024 + tt * 128:(16 + ccj) * 1024 + (tt + 1) * 128],
                                 avs[:, ccj * 1024 + hf * 512:ccj * 1024 + (hf + 1) * 512],
                                 start=(ccj == 0), stop=False)
                    ct.inc(t.matmul(bank[:, 0:512], ones1b[0:1, :], bvrow[0:1, hf * 512:(hf + 1) * 512],
                                    start=False, stop=True))
                    mark(f"p3_vmm{i}", ct)
                @block.vector
                def _(v, tt=tt, hf=hf, i=i):
                    v = AutoDrain(v)
                    wt(v, f"p3_vmm{i}")
                    cv.inc(v.tensor_copy(out=vhT[:, tt * 1024 + hf * 512:tt * 1024 + (hf + 1) * 512],
                                         in_=pf[4 + (i % 2)][:, 0:512]))
                    mark(f"p3_vev{i}", cv)
        mark("vh_done", cv)

        a3 = ExitStack()
        ae3 = a3.enter_context
        oT = ae3(nc.sbuf_tensor("oT", [128, 8192], BF16))
        pnb = ae3(nc.sbuf_tensor("pnb", [128, 2048], BF16))
        pne = ae3(nc.sbuf_tensor("pne", [128, 2048], BF16))
        pTb = ae3(nc.sbuf_tensor("pTb", [128, 512], BF16))
        prow = ae3(nc.sbuf_tensor("prow", [128, 8], F32))
        def att_post(sg):
            for b in range(16):
                i = sg * 16 + b
                tt, h = i // 16, i % 16
                hec, hp = h // 2, h % 2
                sl2 = i % 4
                g4 = i // 4
                @block.vector
                def _(v, i=i, b=b, sl2=sl2, g4=g4):
                    v = AutoDrain(v)
                    wt(v, f"p3_exp{g4}")
                    v.reduce_sum(out=prow[:, sl2 * 2 + 1:sl2 * 2 + 2],
                                 in_=pnb[:, b * 128:(b + 1) * 128], axis=AX)
                    v.reciprocal(out=prow[:, sl2 * 2 + 1:sl2 * 2 + 2], in_=prow[:, sl2 * 2 + 1:sl2 * 2 + 2])
                    cv.inc(v.tensor_scalar(out=pnb[:, b * 128:(b + 1) * 128],
                                           in0=pnb[:, b * 128:(b + 1) * 128],
                                           scalar1=prow[:, sl2 * 2 + 1:sl2 * 2 + 2],
                                           scalar2=None, op0=ALU.mult))
                    mark(f"p3_nrm{i}", cv)
                @block.tensor
                def _(t, i=i, b=b, sl2=sl2):
                    wt(t, f"p3_nrm{i}")
                    if i >= 2:
                        wt(t, f"p3_pT{i - 2}")
                    ct.inc(t.transpose(pbx[i % 2][:, 0:128], pnb[:, b * 128:(b + 1) * 128], identb[:, :]))
                    mark(f"p3_ptr{i}", ct)
                @block.vector
                def _(v, i=i, sl2=sl2):
                    v = AutoDrain(v)
                    wt(v, f"p3_ptr{i}")
                    if i >= 4:
                        wt(v, f"p3_omm{i - 4}")
                    cv.inc(v.tensor_copy(out=pTb[:, sl2 * 128:(sl2 + 1) * 128], in_=pbx[i % 2][:, 0:128]))
                    mark(f"p3_pT{i}", cv)
                @block.tensor
                def _(t, tt=tt, h=h, i=i, hp=hp, sl2=sl2):
                    wt(t, f"p3_pT{i}")
                    if hp == 0 and i >= 3:
                        wt(t, f"p3_oev{i - 3}")
                    ct.inc(t.matmul(pf[4 + ((i // 2) % 2)][hp * 64:hp * 64 + 64, 0:128],
                                    vhT[:, tt * 1024 + h * 64:tt * 1024 + h * 64 + 64],
                                    pTb[:, sl2 * 128:(sl2 + 1) * 128], start=True, stop=True))
                    mark(f"p3_omm{i}", ct)
                if hp == 1:
                    @block.vector
                    def _(v, tt=tt, hec=hec, i=i):
                        v = AutoDrain(v)
                        wt(v, f"p3_omm{i}")
                        cv.inc(v.tensor_copy(out=oT[:, hec * 1024 + tt * 128:hec * 1024 + (tt + 1) * 128],
                                             in_=pf[4 + ((i // 2) % 2)][:, 0:128]))
                        mark(f"p3_oev{i}", cv)

        for sg in range(8):
            if sg >= 1:
                att_post(sg - 1)
            for p16 in range(16):
                i = sg * 16 + p16
                tt, h = i // 16, i % 16
                hec, hp = h // 2, h % 2
                sl2 = i % 4
                @block.tensor
                def _(t, tt=tt, h=h, i=i, hec=hec, hp=hp):
                    if i == 0:
                        wt(t, "qk_done")
                        wt(t, "vh_done")
                    if i >= 4:
                        wt(t, f"p3_sub{i - 4}")
                    bank = pf[i % 4]
                    t.matmul(bank[:, 0:128],
                             qhT[hp * 64:hp * 64 + 64, hec * 1024 + tt * 128:hec * 1024 + (tt + 1) * 128],
                             khT[hp * 64:hp * 64 + 64, hec * 1024 + tt * 128:hec * 1024 + (tt + 1) * 128],
                             start=True, stop=False)
                    ct.inc(t.matmul(bank[:, 0:128], identb[:, :], maskb[:, :],
                                    start=False, stop=True))
                    mark(f"p3_smm{i}", ct)
                @block.vector
                def _(v, i=i, p16=p16, sl2=sl2, sg=sg):
                    v = AutoDrain(v)
                    wt(v, f"p3_smm{i}")
                    if p16 == 0 and sg >= 1:
                        wt(v, f"p3_exp{sg * 4 - 1}")
                    bank = pf[i % 4]
                    v.reduce_max(out=prow[:, sl2 * 2:sl2 * 2 + 1], in_=bank[:, 0:128],
                                 axis=AX, negate=True)
                    cv.inc(v.tensor_scalar(out=pne[:, p16 * 128:(p16 + 1) * 128],
                                           in0=bank[:, 0:128],
                                           scalar1=prow[:, sl2 * 2:sl2 * 2 + 1],
                                           scalar2=None, op0=ALU.add))
                    mark(f"p3_sub{i}", cv)
            @block.scalar
            def _(s, sg=sg):
                s = AutoDrain(s)
                if sg >= 1:
                    wt(s, f"p3_ptr{sg * 16 - 1}")
                for gg in range(4):
                    g4 = sg * 4 + gg
                    wt(s, f"p3_sub{g4 * 4 + 3}")
                    cs.inc(s.activation(out=pnb[:, gg * 512:(gg + 1) * 512],
                                        in_=pne[:, gg * 512:(gg + 1) * 512], func=AF.Exp))
                    mark(f"p3_exp{g4}", cs)
        att_post(7)
        mark("att_done", cv)
        q3.close()
        qk_sc.close()

        o3 = ExitStack()
        oe3 = o3.enter_context
        o2T = oe3(nc.sbuf_tensor("o2T", [128, 8192], BF16, side="right"))
        wob = [oe3(nc.sbuf_tensor(f"wob{i}", [128, 1024], BF16, side="right")) for i in range(2)]
        for wc in range(8):
            @block.sync
            def _(s, wc=wc):
                s = AutoDrain(s)
                if wc == 0:
                    wt(s, "ago_done")
                    wt(s, "att_done")
                if wc >= 2:
                    wt(s, f"p3_o2mmlast{wc - 2}")
                srcap = AP(tensor=ago_out.tensor, offset=wc * 128,
                           ap=[[1024, 128], [131072, 8], [1, 128]])
                QQ(f"ow{wc % 2}").inc(s.dma_start(out=wob[wc % 2][:, :], in_=srcap), 16)
                mark(f"p3_o2w{wc}", QQ(f"ow{wc % 2}"))
            for g in range(2):
                i = wc * 2 + g
                @block.tensor
                def _(t, wc=wc, g=g, i=i):
                    if g == 0:
                        wt(t, f"p3_o2w{wc}")
                    if i >= 2:
                        wt(t, f"p3_o2ev{i - 2}")
                    for hec in range(8):
                        ins = t.matmul(pf[2 + (i % 2)][:, 0:512], wob[wc % 2][:, hec * 128:(hec + 1) * 128],
                                       oT[:, hec * 1024 + g * 512:hec * 1024 + (g + 1) * 512],
                                       start=(hec == 0), stop=(hec == 7))
                    ct.inc(ins)
                    mark(f"p3_o2mm{i}", ct)
                    if g == 1:
                        mark(f"p3_o2mmlast{wc}", ct)
                @block.vector
                def _(v, i=i, wc=wc, g=g):
                    v = AutoDrain(v)
                    wt(v, f"p3_o2mm{i}")
                    cv.inc(v.tensor_scalar(out=o2T[:, wc * 1024 + g * 512:wc * 1024 + (g + 1) * 512],
                                           in0=pf[2 + (i % 2)][:, 0:512],
                                           scalar1=bocol[:, wc:wc + 1], scalar2=None, op0=ALU.add))
                    mark(f"p3_o2ev{i}", cv)
        mark("o2_done", cv)

        a3.close()
        xt3_st = ExitStack()
        xt3 = xt3_st.enter_context(nc.sbuf_tensor("xt3", [128, C * T], F32))
        pr3 = ExitStack()
        pre3 = pr3.enter_context
        pwb = [pre3(nc.sbuf_tensor(f"pwb{i}", [128, 1024], BF16)) for i in range(2)]
        x2b = [pre3(nc.sbuf_tensor(f"x2b{i}", [128, 512], F32)) for i in range(2)]
        for dmc in range(C):
            @block.sync
            def _(s, dmc=dmc):
                s = AutoDrain(s)
                if dmc == 0:
                    wt(s, "agp_done")
                    wt(s, "o2_done")
                if dmc >= 2:
                    wt(s, f"p3_pmmlast{dmc - 2}")
                srcap = AP(tensor=agp_out.tensor, offset=dmc * 128,
                           ap=[[4096, 128], [524288, 8], [1, 128]])
                QQ(f"pw{dmc % 2}").inc(s.dma_start(out=pwb[dmc % 2][:, :], in_=srcap), 16)
                mark(f"p3_pw{dmc}", QQ(f"pw{dmc % 2}"))
            for g in range(2):
                i = dmc * 2 + g
                @block.sync
                def _(s, dmc=dmc, g=g, i=i):
                    s = AutoDrain(s)
                    if i < 2:
                        wt(s, "o2_done")
                    if i >= 2:
                        wt(s, f"p3_pev{i - 2}")
                    QQ(f"x2b{i % 2}").inc(s.dma_start(out=x2b[i % 2][:, :],
                                       in_=x2_h[dmc * 128:(dmc + 1) * 128, g * 512:(g + 1) * 512]), 16)
                    mark(f"p3_x2b{i}", QQ(f"x2b{i % 2}"))
                @block.tensor
                def _(t, dmc=dmc, g=g, i=i):
                    if g == 0:
                        wt(t, f"p3_pw{dmc}")
                    if i >= 2:
                        wt(t, f"p3_pev{i - 2}")
                    for wc in range(8):
                        ins = t.matmul(pf[2 + (i % 2)][:, 0:512], pwb[dmc % 2][:, wc * 128:(wc + 1) * 128],
                                       o2T[:, wc * 1024 + g * 512:wc * 1024 + (g + 1) * 512],
                                       start=(wc == 0), stop=(wc == 7))
                    ct.inc(ins)
                    mark(f"p3_pmm{i}", ct)
                    if g == 1:
                        mark(f"p3_pmmlast{dmc}", ct)
                @block.vector
                def _(v, i=i, dmc=dmc, g=g):
                    v = AutoDrain(v)
                    wt(v, f"p3_pmm{i}")
                    wt(v, f"p3_x2b{i}")
                    cv.inc(v.scalar_tensor_tensor(out=xt3[:, dmc * T + g * 512:dmc * T + (g + 1) * 512],
                                                  in0=x2b[i % 2][:, :], scalar=pbcol[:, dmc:dmc + 1],
                                                  in1=pf[2 + (i % 2)][:, 0:512], op0=ALU.add, op1=ALU.add))
                    mark(f"p3_pev{i}", cv)
        mark("p3_done", cv)
        pr3.close()
        o3.close()
        xt_final = xt3
    else:
        xt_final = xt

    if _PHASES >= 4:
        def out_hook(c, devidx):
            eng_sel3 = block.scalar if c % 2 == 1 else block.sync
            @eng_sel3
            def _(s, c=c, devidx=devidx):
                s = AutoDrain(s)
                wt(s, f"p4_dev{devidx}")
                QQ(f"out{c % 2}").inc(s.dma_start(out=OUT[c * 128:(c + 1) * 128, :],
                                      in_=xt_final[:, c * T:(c + 1) * T]), 16)
        sh2_st = ExitStack()
        shadow2 = sh2_st.enter_context(nc.sbuf_tensor("shadow2", [128, C * 512], BF16))
        ffn_phase(xt_final, shadow2, "p4", ar2_in, ar2_out, "p3_done", tail_hook=out_hook)
        final_mark = "p4_done"
        sh2_st.close()
    else:
        final_mark = {1: "p1_done", 2: "p2_done", 3: "p3_done"}[_PHASES]

    @block.sync
    def _(s):
        s = AutoDrain(s)
        wt(s, final_mark)
        s.wait_ge(cv.h, cv.n)
        if _PHASES < 4:
            for c in range(C):
                QQ("out0").inc(s.dma_start(out=OUT[c * 128:(c + 1) * 128, :], in_=xt_final[:, c * T:(c + 1) * T]), 16)
        for nm_, sm_ in sorted(_sems.items()):
            s.wait_ge(sm_.h, sm_.n)

    if _PHASES >= 3:
        xt3_st.close()
    else:
        xt_st.close()
    st.close()
    return nc


_tables_cache = None


def kernel(**ins):
    global _tables_cache
    if _tables_cache is None:
        _tables_cache = _host_tables()
    tables = _tables_cache
    x = np.asarray(ins["L0_attn_a_u"])
    fW = np.ascontiguousarray(ins["L0_ffn_W"], dtype=np.float32)
    cw = np.ascontiguousarray(ins["L0_attn_b_conv_w"], dtype=np.float32)
    wq = np.ascontiguousarray(ins["L0_attn_b_wq"], dtype=np.float32).reshape(1024, 1024)
    wk = np.ascontiguousarray(ins["L0_attn_b_wk"], dtype=np.float32).reshape(1024, 1024)
    wv = np.ascontiguousarray(ins["L0_attn_b_wv"], dtype=np.float32).reshape(1024, 1024)
    wo = np.ascontiguousarray(ins["L0_attn_b_wo"], dtype=np.float32).reshape(1024, 1024)
    pw = np.ascontiguousarray(ins["L0_attn_b_proj_w"], dtype=np.float32)
    Bp = np.asarray(ins["L0_attn_a_Bp"], dtype=np.float32)
    bpil = np.zeros((128, 2), np.float32)
    bpil[0::2, 0] = Bp[:, 0]
    bpil[1::2, 1] = Bp[:, 0]

    nc = build_nc()
    in_maps = []
    for cid in range(NCORE):
        m = {}
        m["x0T"] = np.ascontiguousarray(x[0, cid * T:(cid + 1) * T, :].T)
        m["aLam"] = np.asarray(ins["L0_attn_a_Lambda_ri"], np.float32)
        m["aP"] = np.asarray(ins["L0_attn_a_p_ri"], np.float32)
        m["aQ"] = np.asarray(ins["L0_attn_a_q_ri"], np.float32)
        m["aVc"] = np.asarray(ins["L0_attn_a_Vc_ri"], np.float32)
        m["aCt"] = np.asarray(ins["L0_attn_a_Ct_ri"], np.float32)
        m["aBpil"] = bpil
        m["aD"] = np.asarray(ins["L0_attn_a_D"], np.float32).reshape(1, 1)
        m["aLS"] = np.asarray(ins["L0_attn_a_log_step"], np.float32).reshape(1, 1)
        m["cwsh"] = np.ascontiguousarray(cw[:, cid * 512:(cid + 1) * 512, :])
        m["cb"] = np.asarray(ins["L0_attn_b_conv_b"], np.float32)
        m["fWsh"] = np.ascontiguousarray(
            np.concatenate([fW[oc * 128:(oc + 1) * 128, :] for oc in (cid, cid + 8, cid + 16, cid + 24)], axis=0))
        m["wqsh"] = np.ascontiguousarray(wq[cid * 128:(cid + 1) * 128, :])
        m["wksh"] = np.ascontiguousarray(wk[cid * 128:(cid + 1) * 128, :])
        m["wvsh"] = np.ascontiguousarray(wv[cid * 128:(cid + 1) * 128, :])
        m["wosh"] = np.ascontiguousarray(wo[cid * 128:(cid + 1) * 128, :])
        m["pwsh"] = np.ascontiguousarray(pw[cid * 128:(cid + 1) * 128, :])
        m["bq"] = np.asarray(ins["L0_attn_b_bq"], np.float32)
        m["bk"] = np.asarray(ins["L0_attn_b_bk"], np.float32)
        m["bv"] = np.asarray(ins["L0_attn_b_bv"], np.float32)
        m["bo"] = np.asarray(ins["L0_attn_b_bo"], np.float32)
        m["pb"] = np.asarray(ins["L0_attn_b_proj_b"], np.float32)
        m["fb"] = np.asarray(ins["L0_ffn_b"], np.float32)
        m["fA"] = np.asarray(ins["L0_ffn_A"], np.float32)
        m["fB"] = np.asarray(ins["L0_ffn_Bmat"], np.float32)
        m["fL"] = np.asarray(ins["L0_ffn_LiMEs"], np.float32)
        oneL = np.zeros((8, 1), np.float32)
        oneR = np.zeros((8, 1), np.float32)
        if cid > 0:
            oneL[cid - 1, 0] = 1.0
        if cid < NCORE - 1:
            oneR[cid + 1, 0] = 1.0
        m["oneL"] = oneL
        m["oneR"] = oneR
        for k, vv in tables.items():
            m[k] = vv
        in_maps.append(m)

    res = run_bass_kernel_spmd(nc, in_maps, core_ids=list(range(NCORE)))
    parts = [np.asarray(res.results[cid]["y"]).T for cid in range(NCORE)]
    out = np.concatenate(parts, axis=0).reshape(1, NCORE * T, D).astype(np.float32)
    return out


# version 15
# speedup vs baseline: 1.0172x; 1.0115x over previous
import sys
sys.path.insert(0, "/opt/trn_rl_repo")
import os
import numpy as np
import concourse.bass as bass
import concourse.mybir as mybir
from concourse.bass_utils import run_bass_kernel_spmd
from contextlib import ExitStack

F32 = mybir.dt.float32
BF16 = mybir.dt.bfloat16
AX = mybir.AxisListType.X
AF = mybir.ActivationFunctionType
ALU = mybir.AluOpType
AP = bass.AP

NCORE = 8
T = 1024
C = 32
D = 4096

CN = {"inv4096": 0, "eps": 1, "neg1": 2, "two": 3, "c15": 4, "big": 5, "eighth": 6,
      "eps9": 7, "half": 8, "one": 9, "neg2": 10}
CONST_VALS = [1.0 / 4096, 1e-5, -1.0, 2.0, 1.5, 1e9, 0.125, 1e-9, 0.5, 1.0, -2.0]

_PHASES = int(os.environ.get("KPHASES", "4"))
_KDBG = int(os.environ.get("KDBG", "0"))


class Sem:
    def __init__(self, h):
        self.h = h
        self.n = 0
    def inc(self, instr, k=1):
        instr.then_inc(self.h, k)
        self.n += k
        return self.n


class AutoDrain:
    _OPS = {"tensor_copy", "tensor_tensor", "tensor_scalar", "scalar_tensor_tensor",
            "reduce_max", "reduce_sum", "reciprocal", "memset", "activation", "copy",
            "tensor_scalar_add", "tensor_scalar_mul", "select"}
    _CACHE = {}
    def __new__(cls, eng):
        if isinstance(eng, AutoDrain):
            return eng
        key = id(eng)
        if key in cls._CACHE:
            return cls._CACHE[key]
        obj = super().__new__(cls)
        obj._eng = eng
        obj._last_out = ((), ())
        obj._last_out2 = ((), ())
        cls._CACHE[key] = obj
        return obj
    def __init__(self, eng):
        pass
    @staticmethod
    def _tname(ap):
        try:
            return ap.tensor.name
        except Exception:
            return None
    def _handle(self, name, kwargs):
        ins = []
        for k in ("in_", "in0", "in1", "scalar1", "scalar2", "scalar", "bias", "scale", "mask",
                  "on_true", "on_false", "ap"):
            v = kwargs.get(k)
            n = self._tname(v) if v is not None and not isinstance(v, (int, float)) else None
            if n:
                ins.append(n)
        outs = []
        for k in ("out", "accum_out", "ap"):
            v = kwargs.get(k)
            n = self._tname(v) if v is not None and not isinstance(v, (int, float)) else None
            if n:
                outs.append(n)
        hazard = False
        for (po, pi) in (self._last_out, self._last_out2):
            if any(n in po for n in ins) or any(n in po or n in pi for n in outs):
                hazard = True
        if hazard:
            self._eng.drain()
            self._last_out2 = ((), ())
        else:
            self._last_out2 = self._last_out
        self._last_out = (tuple(outs), tuple(ins))
    def __getattr__(self, name):
        attr = getattr(self._eng, name)
        if name in AutoDrain._OPS and callable(attr):
            def wrapped(*a, **kw):
                if a and name == "memset":
                    kw2 = dict(kw)
                    kw2["ap"] = a[0]
                    self._handle(name, kw2)
                    return attr(*a, **kw)
                if a:
                    raise TypeError("AutoDrain ops must use kwargs only: " + name)
                self._handle(name, kw)
                return attr(**kw)
            return wrapped
        return attr


def _host_tables():
    L = 4096
    kk = np.arange(L)
    theta32 = (np.float32(-2 * np.pi) * (kk / L).astype(np.float32)).astype(np.float32)
    th = theta32.astype(np.float64)
    om = np.cos(th) + 1j * np.sin(th)
    u_c = (1 - om) / (1 + om)
    c_c = 2.0 / (1 + om)
    f32 = np.float32
    ucat = np.concatenate([np.tile(u_c.real.astype(f32)[None, :], (64, 1)),
                           np.tile(u_c.imag.astype(f32)[None, :], (64, 1))], axis=0)
    cre = c_c.real.astype(f32).reshape(32, 128).T.copy()
    cim = c_c.imag.astype(f32).reshape(32, 128).T.copy()
    t_idx = np.arange(32)
    rho = np.arange(128)
    ang_tr = 2 * np.pi * np.outer(t_idx, rho) / 32
    M1 = np.concatenate([np.cos(ang_tr), -np.sin(ang_tr)], axis=0).astype(f32)
    M2 = np.concatenate([np.sin(ang_tr), np.cos(ang_tr)], axis=0).astype(f32)
    pp = np.arange(128)
    ang_pr = 2 * np.pi * np.outer(pp, rho) / 4096
    TW1 = np.cos(ang_pr).astype(f32)
    TW2 = np.sin(ang_pr).astype(f32)
    tau = np.arange(32)
    ang_pt = 2 * np.pi * np.outer(pp, tau) / 32
    C1 = (np.cos(ang_pt) / 4096).astype(f32)
    C2 = (-np.sin(ang_pt) / 4096).astype(f32)
    ident = np.eye(128, dtype=f32)
    mask = np.where(np.tril(np.ones((128, 128), bool)), 0.0, -1e9).astype(f32)
    return dict(t_ucat=ucat, t_cre=cre, t_cim=cim, t_M1=M1, t_M2=M2, t_TW1=TW1,
                t_TW2=TW2, t_C1=C1, t_C2=C2, t_ident=ident, t_mask=mask)


def build_nc():
    nc = bass.Bass("TRN2", target_bir_lowering=False)
    I = {}
    def di(name, shape):
        I[name] = nc.dram_tensor(name, list(shape), F32, kind="ExternalInput").ap()
    di("x0T", [D, T])
    di("aLam", [64, 2]); di("aP", [64, 2]); di("aQ", [64, 2]); di("aVc", [64, 64, 2])
    di("aCt", [1, 64, 2]); di("aBpil", [128, 2]); di("aD", [1, 1]); di("aLS", [1, 1])
    di("cwsh", [3, 512, 3072]); di("cb", [3072])
    di("fWsh", [512, D])
    di("wqsh", [128, 1024]); di("wksh", [128, 1024]); di("wvsh", [128, 1024]); di("wosh", [128, 1024])
    di("pwsh", [128, D])
    di("bq", [16, 64]); di("bk", [16, 64]); di("bv", [16, 64]); di("bo", [1024])
    di("pb", [D])
    di("fb", [D]); di("fA", [D, 32]); di("fB", [32, D]); di("fL", [8, 32])
    di("oneL", [8, 1]); di("oneR", [8, 1])
    for n, s in [("t_ucat", [128, 4096]), ("t_cre", [128, 32]), ("t_cim", [128, 32]),
                 ("t_M1", [64, 128]), ("t_M2", [64, 128]), ("t_TW1", [128, 128]),
                 ("t_TW2", [128, 128]), ("t_C1", [128, 32]), ("t_C2", [128, 32]),
                 ("t_ident", [128, 128]), ("t_mask", [128, 128])]:
        di(n, s)
    OUT = nc.dram_tensor("y", [D, T], F32, kind="ExternalOutput").ap()
    DBG = nc.dram_tensor("dbg", [128, 4096], F32, kind="ExternalOutput").ap() if _KDBG else None

    kflat_h = nc.dram_tensor("kflat_h", [4096], BF16).ap()
    scr_h = nc.dram_tensor("scr_h", [128 * 4608 + 256], BF16).ap()
    agw_in = nc.dram_tensor("agw_in", [4 * 128 * 4096], BF16).ap()
    agw_out = nc.dram_tensor("agw_out", [NCORE * 4 * 128 * 4096], BF16).ap()
    agc_in = nc.dram_tensor("agc_in", [12 * 128 * 3072], BF16).ap()
    agc_out = nc.dram_tensor("agc_out", [NCORE * 12 * 128 * 3072], BF16).ap()
    agq_in = nc.dram_tensor("agq_in", [128 * 1024], BF16).ap()
    agq_out = nc.dram_tensor("agq_out", [NCORE * 128 * 1024], BF16).ap()
    agk_in = nc.dram_tensor("agk_in", [128 * 1024], BF16).ap()
    agk_out = nc.dram_tensor("agk_out", [NCORE * 128 * 1024], BF16).ap()
    agv_in = nc.dram_tensor("agv_in", [128 * 1024], BF16).ap()
    agv_out = nc.dram_tensor("agv_out", [NCORE * 128 * 1024], BF16).ap()
    ago_in = nc.dram_tensor("ago_in", [128 * 1024], BF16).ap()
    ago_out = nc.dram_tensor("ago_out", [NCORE * 128 * 1024], BF16).ap()
    agp_in = nc.dram_tensor("agp_in", [128 * 4096], BF16).ap()
    agp_out = nc.dram_tensor("agp_out", [NCORE * 128 * 4096], BF16).ap()
    ar1_in = nc.dram_tensor("ar1_in", [1, 2], F32).ap()
    ar1_out = nc.dram_tensor("ar1_out", [1, 2], F32).ap()
    ar2_in = nc.dram_tensor("ar2_in", [1, 2], F32).ap()
    ar2_out = nc.dram_tensor("ar2_out", [1, 2], F32).ap()
    agh_in = nc.dram_tensor("agh_in", [2 * 4096], BF16).ap()
    agh_out = nc.dram_tensor("agh_out", [16 * 4096], BF16).ap()
    x2_h = nc.dram_tensor("x2_h", [D, T], F32).ap()
    hfl_h = nc.dram_tensor("hfl_h", [512], BF16).ap()

    st = ExitStack()
    e = st.enter_context
    xt_st = ExitStack()
    xt = xt_st.enter_context(nc.sbuf_tensor("xt", [128, C * T], F32, side="right"))

    consts = e(nc.sbuf_tensor("consts", [128, 16], F32))
    ones1f = e(nc.sbuf_tensor("ones1f", [1, 128], F32))
    ones1b = e(nc.sbuf_tensor("ones1b", [1, 128], BF16))
    onescb = e(nc.sbuf_tensor("onescb", [128, 1], BF16))
    identf = e(nc.sbuf_tensor("identf", [128, 128], F32))
    identb = e(nc.sbuf_tensor("identb", [128, 128], BF16))
    maskb = e(nc.sbuf_tensor("maskb", [128, 128], BF16))
    maskf = e(nc.sbuf_tensor("maskf", [128, 128], F32))
    DIb = e(nc.sbuf_tensor("DIb", [128, 128], BF16))
    bcol = e(nc.sbuf_tensor("bcol", [128, 32], F32))
    cbcol = e(nc.sbuf_tensor("cbcol", [128, 24], F32))
    bqcol = e(nc.sbuf_tensor("bqcol", [128, 8], F32))
    bkcol = e(nc.sbuf_tensor("bkcol", [128, 8], F32))
    bocol = e(nc.sbuf_tensor("bocol", [128, 8], F32))
    pbcol = e(nc.sbuf_tensor("pbcol", [128, 32], F32))
    bvrow = e(nc.sbuf_tensor("bvrow", [1, 1024], BF16))
    limes_b = e(nc.sbuf_tensor("limes_b", [8, 32], BF16))
    liT = e(nc.sbuf_tensor("liT", [32, 8], F32))
    onelb = e(nc.sbuf_tensor("onelb", [8, 1], BF16))
    onerb = e(nc.sbuf_tensor("onerb", [8, 1], BF16))
    t2s_c = e(nc.sbuf_tensor("t2s_c", [128, 1], F32))
    D_c = e(nc.sbuf_tensor("D_c", [128, 1], F32))
    arin_s = e(nc.sbuf_tensor("arin_s", [1, 2], F32))
    armax_s = e(nc.sbuf_tensor("armax_s", [1, 2], F32))
    hsds = e(nc.sbuf_tensor("hsds", [128, 2], F32))
    srow = e(nc.sbuf_tensor("srow", [1, 512], F32))
    brow = e(nc.sbuf_tensor("brow", [1, 512], F32))
    s_bc = [e(nc.sbuf_tensor(f"s_bc{g}", [128, 512], F32)) for g in range(2)]
    b_bc = [e(nc.sbuf_tensor(f"b_bc{g}", [128, 512], F32)) for g in range(2)]
    xbb = e(nc.sbuf_tensor("xbb", [128, 1024], BF16))
    sqb2 = e(nc.sbuf_tensor("sqb2", [128, 1024], BF16))
    t1f = e(nc.sbuf_tensor("t1f", [128, 512], F32))

    pf = [e(nc.psum_tensor(f"pf{i}", [128, 512], F32)) for i in range(6)]
    pbx = [e(nc.psum_tensor(f"pbx{i}", [128, 1024], BF16)) for i in range(2)]

    _sems = {}
    def QQ(name):
        if name not in _sems:
            _sems[name] = Sem(e(nc.semaphore("q_" + name)))
        return _sems[name]
    ct = Sem(e(nc.semaphore("ct")))
    cv = Sem(e(nc.semaphore("cv")))
    cs = Sem(e(nc.semaphore("cs")))
    block = e(nc.Block())

    cp = {}
    def mark(name, sem):
        cp[name] = (sem, sem.n)
    def wt(eng, name):
        s, v = cp[name]
        if v > 0:
            eng.wait_ge(s.h, v)

    @block.sync
    def _(s):
        s = AutoDrain(s)
        for c in range(C):
            QQ("x0").inc(s.dma_start(out=xt[:, c * T:(c + 1) * T], in_=I["x0T"][c * 128:(c + 1) * 128, :]), 16)
    mark("x0_loaded", QQ("x0"))

    Lcat = e(nc.sbuf_tensor("Lcat", [128, 1], F32))
    g1 = ExitStack()
    ge = g1.enter_context
    Lnat = ge(nc.sbuf_tensor("Lnat", [64, 2], F32))
    rows6 = ge(nc.sbuf_tensor("rows6", [1, 384], F32))
    Bprow = ge(nc.sbuf_tensor("Bprow", [128, 2], F32))
    Vcnat = ge(nc.sbuf_tensor("Vcnat", [64, 128], F32))
    Vt = ge(nc.sbuf_tensor("Vt", [128, 64], F32))
    lsD = ge(nc.sbuf_tensor("lsD", [1, 2], F32))
    m2s = ge(nc.sbuf_tensor("m2s", [1, 2], F32))
    bnat = ge(nc.sbuf_tensor("bnat", [32, 128], F32))
    cbnat = ge(nc.sbuf_tensor("cbnat", [24, 128], F32))
    pbnat = ge(nc.sbuf_tensor("pbnat", [32, 128], F32))
    bqnat = ge(nc.sbuf_tensor("bqnat", [8, 128], F32))
    bknat = ge(nc.sbuf_tensor("bknat", [8, 128], F32))
    bonat = ge(nc.sbuf_tensor("bonat", [8, 128], F32))
    limes_f = ge(nc.sbuf_tensor("limes_f", [8, 32], F32))
    sh1 = ge(nc.sbuf_tensor("sh1", [128, 4096], BF16))
    wstg = ge(nc.sbuf_tensor("wstg", [128, 4096], BF16))

    tbl = {}
    kst = ExitStack()
    ke = kst.enter_context
    for nm, shp in [("t_cre", [128, 32]), ("t_cim", [128, 32]), ("t_M1", [64, 128]),
                    ("t_M2", [64, 128]), ("t_TW1", [128, 128]), ("t_TW2", [128, 128]),
                    ("t_C1", [128, 32]), ("t_C2", [128, 32])]:
        tbl[nm] = ke(nc.sbuf_tensor("s" + nm, shp, F32))

    @block.sync
    def _(s):
        s = AutoDrain(s)
        QQ("tbl").inc(s.dma_start(out=identf[:, :], in_=I["t_ident"]), 16)
        for nm in tbl:
            QQ("tbl").inc(s.dma_start(out=tbl[nm][:, :], in_=I[nm]), 16)
    mark("tables_loaded", QQ("tbl"))

    @block.gpsimd
    def _(g):
        g = AutoDrain(g)
        QQ("small").inc(g.dma_start(out=Lnat[:, :], in_=I["aLam"]), 16)
        for j, (nm, off) in enumerate([("aCt", 0), ("aCt", 1), ("aP", 0), ("aP", 1), ("aQ", 0), ("aQ", 1)]):
            src = AP(tensor=I[nm].tensor, offset=off, ap=[[2, 64]])
            QQ("small").inc(g.dma_start(out=rows6[0:1, j * 64:(j + 1) * 64], in_=src,
                               allow_slow_non_contiguous=True), 16)
        QQ("small").inc(g.dma_start(out=Bprow[:, :], in_=I["aBpil"]), 16)
        QQ("small").inc(g.dma_start(out=Vcnat[:, :], in_=AP(tensor=I["aVc"].tensor, offset=0,
                                                   ap=[[128, 64], [1, 128]])), 16)
        QQ("small").inc(g.dma_start(out=lsD[0:1, 0:1], in_=I["aLS"]), 16)
        QQ("small").inc(g.dma_start(out=lsD[0:1, 1:2], in_=I["aD"]), 16)
        QQ("small").inc(g.dma_start(out=bnat[:, :], in_=AP(tensor=I["fb"].tensor, offset=0, ap=[[128, 32], [1, 128]])), 16)
        QQ("small").inc(g.dma_start(out=cbnat[:, :], in_=AP(tensor=I["cb"].tensor, offset=0, ap=[[128, 24], [1, 128]])), 16)
        QQ("small").inc(g.dma_start(out=pbnat[:, :], in_=AP(tensor=I["pb"].tensor, offset=0, ap=[[128, 32], [1, 128]])), 16)
        QQ("small").inc(g.dma_start(out=bqnat[:, :], in_=AP(tensor=I["bq"].tensor, offset=0, ap=[[128, 8], [1, 128]])), 16)
        QQ("small").inc(g.dma_start(out=bknat[:, :], in_=AP(tensor=I["bk"].tensor, offset=0, ap=[[128, 8], [1, 128]])), 16)
        QQ("small").inc(g.dma_start(out=bonat[:, :], in_=AP(tensor=I["bo"].tensor, offset=0, ap=[[128, 8], [1, 128]])), 16)
        QQ("small").inc(g.dma_start(out=bvrow[0:1, :], in_=AP(tensor=I["bv"].tensor, offset=0, ap=[[1, 1024]])), 16)
        QQ("small").inc(g.dma_start(out=limes_f[:, :], in_=I["fL"]), 16)
        QQ("small").inc(g.dma_start(out=identb[:, :], in_=I["t_ident"]), 16)
        QQ("small").inc(g.dma_start(out=maskb[:, :], in_=I["t_mask"]), 16)
        QQ("small").inc(g.dma_start(out=onelb[:, :], in_=I["oneL"]), 16)
        QQ("small").inc(g.dma_start(out=onerb[:, :], in_=I["oneR"]), 16)
    mark("small_loaded", QQ("small"))

    @block.vector
    def _(v):
        v = AutoDrain(v)
        for j, val in enumerate(CONST_VALS):
            v.memset(consts[:, j:j + 1], val)
        v.memset(ones1f[:, :], 1.0)
        v.memset(ones1b[:, :], 1.0)
        cv.inc(v.memset(onescb[:, :], 1.0))
    mark("consts_ready", cv)

    @block.vector
    def _(v):
        v = AutoDrain(v)
        wt(v, "small_loaded")
        cv.inc(v.tensor_copy(out=maskf[:, :], in_=maskb[:, :]))
    mark("maskf_ready", cv)

    @block.gpsimd
    def _(g):
        g = AutoDrain(g)
        wt(g, "small_loaded")
        QQ("lcat").inc(g.dma_start(out=Lcat[0:64, 0:1], in_=Lnat[:, 0:1]), 16)
        QQ("lcat").inc(g.dma_start(out=Lcat[64:128, 0:1], in_=Lnat[:, 1:2]), 16)
    mark("lcat_ready", QQ("lcat"))

    @block.scalar
    def _(s):
        s = AutoDrain(s)
        wt(s, "small_loaded")
        cs.inc(s.activation(out=lsD[0:1, 0:1], in_=lsD[0:1, 0:1], func=AF.Exp))
    mark("step_ready", cs)

    @block.vector
    def _(v):
        v = AutoDrain(v)
        wt(v, "step_ready")
        wt(v, "consts_ready")
        v.reciprocal(out=m2s[0:1, 0:1], in_=lsD[0:1, 0:1])
        cv.inc(v.tensor_scalar(out=m2s[0:1, 0:1], in0=m2s[0:1, 0:1],
                               scalar1=consts[0:1, CN["two"]:CN["two"] + 1], scalar2=None, op0=ALU.mult))
    mark("m2s_ready", cv)

    @block.tensor
    def _(t):
        wt(t, "m2s_ready")
        wt(t, "tables_loaded")
        t.matmul(pf[0][:, 0:1], ones1f[0:1, :], m2s[0:1, 0:1], start=True, stop=True)
        t.matmul(pf[0][:, 1:2], ones1f[0:1, :], lsD[0:1, 1:2], start=True, stop=True)
        ct.inc(t.transpose(pf[1][0:128, 0:64], Vcnat[:, :], identf[0:64, 0:64]))
    mark("bcast1_done", ct)

    @block.vector
    def _(v):
        v = AutoDrain(v)
        wt(v, "bcast1_done")
        v.tensor_copy(out=t2s_c[:, :], in_=pf[0][:, 0:1])
        v.tensor_copy(out=D_c[:, :], in_=pf[0][:, 1:2])
        v.tensor_copy(out=Vt[:, :], in_=pf[1][:, 0:64])
        cv.inc(v.tensor_scalar(out=DIb[:, :], in0=identf[:, :], scalar1=D_c[:, 0:1],
                               scalar2=None, op0=ALU.mult))
    mark("vt_ready", cv)

    @block.tensor
    def _(t):
        wt(t, "vt_ready")
        t.matmul(pf[2][0:1, 0:64], Bprow[:, 0:1], Vt[:, :], start=True, stop=True)
        ct.inc(t.matmul(pf[2][0:1, 64:128], Bprow[:, 1:2], Vt[:, :], start=True, stop=True))
    mark("B_done", ct)

    @block.tensor
    def _(t):
        wt(t, "B_done")
        t.transpose(pf[3][0:128, 0:32], bnat[:, :], identf[0:32, 0:32])
        t.transpose(pf[3][0:128, 32:56], cbnat[:, :], identf[0:24, 0:24])
        t.transpose(pf[4][0:128, 0:32], pbnat[:, :], identf[0:32, 0:32])
        t.transpose(pf[4][0:128, 32:40], bqnat[:, :], identf[0:8, 0:8])
        t.transpose(pf[4][0:128, 40:48], bknat[:, :], identf[0:8, 0:8])
        t.transpose(pf[4][0:128, 48:56], bonat[:, :], identf[0:8, 0:8])
        ct.inc(t.transpose(pf[5][0:32, 0:8], limes_f[:, :], identf[0:8, 0:8]))
    mark("biasT_done", ct)

    @block.vector
    def _(v):
        v = AutoDrain(v)
        wt(v, "biasT_done")
        v.tensor_copy(out=bcol[:, :], in_=pf[3][:, 0:32])
        v.tensor_copy(out=cbcol[:, :], in_=pf[3][:, 32:56])
        v.tensor_copy(out=pbcol[:, :], in_=pf[4][:, 0:32])
        v.tensor_copy(out=bkcol[:, :], in_=pf[4][:, 40:48])
        v.tensor_copy(out=bocol[:, :], in_=pf[4][:, 48:56])
        v.tensor_copy(out=liT[:, :], in_=pf[5][0:32, 0:8])
        v.tensor_copy(out=limes_b[:, :], in_=limes_f[:, :])
        cv.inc(v.tensor_scalar(out=bqcol[:, :], in0=pf[4][:, 32:40],
                               scalar1=consts[:, CN["eighth"]:CN["eighth"] + 1], scalar2=None, op0=ALU.mult))
    mark("bias_ready", cv)

    dcat = ke(nc.sbuf_tensor("dcat", [128, 512], F32))
    dsh = ke(nc.sbuf_tensor("dsh", [64, 512], F32))
    scr1 = ke(nc.sbuf_tensor("scr1", [64, 512], F32))
    scr2 = ke(nc.sbuf_tensor("scr2", [64, 512], F32))
    RCPT = ke(nc.sbuf_tensor("RCPT", [128, 4096], F32))
    for qq in range(8):
        ks = slice(qq * 512, (qq + 1) * 512)
        @block.sync
        def _(s, qq=qq, ks=ks):
            s = AutoDrain(s)
            if qq >= 1:
                wt(s, f"kq_done{qq - 1}")
            QQ("kqld").inc(s.dma_start(out=dcat[:, :], in_=I["t_ucat"][:, ks]), 16)
            mark(f"kq_load{qq}", QQ("kqld"))
        @block.vector
        def _(v, qq=qq):
            v = AutoDrain(v)
            wt(v, f"kq_load{qq}")
            wt(v, "lcat_ready")
            wt(v, "m2s_ready")
            cv.inc(v.tensor_scalar(out=dcat[:, :], in0=dcat[:, :], scalar1=t2s_c[:, 0:1],
                                   scalar2=Lcat[:, 0:1], op0=ALU.mult, op1=ALU.subtract))
            mark(f"kq_dcat{qq}", cv)
        @block.gpsimd
        def _(g, qq=qq):
            g = AutoDrain(g)
            wt(g, f"kq_dcat{qq}")
            QQ("kq").inc(g.dma_start(out=dsh[0:64, :], in_=dcat[64:128, :]), 16)
            mark(f"kq_dsh{qq}", QQ("kq"))
        @block.scalar
        def _(s, qq=qq, ks=ks):
            s = AutoDrain(s)
            wt(s, f"kq_dsh{qq}")
            s.activation(out=scr1[0:64, :], in_=dcat[0:64, :], func=AF.Square)
            cs.inc(s.activation(out=scr2[0:64, :], in_=dsh[0:64, :], func=AF.Square))
            mark(f"kq_sq{qq}", cs)
        @block.vector
        def _(v, qq=qq, ks=ks):
            v = AutoDrain(v)
            wt(v, f"kq_sq{qq}")
            v.tensor_tensor(out=scr1[0:64, :], in0=scr1[0:64, :], in1=scr2[0:64, :], op=ALU.add)
            v.reciprocal(out=scr1[0:64, :], in_=scr1[0:64, :])
            v.tensor_tensor(out=RCPT[0:64, ks], in0=dcat[0:64, :], in1=scr1[0:64, :], op=ALU.mult)
            cv.inc(v.tensor_tensor(out=dsh[0:64, :], in0=dsh[0:64, :], in1=scr1[0:64, :], op=ALU.mult))
            mark(f"kq_rcp{qq}", cv)
        @block.gpsimd
        def _(g, qq=qq, ks=ks):
            g = AutoDrain(g)
            wt(g, f"kq_rcp{qq}")
            QQ("kq").inc(g.dma_start(out=RCPT[64:128, ks], in_=dsh[0:64, :]), 16)
            mark(f"kq_done{qq}", QQ("kq"))

    vstage = ke(nc.sbuf_tensor("vstage", [1, 512], F32))
    Brows = ke(nc.sbuf_tensor("Brows", [1, 128], F32))
    vneg = ke(nc.sbuf_tensor("vneg", [1, 256], F32))
    Vmat = ke(nc.sbuf_tensor("Vmat", [128, 8], F32))
    @block.vector
    def _(v):
        v = AutoDrain(v)
        wt(v, "B_done")
        wt(v, "small_loaded")
        v.tensor_copy(out=Brows[0:1, :], in_=pf[2][0:1, 0:128])
        ctr, cti = rows6[0:1, 0:64], rows6[0:1, 64:128]
        pr, pi = rows6[0:1, 128:192], rows6[0:1, 192:256]
        qr, qi = rows6[0:1, 256:320], rows6[0:1, 320:384]
        Br, Bi = Brows[0:1, 0:64], Brows[0:1, 64:128]
        def cmulj(ar, ai, br, bi, o):
            re = vstage[0:1, o * 128:o * 128 + 64]
            im = vstage[0:1, o * 128 + 64:o * 128 + 128]
            v.tensor_tensor(out=re, in0=ar, in1=br, op=ALU.mult)
            v.tensor_tensor(out=t1f[0:1, 0:64], in0=ai, in1=bi, op=ALU.mult)
            v.tensor_tensor(out=re, in0=re, in1=t1f[0:1, 0:64], op=ALU.add)
            v.tensor_tensor(out=im, in0=ar, in1=bi, op=ALU.mult)
            v.tensor_tensor(out=t1f[0:1, 0:64], in0=ai, in1=br, op=ALU.mult)
            v.tensor_tensor(out=im, in0=im, in1=t1f[0:1, 0:64], op=ALU.subtract)
        cmulj(ctr, cti, Br, Bi, 0)
        cmulj(ctr, cti, pr, pi, 1)
        cmulj(qr, qi, Br, Bi, 2)
        cmulj(qr, qi, pr, pi, 3)
        for j in range(4):
            ins = v.tensor_scalar(out=vneg[0:1, j * 64:(j + 1) * 64], in0=vstage[0:1, j * 128:j * 128 + 64],
                                  scalar1=consts[0:1, CN["neg1"]:CN["neg1"] + 1], scalar2=None, op0=ALU.mult)
        cv.inc(ins)
    mark("vprod", cv)

    @block.gpsimd
    def _(g):
        g = AutoDrain(g)
        wt(g, "vprod")
        for j in range(4):
            QQ("vmat").inc(g.dma_start(out=Vmat[0:64, 2 * j:2 * j + 1], in_=vstage[0:1, j * 128:j * 128 + 64]), 16)
            QQ("vmat").inc(g.dma_start(out=Vmat[64:128, 2 * j:2 * j + 1], in_=vstage[0:1, j * 128 + 64:j * 128 + 128]), 16)
            QQ("vmat").inc(g.dma_start(out=Vmat[0:64, 2 * j + 1:2 * j + 2], in_=vstage[0:1, j * 128 + 64:j * 128 + 128]), 16)
            QQ("vmat").inc(g.dma_start(out=Vmat[64:128, 2 * j + 1:2 * j + 2], in_=vneg[0:1, j * 64:(j + 1) * 64]), 16)
    mark("vmat", QQ("vmat"))

    kxy = ke(nc.sbuf_tensor("kxy", [128, 256], F32))
    arcat = ke(nc.sbuf_tensor("arcat", [128, 64], F32))
    gsc = ke(nc.sbuf_tensor("gsc", [128, 160], F32))
    @block.tensor
    def _(t):
        wt(t, "vmat")
        wt(t, "kq_done7")
        wt(t, "vt_ready")
        for tt in range(32):
            ins = t.matmul(pf[0][:, 8 * tt:8 * tt + 8], RCPT[:, tt * 128:(tt + 1) * 128], Vmat[:, :],
                           start=True, stop=True)
        ct.inc(ins)
    mark("kxy_mm", ct)
    @block.vector
    def _(v):
        v = AutoDrain(v)
        wt(v, "kxy_mm")
        cv.inc(v.tensor_copy(out=kxy[:, :], in_=pf[0][:, 0:256]))
    mark("kxy_s", cv)

    @block.vector
    def _(v):
        v = AutoDrain(v)
        wt(v, "kxy_s")
        k00r, k00i = kxy[:, 0:256:8], kxy[:, 1:256:8]
        k01r, k01i = kxy[:, 2:256:8], kxy[:, 3:256:8]
        k10r, k10i = kxy[:, 4:256:8], kxy[:, 5:256:8]
        k11r, k11i = kxy[:, 6:256:8], kxy[:, 7:256:8]
        g32 = lambda j: gsc[:, j * 32:(j + 1) * 32]
        v.tensor_scalar(out=g32(0), in0=k11r, scalar1=consts[:, CN["one"]:CN["one"] + 1], scalar2=None, op0=ALU.add)
        v.tensor_tensor(out=g32(1), in0=g32(0), in1=g32(0), op=ALU.mult)
        v.tensor_tensor(out=g32(2), in0=k11i, in1=k11i, op=ALU.mult)
        v.tensor_tensor(out=g32(1), in0=g32(1), in1=g32(2), op=ALU.add)
        v.reciprocal(out=g32(1), in_=g32(1))
        v.tensor_tensor(out=g32(2), in0=k10r, in1=g32(0), op=ALU.mult)
        v.tensor_tensor(out=g32(3), in0=k10i, in1=k11i, op=ALU.mult)
        v.tensor_tensor(out=g32(2), in0=g32(2), in1=g32(3), op=ALU.add)
        v.tensor_tensor(out=g32(2), in0=g32(2), in1=g32(1), op=ALU.mult)
        v.tensor_tensor(out=g32(3), in0=k10i, in1=g32(0), op=ALU.mult)
        v.tensor_tensor(out=g32(4), in0=k10r, in1=k11i, op=ALU.mult)
        v.tensor_tensor(out=g32(3), in0=g32(3), in1=g32(4), op=ALU.subtract)
        v.tensor_tensor(out=g32(3), in0=g32(3), in1=g32(1), op=ALU.mult)
        v.tensor_tensor(out=g32(0), in0=k01r, in1=g32(2), op=ALU.mult)
        v.tensor_tensor(out=g32(1), in0=k01i, in1=g32(3), op=ALU.mult)
        v.tensor_tensor(out=g32(0), in0=g32(0), in1=g32(1), op=ALU.subtract)
        v.tensor_tensor(out=g32(1), in0=k01r, in1=g32(3), op=ALU.mult)
        v.tensor_tensor(out=g32(4), in0=k01i, in1=g32(2), op=ALU.mult)
        v.tensor_tensor(out=g32(1), in0=g32(1), in1=g32(4), op=ALU.add)
        v.tensor_tensor(out=g32(0), in0=k00r, in1=g32(0), op=ALU.subtract)
        v.tensor_tensor(out=g32(1), in0=k00i, in1=g32(1), op=ALU.subtract)
        v.tensor_tensor(out=g32(2), in0=tbl["t_cre"][:, :], in1=g32(0), op=ALU.mult)
        v.tensor_tensor(out=g32(3), in0=tbl["t_cim"][:, :], in1=g32(1), op=ALU.mult)
        v.tensor_tensor(out=arcat[:, 0:32], in0=g32(2), in1=g32(3), op=ALU.subtract)
        v.tensor_tensor(out=g32(2), in0=tbl["t_cre"][:, :], in1=g32(1), op=ALU.mult)
        v.tensor_tensor(out=g32(3), in0=tbl["t_cim"][:, :], in1=g32(0), op=ALU.mult)
        cv.inc(v.tensor_tensor(out=arcat[:, 32:64], in0=g32(2), in1=g32(3), op=ALU.add))
    mark("arcat", cv)

    arcT = ke(nc.sbuf_tensor("arcT", [64, 128], F32))
    Spre = ke(nc.sbuf_tensor("Spre", [128, 128], F32))
    Spim = ke(nc.sbuf_tensor("Spim", [128, 128], F32))
    Ksb = ke(nc.sbuf_tensor("Ksb", [128, 32], BF16))
    K2sb = ke(nc.sbuf_tensor("K2sb", [32, 128], BF16))
    @block.tensor
    def _(t):
        wt(t, "arcat")
        ct.inc(t.transpose(pf[1][0:64, 0:128], arcat[:, :], identf[:, :]))
    mark("arcT_mm", ct)
    @block.vector
    def _(v):
        v = AutoDrain(v)
        wt(v, "arcT_mm")
        cv.inc(v.tensor_copy(out=arcT[0:64, :], in_=pf[1][0:64, 0:128]))
    mark("arcT_s", cv)
    @block.tensor
    def _(t):
        wt(t, "arcT_s")
        wt(t, "vprod")
        t.matmul(pf[2][:, 0:128], arcT[0:64, :], tbl["t_M1"][:, :], start=True, stop=True)
        ct.inc(t.matmul(pf[3][:, 0:128], arcT[0:64, :], tbl["t_M2"][:, :], start=True, stop=True))
    mark("S_mm", ct)
    @block.vector
    def _(v):
        v = AutoDrain(v)
        wt(v, "S_mm")
        v.tensor_tensor(out=Spre[:, :], in0=pf[2][:, 0:128], in1=tbl["t_TW1"][:, :], op=ALU.mult)
        v.tensor_tensor(out=t1f[:, 0:128], in0=pf[3][:, 0:128], in1=tbl["t_TW2"][:, :], op=ALU.mult)
        v.tensor_tensor(out=Spre[:, :], in0=Spre[:, :], in1=t1f[:, 0:128], op=ALU.subtract)
        v.tensor_tensor(out=Spim[:, :], in0=pf[2][:, 0:128], in1=tbl["t_TW2"][:, :], op=ALU.mult)
        v.tensor_tensor(out=t1f[:, 0:128], in0=pf[3][:, 0:128], in1=tbl["t_TW1"][:, :], op=ALU.mult)
        cv.inc(v.tensor_tensor(out=Spim[:, :], in0=Spim[:, :], in1=t1f[:, 0:128], op=ALU.add))
    mark("Sp", cv)
    @block.tensor
    def _(t):
        wt(t, "Sp")
        wt(t, "bias_ready")
        t.matmul(pf[4][:, 0:32], Spre[:, :], tbl["t_C1"][:, :], start=True, stop=False)
        ct.inc(t.matmul(pf[4][:, 0:32], Spim[:, :], tbl["t_C2"][:, :], start=False, stop=True))
    mark("K_mm", ct)
    @block.vector
    def _(v):
        v = AutoDrain(v)
        wt(v, "K_mm")
        cv.inc(v.tensor_copy(out=Ksb[:, :], in_=pf[4][:, 0:32]))
    mark("Ksb_s", cv)
    @block.tensor
    def _(t):
        wt(t, "Ksb_s")
        ct.inc(t.transpose(pbx[0][0:32, 0:128], Ksb[:, :], identb[:, :]))
    mark("K2_mm", ct)
    @block.vector
    def _(v):
        v = AutoDrain(v)
        wt(v, "K2_mm")
        cv.inc(v.tensor_copy(out=K2sb[0:32, :], in_=pbx[0][0:32, 0:128]))
    mark("K2_s", cv)

    zsb = ke(nc.sbuf_tensor("zsb", [128, 288], BF16))
    @block.vector
    def _(v):
        v = AutoDrain(v)
        cv.inc(v.memset(zsb[:, :], 0.0))
    mark("zsb0", cv)
    @block.gpsimd
    def _(g):
        g = AutoDrain(g)
        wt(g, "zsb0")
        wt(g, "K2_s")
        QQ("scr").inc(g.dma_start(out=kflat_h, in_=K2sb[0:32, :]), 16)
        for qq in range(16):
            dst = AP(tensor=scr_h.tensor, offset=qq * 288, ap=[[4608, 128], [1, 288]])
            QQ("scr").inc(g.dma_start(out=dst, in_=zsb[:, :]), 16)
        g.wait_ge(QQ("scr").h, QQ("scr").n)
        dstk = AP(tensor=scr_h.tensor, offset=384, ap=[[4609, 128], [1, 4096]])
        srck = AP(tensor=kflat_h.tensor, offset=0, ap=[[0, 128], [1, 4096]])
        QQ("scr2").inc(g.dma_start(out=dstk, in_=srck), 16)
    mark("scr_k", QQ("scr2"))
    for ocl in range(4):
        @block.gpsimd
        def _(g, ocl=ocl):
            g = AutoDrain(g)
            if ocl == 0:
                wt(g, "small_loaded")
            else:
                wt(g, f"wc_out{ocl - 1}")
            QQ("wcld").inc(g.dma_start(out=sh1[:, :], in_=I["fWsh"][ocl * 128:(ocl + 1) * 128, :]), 16)
            mark(f"wc_ld{ocl}", QQ("wcld"))
        for dc in range(C):
            i = ocl * C + dc
            @block.tensor
            def _(t, ocl=ocl, dc=dc, i=i):
                if dc == 0:
                    wt(t, f"wc_ld{ocl}")
                    wt(t, "small_loaded")
                if i < 2:
                    wt(t, "K2_s")
                if i >= 2:
                    wt(t, f"wc_cv{i - 2}")
                ct.inc(t.transpose(pbx[i % 2][:, 0:128], sh1[:, dc * 128:(dc + 1) * 128], identb[:, :]))
                mark(f"wc_ct{i}", ct)
            @block.vector
            def _(v, dc=dc, i=i):
                v = AutoDrain(v)
                wt(v, f"wc_ct{i}")
                if dc == 0:
                    s_, v_ = cp.get(f"wc_out{(i // C) - 1}", (None, 0))
                    if s_ is not None and v_ > 0:
                        v.wait_ge(s_.h, v_)
                cv.inc(v.tensor_copy(out=wstg[:, dc * 128:(dc + 1) * 128], in_=pbx[i % 2][:, 0:128]))
                mark(f"wc_cv{i}", cv)
        @block.gpsimd
        def _(g, ocl=ocl):
            g = AutoDrain(g)
            wt(g, f"wc_cv{ocl * C + C - 1}")
            dst = AP(tensor=agw_in.tensor, offset=ocl * 128 * 4096, ap=[[4096, 128], [1, 4096]])
            QQ("wcout").inc(g.dma_start(out=dst, in_=wstg[:, :]), 16)
            mark(f"wc_out{ocl}", QQ("wcout"))
    cp["wcast_doneA"] = cp["wc_out3"]
    cp["wcast_doneB"] = cp["wc_out3"]

    castbufs = [sh1, wstg]
    for j in range(12):
        w_, dcl = j // 4, j % 4
        @block.gpsimd
        def _(g, j=j, w_=w_, dcl=dcl):
            g = AutoDrain(g)
            if j < 2:
                wt(g, "wcast_doneA")
                wt(g, "wcast_doneB")
            else:
                wt(g, f"cc_g1_{j - 2}")
            QQ(f"ccg{j % 2}").inc(g.dma_start(out=castbufs[j % 2][:, 0:3072],
                               in_=AP(tensor=I["cwsh"].tensor,
                                      offset=w_ * 512 * 3072 + dcl * 128 * 3072, ap=[[3072, 128], [1, 3072]])), 16)
            mark(f"cc_g0_{j}", QQ(f"ccg{j % 2}"))
            wt(g, f"cc_g0_{j}")
            dst = AP(tensor=agc_in.tensor, offset=(dcl * 3 + w_) * 128 * 3072, ap=[[3072, 128], [1, 3072]])
            QQ(f"ccg{j % 2}").inc(g.dma_start(out=dst, in_=castbufs[j % 2][:, 0:3072]), 16)
            mark(f"cc_g1_{j}", QQ(f"ccg{j % 2}"))
    for jq, (nm, srcI, dstH, cols) in enumerate([("q", "wqsh", agq_in, 1024), ("k", "wksh", agk_in, 1024),
                                 ("v", "wvsh", agv_in, 1024), ("o", "wosh", ago_in, 1024),
                                 ("p", "pwsh", agp_in, 4096)]):
        @block.gpsimd
        def _(g, jq=jq, nm=nm, srcI=srcI, dstH=dstH, cols=cols):
            g = AutoDrain(g)
            prevm = f"cc_g1_{10 + (jq % 2)}" if jq < 2 else f"sc_{'qkvop'[jq - 2]}1"
            wt(g, prevm)
            QQ(f"ccg{jq % 2}").inc(g.dma_start(out=castbufs[jq % 2][:, 0:cols], in_=I[srcI]), 16)
            mark(f"sc_{nm}0", QQ(f"ccg{jq % 2}"))
            wt(g, f"sc_{nm}0")
            QQ(f"ccg{jq % 2}").inc(g.dma_start(out=AP(tensor=dstH.tensor, offset=0, ap=[[cols, 128], [1, cols]]),
                               in_=castbufs[jq % 2][:, 0:cols]), 16)
            mark(f"sc_{nm}1", QQ(f"ccg{jq % 2}"))
    @block.gpsimd
    def _(g):
        g = AutoDrain(g)
        wt(g, "wcast_doneA")
        wt(g, "wcast_doneB")
        QQ("ccagw").inc(g.collective_compute("AllGather", ALU.bypass, replica_groups=[list(range(NCORE))],
                                    ins=[agw_in.opt()], outs=[agw_out.opt()]))
        mark("agw_done", QQ("ccagw"))
        for nm, i_, o_ in [("q", agq_in, agq_out), ("k", agk_in, agk_out), ("v", agv_in, agv_out),
                           ("o", ago_in, ago_out), ("p", agp_in, agp_out)]:
            wt(g, f"sc_{nm}1")
            QQ("ccag" + nm).inc(g.collective_compute("AllGather", ALU.bypass, replica_groups=[list(range(NCORE))],
                                        ins=[i_.opt()], outs=[o_.opt()]))
            mark(f"ag{nm}_done", QQ("ccag" + nm))


    kst.close()
    g1.close()
    tt_st = ExitStack()
    Tt = tt_st.enter_context(nc.sbuf_tensor("Tt", [128, 4096], BF16, side="right"))
    @block.sync
    def _(s):
        s = AutoDrain(s)
        wt(s, "scr_k")
        for j in range(32):
            src = AP(tensor=scr_h.tensor, offset=384 + 128 * j, ap=[[4608, 128], [1, 128]])
            QQ("tt").inc(s.dma_start(out=Tt[:, j * 128:(j + 1) * 128], in_=src), 16)
    mark("tt_loaded", QQ("tt"))
    if _KDBG:
        @block.vector
        def _(v):
            v = AutoDrain(v)
            wt(v, "tt_loaded")
            v.tensor_copy(out=dcat[:, 0:256], in_=kxy[:, 0:256])
            v.tensor_copy(out=dcat[:, 256:320], in_=arcat[:, 0:64])
            v.tensor_copy(out=dcat[:, 320:352], in_=Ksb[:, 0:32])
            v.tensor_copy(out=dcat[:, 352:480], in_=Spre[:, :])
            v.tensor_copy(out=dcat[:, 480:608], in_=Spim[:, :])
            v.tensor_copy(out=dcat[:, 608:616], in_=Vmat[:, 0:8])
            v.tensor_copy(out=dcat[0:64, 616:744], in_=arcT[0:64, :])
            v.tensor_copy(out=dcat[0:32, 744:872], in_=K2sb[0:32, :])
            cv.inc(v.tensor_copy(out=scr1[0:64, 0:384], in_=rows6[0:1, 0:384].partition_broadcast(64) if False else rows6[0:1, 0:384]) if False else v.tensor_copy(out=dcat[:, 872:873], in_=t2s_c[:, 0:1]))
        mark("dbgcp", cv)
        @block.sync
        def _(s):
            s = AutoDrain(s)
            wt(s, "dbgcp")
            QQ("dbg").inc(s.dma_start(out=DBG[:, 0:1024], in_=dcat[:, 0:1024]), 16)
            QQ("dbg").inc(s.dma_start(out=DBG[0:1, 1024:1408], in_=rows6[0:1, :]), 16)
            QQ("dbg").inc(s.dma_start(out=DBG[0:1, 1408:1536], in_=Brows[0:1, :]), 16)
            QQ("dbg").inc(s.dma_start(out=DBG[:, 1536:1600], in_=Vt[:, :]), 16)
            QQ("dbg").inc(s.dma_start(out=DBG[:, 1600:1602], in_=Bprow[:, :]), 16)
            QQ("dbg").inc(s.dma_start(out=DBG[0:64, 1602:1730], in_=Vcnat[:, :]), 16)
        mark("dbgout", QQ("dbg"))

    def stats_pass(xt_t, tag, first_wait=()):
        for i in range(64):
            c, h = i // 2, i % 2
            sl = slice(c * T + h * 512, c * T + h * 512 + 512)
            bs = slice((i % 2) * 512, (i % 2) * 512 + 512)
            @block.vector
            def _(v, i=i, sl=sl, bs=bs):
                v = AutoDrain(v)
                if i == 0:
                    for fwn in first_wait:
                        wt(v, fwn)
                if i >= 2:
                    wt(v, f"{tag}_stmm{i - 2}")
                v.tensor_copy(out=xbb[:, bs], in_=xt_t[:, sl])
                cv.inc(v.tensor_tensor(out=sqb2[:, bs], in0=xbb[:, bs], in1=xbb[:, bs], op=ALU.mult))
                mark(f"{tag}_sq{i}", cv)
            @block.tensor
            def _(t, i=i, c=c, h=h, bs=bs, tag=tag):
                wt(t, f"{tag}_sq{i}")
                if i == 0 and tag == "p1":
                    wt(t, "Sp")
                    wt(t, "Ksb_s")
                    wt(t, "bias_ready")
                    wt(t, "vprod")
                t.matmul(pf[2 + h][0:1, 0:512], onescb[:, 0:1], xbb[:, bs],
                         start=(c == 0), stop=(c == C - 1))
                ct.inc(t.matmul(pf[4 + h][0:1, 0:512], onescb[:, 0:1], sqb2[:, bs],
                                start=(c == 0), stop=(c == C - 1)))
                mark(f"{tag}_stmm{i}", ct)
        mark(f"{tag}_stats_done", ct)

    def rows_and_bcast(tag, h, sb=None, qb=None):
        if sb is None:
            sb, qb = 2 + h, 4 + h
        @block.vector
        def _(v, h=h, sb=sb, qb=qb):
            v = AutoDrain(v)
            wt(v, f"{tag}_stats_done")
            v.tensor_scalar(out=pf[0][0:1, 0:512], in0=pf[sb][0:1, 0:512],
                            scalar1=consts[0:1, 0:1], scalar2=None, op0=ALU.mult)
            v.tensor_scalar(out=pf[1][0:1, 0:512], in0=pf[qb][0:1, 0:512],
                            scalar1=consts[0:1, 0:1], scalar2=None, op0=ALU.mult)
            v.tensor_copy(out=t1f[0:1, 0:512], in_=pf[0][0:1, 0:512])
            v.tensor_tensor(out=t1f[0:1, 0:512], in0=pf[0][0:1, 0:512], in1=t1f[0:1, 0:512],
                            op=ALU.mult)
            cv.inc(v.tensor_tensor(out=t1f[0:1, 0:512], in0=pf[1][0:1, 0:512], in1=t1f[0:1, 0:512],
                                   op=ALU.subtract))
            mark(f"{tag}_var{h}", cv)
        @block.scalar
        def _(s, h=h):
            s = AutoDrain(s)
            wt(s, f"{tag}_var{h}")
            cs.inc(s.activation(out=t1f[0:1, 0:512], in_=t1f[0:1, 0:512], func=AF.Sqrt,
                                bias=consts[0:1, CN["eps"]:CN["eps"] + 1]))
            mark(f"{tag}_sd{h}", cs)
        @block.vector
        def _(v, h=h):
            v = AutoDrain(v)
            wt(v, f"{tag}_sd{h}")
            v.reciprocal(out=srow[0:1, :], in_=t1f[0:1, 0:512])
            v.tensor_tensor(out=brow[0:1, :], in0=pf[0][0:1, 0:512], in1=srow[0:1, :], op=ALU.mult)
            cv.inc(v.tensor_scalar(out=brow[0:1, :], in0=brow[0:1, :],
                                   scalar1=consts[0:1, CN["neg1"]:CN["neg1"] + 1], scalar2=None, op0=ALU.mult))
            mark(f"{tag}_srow{h}", cv)
        @block.tensor
        def _(t, h=h):
            wt(t, f"{tag}_srow{h}")
            t.matmul(pf[0][:, 0:512], ones1f[0:1, :], srow[0:1, :], start=True, stop=True)
            ct.inc(t.matmul(pf[1][:, 0:512], ones1f[0:1, :], brow[0:1, :], start=True, stop=True))
            mark(f"{tag}_bcmm{h}", ct)
        @block.vector
        def _(v, h=h):
            v = AutoDrain(v)
            wt(v, f"{tag}_bcmm{h}")
            v.tensor_copy(out=s_bc[h][:, :], in_=pf[0][:, 0:512])
            cv.inc(v.tensor_copy(out=b_bc[h][:, :], in_=pf[1][:, 0:512]))
            mark(f"{tag}_bc{h}", cv)

    def shadow_half(xt_t, shadow_t, tag, h, shadow_cols=512, col_off=0, extra=None, first_wait=()):
        for c in range(C):
            @block.vector
            def _(v, c=c, h=h):
                v = AutoDrain(v)
                if c == 0:
                    wt(v, f"{tag}_bc1")
                    for fwn in first_wait:
                        wt(v, fwn)
                sl = slice(c * T + h * 512, c * T + h * 512 + 512)
                osl = slice(c * shadow_cols + col_off, c * shadow_cols + col_off + 512)
                v.tensor_tensor(out=t1f[:, 0:512], in0=xt_t[:, sl], in1=s_bc[h][:, :], op=ALU.mult)
                last = v.tensor_tensor(out=shadow_t[:, osl], in0=t1f[:, 0:512], in1=b_bc[h][:, :], op=ALU.add)
                if extra is not None:
                    last = extra(v, c, h)
                cv.inc(last)
        mark(f"{tag}_shadow{h}", cv)

    sh_st = ExitStack()
    shadow = sh_st.enter_context(nc.sbuf_tensor("shadow", [128, C * 512], BF16))
    stats_pass(xt, "p1", first_wait=("x0_loaded", "consts_ready"))
    rows_and_bcast("p1", 0)
    rows_and_bcast("p1", 1)
    for g in range(2):
        if g == 0:
            fw = ("scr_k", "kxy_s", "K2_s", "arcat", "sc_p1", "sc_o1")
            shadow_half(xt, shadow, "p1", g, first_wait=fw)
        for n in range(C):
            i = g * C + n
            if g == 1:
                @block.vector
                def _(v, n=n):
                    v = AutoDrain(v)
                    if n == 0:
                        wt(v, "p1_mm31")
                        wt(v, "p1_bc1")
                    sl = slice(n * T + 512, n * T + 1024)
                    v.tensor_tensor(out=t1f[:, 0:512], in0=xt[:, sl], in1=s_bc[1][:, :], op=ALU.mult)
                    cv.inc(v.tensor_tensor(out=shadow[:, n * 512:(n + 1) * 512],
                                           in0=t1f[:, 0:512], in1=b_bc[1][:, :], op=ALU.add))
                    mark(f"p1_sh1c{n}", cv)
            @block.tensor
            def _(t, n=n, g=g, i=i):
                if n == 0:
                    wt(t, "tt_loaded")
                if g == 0:
                    if n == 0:
                        wt(t, f"p1_shadow{g}")
                else:
                    wt(t, f"p1_sh1c{n}")
                if i >= 2:
                    wt(t, f"p1_ev{i - 2}")
                bank = pf[i % 2]
                for k in range(n + 1):
                    d0 = n - k
                    t.matmul(bank[:, 0:512], Tt[:, d0 * 128:(d0 + 1) * 128],
                             shadow[:, k * 512:(k + 1) * 512], start=(k == 0), stop=False)
                ct.inc(t.matmul(bank[:, 0:512], DIb[:, :], shadow[:, n * 512:(n + 1) * 512],
                                start=False, stop=True))
                mark(f"p1_mm{i}", ct)
            @block.vector
            def _(v, n=n, g=g, i=i):
                v = AutoDrain(v)
                wt(v, f"p1_mm{i}")
                sl = slice(n * T + g * 512, n * T + g * 512 + 512)
                cv.inc(v.tensor_tensor(out=xt[:, sl], in0=xt[:, sl], in1=pf[i % 2][:, 0:512], op=ALU.add))
                mark(f"p1_ev{i}", cv)
    mark("p1_done", cv)
    tt_st.close()

    def ffn_phase(xt_t, shadow_t, tag, arb_in, arb_out, prev_done, post_abf=None, tail_hook=None):
        fst = ExitStack()
        fe = fst.enter_context
        A_bf = fe(nc.sbuf_tensor(f"Abf{tag}", [128, C * 32], BF16))
        wsl = [fe(nc.sbuf_tensor(f"wsl{tag}{i}", [128, 1024], BF16)) for i in range(2)]
        zT = fe(nc.sbuf_tensor(f"zT{tag}", [32, 1024], F32))
        zp_bf = fe(nc.sbuf_tensor(f"zp{tag}", [32, 1024], BF16))
        HslT = fe(nc.sbuf_tensor(f"Hsl{tag}", [8, 1024], F32))
        Han = fe(nc.sbuf_tensor(f"Han{tag}", [8, 512], F32))
        Dan = fe(nc.sbuf_tensor(f"Dan{tag}", [8, 512], F32))
        gt = [fe(nc.sbuf_tensor(f"gt{tag}{i}", [128, 8], F32)) for i in range(4)]
        gcol = fe(nc.sbuf_tensor(f"gcol{tag}", [128, 6], F32))
        wfT = fe(nc.sbuf_tensor(f"wfT{tag}", [8, 1024], BF16))
        Bsl = [fe(nc.sbuf_tensor(f"Bsl{tag}{i}", [32, 512], BF16)) for i in range(2)]

        if post_abf is not None:
            post_abf()
        @block.gpsimd
        def _(g):
            g = AutoDrain(g)
            wt(g, prev_done)
            for c in range(C):
                QQ(tag + "abf").inc(g.dma_start(out=A_bf[:, c * 32:(c + 1) * 32],
                                   in_=I["fA"][c * 128:(c + 1) * 128, :]), 16)
        mark(f"{tag}_abf", QQ(tag + "abf"))

        stats_pass(xt_t, tag, first_wait=(prev_done,))
        rows_and_bcast(tag, 0)
        rows_and_bcast(tag, 1)
        for g in range(2):
            for oc in range(C):
                i = g * C + oc
                for qj in range(4):
                    qidx = i * 4 + qj
                    if oc == 0:
                        @block.vector
                        def _(v, g=g, qj=qj):
                            v = AutoDrain(v)
                            if qj == 0:
                                wt(v, f"{tag}_bc{g}")
                                if g == 1:
                                    wt(v, f"{tag}_hmm{C - 1}")
                                    wt(v, f"{tag}_zmm0")
                            for c in range(qj * 8, qj * 8 + 8):
                                sl = slice(c * T + g * 512, c * T + g * 512 + 512)
                                v.tensor_tensor(out=t1f[:, 0:512], in0=xt_t[:, sl],
                                                in1=s_bc[g][:, :], op=ALU.mult)
                                last = v.tensor_tensor(out=shadow_t[:, c * 512:(c + 1) * 512],
                                                       in0=t1f[:, 0:512], in1=b_bc[g][:, :], op=ALU.add)
                            cv.inc(last)
                            mark(f"{tag}_sh{g}q{qj}", cv)
                    eng_sel = block.scalar if qidx % 2 == 1 else block.sync
                    @eng_sel
                    def _(s, oc=oc, qj=qj, qidx=qidx):
                        s = AutoDrain(s)
                        if qidx < 2:
                            wt(s, "agw_done")
                            wt(s, prev_done)
                        if qidx >= 2:
                            wt(s, f"{tag}_hq{qidx - 2}")
                        base = ((oc % 8) * 4 + (oc // 8)) * 128 * 4096 + qj * 1024
                        src = AP(tensor=agw_out.tensor, offset=base, ap=[[4096, 128], [1, 1024]])
                        QQ(f"{tag}wsl{qidx % 2}").inc(s.dma_start(out=wsl[qidx % 2][:, :], in_=src), 16)
                        mark(f"{tag}_wq{qidx}", QQ(f"{tag}wsl{qidx % 2}"))
                    @block.tensor
                    def _(t, oc=oc, g=g, i=i, qj=qj, qidx=qidx):
                        if qj == 0:
                            if i >= 2:
                                wt(t, f"{tag}_hev{i - 2}")
                        if oc == 0:
                            wt(t, f"{tag}_sh{g}q{qj}")
                        bank = pf[2 + (i % 2)]
                        wt(t, f"{tag}_wq{qidx}")
                        for dk in range(8):
                            dc = qj * 8 + dk
                            ins = t.matmul(bank[:, 0:512], wsl[qidx % 2][:, dk * 128:(dk + 1) * 128],
                                           shadow_t[:, dc * 512:(dc + 1) * 512],
                                           start=(dc == 0), stop=(dc == C - 1))
                        ct.inc(ins)
                        mark(f"{tag}_hq{qidx}", ct)
                        if qj == 3:
                            mark(f"{tag}_hmm{i}", ct)
                if oc == 0:
                    @block.tensor
                    def _(t, g=g):
                        wt(t, f"{tag}_sh{g}q3")
                        wt(t, f"{tag}_abf")
                        for c in range(C):
                            ins = t.matmul(pf[4][0:32, 0:512], A_bf[:, c * 32:(c + 1) * 32],
                                           shadow_t[:, c * 512:(c + 1) * 512], start=(c == 0), stop=(c == C - 1))
                        ct.inc(ins)
                        mark(f"{tag}_zmm{g}", ct)
                    @block.vector
                    def _(v, g=g):
                        v = AutoDrain(v)
                        wt(v, f"{tag}_zmm{g}")
                        cv.inc(v.tensor_copy(out=zT[0:32, g * 512:(g + 1) * 512], in_=pf[4][0:32, 0:512]))
                        mark(f"{tag}_zs{g}", cv)
                    @block.vector
                    def _(v, g=g, i=i):
                        v = AutoDrain(v)
                        wt(v, f"{tag}_hmm{i}")
                        cv.inc(v.tensor_scalar(out=HslT[0:8, g * 512:(g + 1) * 512],
                                               in0=pf[2 + (i % 2)][0:8, 0:512],
                                               scalar1=bcol[0:8, 0:1], scalar2=None, op0=ALU.add))
                        mark(f"{tag}_hsl{g}", cv)
                @block.vector
                def _(v, oc=oc, g=g, i=i):
                    v = AutoDrain(v)
                    wt(v, f"{tag}_hmm{i}")
                    if oc == 0:
                        wt(v, f"{tag}_hsl{g}")
                    sl = slice(oc * T + g * 512, oc * T + g * 512 + 512)
                    cv.inc(v.scalar_tensor_tensor(out=xt_t[:, sl], in0=xt_t[:, sl],
                                                  scalar=bcol[:, oc:oc + 1],
                                                  in1=pf[2 + (i % 2)][:, 0:512], op0=ALU.add, op1=ALU.add))
                    mark(f"{tag}_hev{i}", cv)
        mark(f"{tag}_H_done", cv)

        @block.tensor
        def _(t):
            wt(t, f"{tag}_zs1")
            zap = zT[0:32, :]
            zan = AP(tensor=zap.tensor, offset=zap.offset + 1, ap=[zap.ap[0], [2, 512]])
            ct.inc(t.matmul(pf[4][0:8, 0:512], liT[0:32, 0:8], zan, start=True, stop=True))
            mark(f"{tag}_dslmm", ct)
        @block.vector
        def _(v):
            v = AutoDrain(v)
            wt(v, f"{tag}_dslmm")
            wt(v, f"{tag}_hsl1")
            v.tensor_copy(out=Dan[0:8, :], in_=pf[4][0:8, 0:512])
            hap = HslT[0:8, :]
            hslan = AP(tensor=hap.tensor, offset=hap.offset + 1, ap=[hap.ap[0], [2, 512]])
            v.tensor_copy(out=Han[0:8, :], in_=hslan)
            v.reduce_max(out=gt[0][0:8, 0:1], in_=Han[0:8, :], axis=AX, apply_absolute_value=True)
            cv.inc(v.reduce_max(out=gt[0][0:8, 1:2], in_=Dan[0:8, :], axis=AX, apply_absolute_value=True))
            mark(f"{tag}_lmax", cv)
        @block.tensor
        def _(t):
            wt(t, f"{tag}_lmax")
            t.transpose(pf[5][0:1, 0:8], gt[0][0:8, 0:1], identf[0:8, 0:8])
            ct.inc(t.transpose(pf[5][0:1, 8:16], gt[0][0:8, 1:2], identf[0:8, 0:8]))
            mark(f"{tag}_lmaxT", ct)
        @block.vector
        def _(v):
            v = AutoDrain(v)
            wt(v, f"{tag}_lmaxT")
            v.reduce_max(out=arin_s[0:1, 0:1], in_=pf[5][0:1, 0:8], axis=AX)
            cv.inc(v.reduce_max(out=arin_s[0:1, 1:2], in_=pf[5][0:1, 8:16], axis=AX))
            mark(f"{tag}_lmax2", cv)
        @block.gpsimd
        def _(g):
            g = AutoDrain(g)
            wt(g, f"{tag}_lmax2")
            ars = QQ(tag + "ar")
            ars.inc(g.dma_start(out=arb_in, in_=arin_s[0:1, :]), 16)
            g.wait_ge(ars.h, ars.n)
            QQ(tag + "cc").inc(g.collective_compute("AllReduce", ALU.max, replica_groups=[list(range(NCORE))],
                                        ins=[arb_in.opt()], outs=[arb_out.opt()]))
            mark(f"{tag}_ar", QQ(tag + "cc"))
            wt(g, f"{tag}_ar")
            ars.inc(g.dma_start(out=armax_s[0:1, :], in_=arb_out), 16)
        mark(f"{tag}_armax", QQ(tag + "ar"))

        @block.vector
        def _(v):
            v = AutoDrain(v)
            wt(v, f"{tag}_armax")
            v.reciprocal(out=armax_s[0:1, :], in_=armax_s[0:1, :])
            cv.inc(v.tensor_scalar(out=armax_s[0:1, :], in0=armax_s[0:1, :],
                                   scalar1=consts[0:1, CN["half"]:CN["half"] + 1], scalar2=None, op0=ALU.mult))
            mark(f"{tag}_hsrow", cv)
        @block.tensor
        def _(t):
            wt(t, f"{tag}_hsrow")
            ct.inc(t.matmul(pf[4][:, 0:2], ones1f[0:1, :], armax_s[0:1, 0:2], start=True, stop=True))
            mark(f"{tag}_hsmm", ct)
        @block.vector
        def _(v):
            v = AutoDrain(v)
            wt(v, f"{tag}_hsmm")
            cv.inc(v.tensor_copy(out=hsds[:, :], in_=pf[4][:, 0:2]))
            mark(f"{tag}_hsds", cv)

        for a in range(4):
            @block.tensor
            def _(t, a=a):
                wt(t, f"{tag}_hsds")
                if a >= 1:
                    wt(t, f"{tag}_gw{a - 1}")
                t.transpose(pf[5][0:128, 0:8], Han[0:8, a * 128:(a + 1) * 128], identf[0:8, 0:8])
                ct.inc(t.transpose(pf[5][0:128, 8:16], Dan[0:8, a * 128:(a + 1) * 128], identf[0:8, 0:8]))
                mark(f"{tag}_gtr{a}", ct)
            @block.vector
            def _(v, a=a):
                v = AutoDrain(v)
                wt(v, f"{tag}_gtr{a}")
                v.tensor_scalar(out=gt[1][:, :], in0=pf[5][:, 8:16], scalar1=hsds[:, 1:2], scalar2=None, op0=ALU.mult)
                v.scalar_tensor_tensor(out=gt[0][:, :], in0=pf[5][:, 0:8], scalar=hsds[:, 0:1],
                                       in1=gt[1][:, :], op0=ALU.mult, op1=ALU.add)
                cv.inc(v.reduce_max(out=gcol[:, 0:1], in_=gt[0][:, :], axis=AX, negate=True))
                mark(f"{tag}_glog{a}", cv)
            @block.scalar
            def _(s, a=a):
                s = AutoDrain(s)
                wt(s, f"{tag}_glog{a}")
                cs.inc(s.activation(out=gt[1][:, :], in_=gt[0][:, :], func=AF.Exp,
                                    bias=gcol[:, 0:1], accum_out=gcol[:, 1:2]))
                mark(f"{tag}_gexp{a}", cs)
            @block.vector
            def _(v, a=a):
                v = AutoDrain(v)
                wt(v, f"{tag}_gexp{a}")
                v.reciprocal(out=gcol[:, 2:3], in_=gcol[:, 1:2])
                v.tensor_scalar(out=gt[1][:, :], in0=gt[1][:, :], scalar1=gcol[:, 2:3], scalar2=None, op0=ALU.mult)
                v.reduce_max(out=gcol[:, 0:1], in_=gt[1][:, :], axis=AX)
                v.tensor_scalar(out=gt[2][:, :], in0=gt[1][:, :], scalar1=gcol[:, 0:1], scalar2=None, op0=ALU.is_ge)
                v.reduce_sum(out=gcol[:, 1:2], in_=gt[2][:, :], axis=AX)
                v.tensor_scalar(out=gt[2][:, :], in0=gt[2][:, :], scalar1=consts[:, CN["big"]:CN["big"] + 1],
                                scalar2=None, op0=ALU.mult)
                v.tensor_tensor(out=gt[3][:, :], in0=gt[1][:, :], in1=gt[2][:, :], op=ALU.subtract)
                v.reduce_max(out=gcol[:, 2:3], in_=gt[3][:, :], axis=AX)
                v.tensor_scalar(out=gcol[:, 3:4], in0=gcol[:, 1:2], scalar1=consts[:, CN["c15"]:CN["c15"] + 1],
                                scalar2=None, op0=ALU.is_ge)
                v.tensor_tensor(out=gcol[:, 4:5], in0=gcol[:, 0:1], in1=gcol[:, 2:3], op=ALU.subtract)
                v.tensor_tensor(out=gcol[:, 5:6], in0=gcol[:, 3:4], in1=gcol[:, 4:5], op=ALU.mult)
                v.tensor_tensor(out=gcol[:, 0:1], in0=gcol[:, 2:3], in1=gcol[:, 5:6], op=ALU.add)
                cv.inc(v.tensor_scalar(out=gcol[:, 1:2], in0=gcol[:, 0:1],
                                       scalar1=consts[:, CN["neg2"]:CN["neg2"] + 1], scalar2=None, op0=ALU.mult))
                mark(f"{tag}_gth{a}", cv)
            @block.scalar
            def _(s, a=a):
                s = AutoDrain(s)
                wt(s, f"{tag}_gth{a}")
                cs.inc(s.activation(out=gt[2][:, :], in_=gt[1][:, :], func=AF.Sigmoid,
                                    scale=consts[:, CN["two"]:CN["two"] + 1], bias=gcol[:, 1:2]))
                mark(f"{tag}_gsig{a}", cs)
            @block.vector
            def _(v, a=a):
                v = AutoDrain(v)
                wt(v, f"{tag}_gsig{a}")
                v.tensor_tensor(out=gt[2][:, :], in0=gt[1][:, :], in1=gt[2][:, :], op=ALU.mult)
                v.reduce_sum(out=gcol[:, 2:3], in_=gt[2][:, :], axis=AX)
                v.tensor_scalar(out=gcol[:, 2:3], in0=gcol[:, 2:3],
                                scalar1=consts[:, CN["eps9"]:CN["eps9"] + 1], scalar2=None, op0=ALU.add)
                v.reciprocal(out=gcol[:, 3:4], in_=gcol[:, 2:3])
                cv.inc(v.tensor_scalar(out=gt[2][:, :], in0=gt[2][:, :], scalar1=gcol[:, 3:4],
                                       scalar2=None, op0=ALU.mult))
                mark(f"{tag}_gwts{a}", cv)
            @block.tensor
            def _(t, a=a):
                wt(t, f"{tag}_gwts{a}")
                ct.inc(t.transpose(pf[4][0:8, 0:128], gt[2][:, :], identf[:, :]))
                mark(f"{tag}_gwT{a}", ct)
            @block.vector
            def _(v, a=a):
                v = AutoDrain(v)
                wt(v, f"{tag}_gwT{a}")
                wap = wfT[0:8, :]
                even = AP(tensor=wap.tensor, offset=wap.offset + a * 256, ap=[wap.ap[0], [2, 128]])
                odd = AP(tensor=wap.tensor, offset=wap.offset + a * 256 + 1, ap=[wap.ap[0], [2, 128]])
                v.tensor_copy(out=even, in_=pf[4][0:8, 0:128])
                cv.inc(v.tensor_copy(out=odd, in_=pf[4][0:8, 0:128]))
                mark(f"{tag}_gw{a}", cv)
        mark(f"{tag}_wfT", cv)
        @block.tensor
        def _(t):
            wt(t, f"{tag}_wfT")
            t.matmul(pf[4][0:32, 0:512], limes_b[:, :], wfT[0:8, 0:512], start=True, stop=True)
            ct.inc(t.matmul(pf[5][0:32, 0:512], limes_b[:, :], wfT[0:8, 512:1024], start=True, stop=True))
            mark(f"{tag}_pmx", ct)
        @block.vector
        def _(v):
            v = AutoDrain(v)
            wt(v, f"{tag}_pmx")
            v.scalar_tensor_tensor(out=zp_bf[0:32, 0:512], in0=zT[0:32, 0:512],
                                   scalar=consts[0:32, CN["half"]:CN["half"] + 1],
                                   in1=pf[4][0:32, 0:512], op0=ALU.mult, op1=ALU.mult)
            cv.inc(v.scalar_tensor_tensor(out=zp_bf[0:32, 512:1024], in0=zT[0:32, 512:1024],
                                          scalar=consts[0:32, CN["half"]:CN["half"] + 1],
                                          in1=pf[5][0:32, 0:512], op0=ALU.mult, op1=ALU.mult))
            mark(f"{tag}_zp", cv)

        for ocg in range(8):
            @block.gpsimd
            def _(gg, ocg=ocg):
                gg = AutoDrain(gg)
                if ocg < 2:
                    wt(gg, prev_done)
                if ocg >= 2:
                    wt(gg, f"{tag}_dmm{(ocg - 2) * 8 + 7}")
                QQ(f"{tag}bsl{ocg % 2}").inc(gg.dma_start(out=Bsl[ocg % 2][:, :],
                                             in_=I["fB"][:, ocg * 512:(ocg + 1) * 512]), 16)
                mark(f"{tag}_dbsl{ocg}", QQ(f"{tag}bsl{ocg % 2}"))
            for g in range(2):
                for ocl in range(4):
                    oc = ocg * 4 + ocl
                    i = ocg * 8 + g * 4 + ocl
                    @block.tensor
                    def _(t, oc=oc, ocg=ocg, ocl=ocl, g=g, i=i):
                        if g == 0 and ocl == 0:
                            wt(t, f"{tag}_dbsl{ocg}")
                        if i == 0:
                            wt(t, f"{tag}_zp")
                            wt(t, f"{tag}_H_done")
                        if i >= 2:
                            wt(t, f"{tag}_dev{i - 2}")
                        ct.inc(t.matmul(pf[2 + (i % 2)][:, 0:512], Bsl[ocg % 2][:, ocl * 128:(ocl + 1) * 128],
                                        zp_bf[0:32, g * 512:(g + 1) * 512], start=True, stop=True))
                        mark(f"{tag}_dmm{i}", ct)
                    @block.vector
                    def _(v, oc=oc, g=g, i=i):
                        v = AutoDrain(v)
                        wt(v, f"{tag}_dmm{i}")
                        sl = slice(oc * T + g * 512, oc * T + g * 512 + 512)
                        cv.inc(v.tensor_tensor(out=xt_t[:, sl], in0=xt_t[:, sl],
                                               in1=pf[2 + (i % 2)][:, 0:512], op=ALU.add))
                        mark(f"{tag}_dev{i}", cv)
                    if tail_hook is not None and g == 1:
                        tail_hook(oc, i)
        mark(f"{tag}_done", cv)
        fst.close()

    def _emit_agc():
        @block.gpsimd
        def _(g):
            g = AutoDrain(g)
            wt(g, "cc_g1_11")
            QQ("ccagc").inc(g.collective_compute("AllGather", ALU.bypass, replica_groups=[list(range(NCORE))],
                                        ins=[agc_in.opt()], outs=[agc_out.opt()]))
            mark("agc_done", QQ("ccagc"))
    def p3_hook(c, devidx):
        def _stats_chunk(c):
            for h in range(2):
                i = c * 2 + h
                bs = slice(h * 512, h * 512 + 512)
                sl = slice(c * T + h * 512, c * T + h * 512 + 512)
                sb, qb = (0, 1) if h == 0 else (4, 5)
                @block.vector
                def _(v, i=i, sl=sl, bs=bs):
                    v = AutoDrain(v)
                    if i >= 2:
                        wt(v, f"p3_stmm{i - 2}")
                    v.tensor_copy(out=xbb[:, bs], in_=xt[:, sl])
                    cv.inc(v.tensor_tensor(out=sqb2[:, bs], in0=xbb[:, bs], in1=xbb[:, bs], op=ALU.mult))
                    mark(f"p3_sq{i}", cv)
                @block.tensor
                def _(t, i=i, c=c, bs=bs, sb=sb, qb=qb):
                    wt(t, f"p3_sq{i}")
                    t.matmul(pf[sb][0:1, 0:512], onescb[:, 0:1], xbb[:, bs],
                             start=(c == 0), stop=(c == C - 1))
                    ct.inc(t.matmul(pf[qb][0:1, 0:512], onescb[:, 0:1], sqb2[:, bs],
                                    start=(c == 0), stop=(c == C - 1)))
                    mark(f"p3_stmm{i}", ct)
        if c >= 1:
            _stats_chunk(c - 1)
        @block.sync
        def _(s, c=c, devidx=devidx):
            s = AutoDrain(s)
            wt(s, f"p2_dev{devidx}")
            QQ("x2sp").inc(s.dma_start(out=x2_h[c * 128:(c + 1) * 128, :], in_=xt[:, c * T:(c + 1) * T]), 16)
        if c == C - 1:
            _stats_chunk(C - 1)
            mark("p3_stats_done", ct)
            mark("x2_spilled", QQ("x2sp"))

    if _PHASES >= 2:
        ffn_phase(xt, shadow, "p2", ar1_in, ar1_out, "p1_done", post_abf=_emit_agc,
                  tail_hook=p3_hook if _PHASES >= 3 else None)
    sh_st.close()

    if _PHASES >= 3:
        xt_st.close()
        p3 = ExitStack()
        pe3 = p3.enter_context
        ln2 = pe3(nc.sbuf_tensor("ln2", [128, C * 1026], BF16))
        hstL = pe3(nc.sbuf_tensor("hstL", [128, 32], BF16))
        hstR = pe3(nc.sbuf_tensor("hstR", [128, 32], BF16))
        x2s = [pe3(nc.sbuf_tensor(f"x2s{i}", [128, 512], F32)) for i in range(4)]

        rows_and_bcast("p3", 0, sb=0, qb=1)
        rows_and_bcast("p3", 1, sb=4, qb=5)
        for h in range(2):
            for c in range(C):
                i2 = h * C + c
                eng_sel2 = block.scalar if i2 % 4 in (1, 3) else block.sync
                @eng_sel2
                def _(s, c=c, h=h, i2=i2):
                    s = AutoDrain(s)
                    if i2 < 4:
                        wt(s, "x2_spilled")
                        wt(s, "p3_stats_done")
                    if i2 >= 4:
                        wt(s, f"p3_sh{i2 - 4}")
                    QQ(f"x2s{i2 % 4}").inc(s.dma_start(out=x2s[i2 % 4][:, :],
                                       in_=x2_h[c * 128:(c + 1) * 128, h * 512:(h + 1) * 512]), 16)
                    mark(f"p3_xs{i2}", QQ(f"x2s{i2 % 4}"))
                @block.vector
                def _(v, c=c, h=h, i2=i2):
                    v = AutoDrain(v)
                    wt(v, f"p3_xs{i2}")
                    if i2 == 0:
                        wt(v, "p3_bc1")
                    col_off = 1 if h == 0 else 513
                    osl = slice(c * 1026 + col_off, c * 1026 + col_off + 512)
                    v.tensor_tensor(out=t1f[:, 0:512], in0=x2s[i2 % 4][:, :], in1=s_bc[h][:, :], op=ALU.mult)
                    v.tensor_tensor(out=ln2[:, osl], in0=t1f[:, 0:512], in1=b_bc[h][:, :], op=ALU.add)
                    if h == 0:
                        last = v.tensor_copy(out=hstL[:, c:c + 1], in_=ln2[:, c * 1026 + 1:c * 1026 + 2])
                    else:
                        last = v.tensor_copy(out=hstR[:, c:c + 1], in_=ln2[:, c * 1026 + 1024:c * 1026 + 1025])
                    cv.inc(last)
                    mark(f"p3_sh{i2}", cv)
            mark(f"p3_shadow{h}", cv)

        @block.gpsimd
        def _(g):
            g = AutoDrain(g)
            wt(g, "p3_shadow1")
            hs_ = QQ("halo")
            hs_.inc(g.dma_start(out=AP(tensor=agh_in.tensor, offset=0, ap=[[32, 128], [1, 32]]),
                               in_=hstL[:, :]), 16)
            hs_.inc(g.dma_start(out=AP(tensor=agh_in.tensor, offset=4096, ap=[[32, 128], [1, 32]]),
                               in_=hstR[:, :]), 16)
            g.wait_ge(hs_.h, hs_.n)
            QQ("ccagh").inc(g.collective_compute("AllGather", ALU.bypass, replica_groups=[list(range(NCORE))],
                                        ins=[agh_in.opt()], outs=[agh_out.opt()]))
        mark("agh_done", QQ("ccagh"))

        qk_sc = ExitStack()
        qkvT = qk_sc.enter_context(nc.sbuf_tensor("qkvT", [128, 24 * 1024], BF16, side="right"))
        c3 = ExitStack()
        ce3 = c3.enter_context
        cwb = [ce3(nc.sbuf_tensor(f"cwb{i}", [128, 512], BF16)) for i in range(3)]

        hi = ExitStack()
        he_ = hi.enter_context
        GlGf = he_(nc.sbuf_tensor("GlGf", [8, 8192], BF16, side="right"))
        hfl = he_(nc.sbuf_tensor("hfl", [1, 512], BF16, side="right"))
        haloL = he_(nc.sbuf_tensor("haloL", [128, 32], BF16, side="right"))
        haloR = he_(nc.sbuf_tensor("haloR", [128, 32], BF16, side="right"))
        @block.gpsimd
        def _(g):
            g = AutoDrain(g)
            wt(g, "agh_done")
            wt(g, "x2_spilled")
            QQ("halo").inc(g.dma_start(out=GlGf[0:8, 0:4096],
                               in_=AP(tensor=agh_out.tensor, offset=4096, ap=[[8192, 8], [1, 4096]])), 16)
            QQ("halo").inc(g.dma_start(out=GlGf[0:8, 4096:8192],
                               in_=AP(tensor=agh_out.tensor, offset=0, ap=[[8192, 8], [1, 4096]])), 16)
        mark("gl_loaded", QQ("halo"))
        for j in range(16):
            side, blk_ = j // 8, j % 8
            @block.tensor
            def _(t, j=j, side=side, blk_=blk_):
                wt(t, "gl_loaded")
                if j >= 2:
                    wt(t, f"p3_hf{j - 2}")
                lhs = onelb if side == 0 else onerb
                ct.inc(t.matmul(pf[4 + (j % 2)][0:1, 0:512], lhs[0:8, 0:1],
                                GlGf[0:8, side * 4096 + blk_ * 512: side * 4096 + (blk_ + 1) * 512],
                                start=True, stop=True))
                mark(f"p3_hfmm{j}", ct)
            @block.vector
            def _(v, j=j):
                v = AutoDrain(v)
                wt(v, f"p3_hfmm{j}")
                if j >= 1:
                    wt(v, f"p3_hf{j - 1}")
                cv.inc(v.tensor_copy(out=hfl[0:1, :], in_=pf[4 + (j % 2)][0:1, 0:512]))
                mark(f"p3_hfe{j}", cv)
            @block.gpsimd
            def _(g, j=j, side=side, blk_=blk_):
                g = AutoDrain(g)
                wt(g, f"p3_hfe{j}")
                hq_ = QQ("halo")
                hq_.inc(g.dma_start(out=hfl_h, in_=hfl[0:1, :]), 16)
                g.wait_ge(hq_.h, hq_.n)
                dstT = haloL if side == 0 else haloR
                hq_.inc(g.dma_start(out=dstT[blk_ * 16:(blk_ + 1) * 16, :],
                                   in_=AP(tensor=hfl_h.tensor, offset=0, ap=[[32, 16], [1, 32]])), 16)
                mark(f"p3_hf{j}", QQ("halo"))
        @block.vector
        def _(v):
            v = AutoDrain(v)
            wt(v, "p3_hf15")
            lap = ln2[:, :]
            v.tensor_copy(out=AP(tensor=lap.tensor, offset=lap.offset, ap=[lap.ap[0], [1026, 32]]),
                          in_=haloL[:, :])
            cv.inc(v.tensor_copy(out=AP(tensor=lap.tensor, offset=lap.offset + 1025, ap=[lap.ap[0], [1026, 32]]),
                                 in_=haloR[:, :]))
        mark("halo_in", cv)
        hi.close()

        def conv_bank(idx):
            if idx < 6:
                return pf[idx][:, 0:512]
            return pbx[idx - 6][:, :].bitcast(F32)[:, 0:512]
        for ccg in range(6):
            for dc in range(C):
                for w_ in range(3):
                    j = (ccg * C + dc) * 3 + w_
                    @block.sync
                    def _(s, j=j, ccg=ccg, dc=dc, w_=w_):
                        s = AutoDrain(s)
                        if j == 0:
                            wt(s, "agc_done")
                            wt(s, "x2_spilled")
                            wt(s, "p3_stats_done")
                        if j >= 3:
                            wt(s, f"p3_cmm{j - 3}")
                        base = ((dc // 4) * 12 + (dc % 4) * 3 + w_) * 128 * 3072 + ccg * 512
                        src = AP(tensor=agc_out.tensor, offset=base, ap=[[3072, 128], [1, 512]])
                        QQ(f"cw{j % 3}").inc(s.dma_start(out=cwb[j % 3][:, :], in_=src), 16)
                        mark(f"p3_cw{j}", QQ(f"cw{j % 3}"))
                    @block.tensor
                    def _(t, j=j, ccg=ccg, dc=dc, w_=w_):
                        wt(t, f"p3_cw{j}")
                        first = dc == 0 and w_ == 0
                        last = dc == C - 1 and w_ == 2
                        if first:
                            wt(t, "halo_in")
                            if ccg >= 1:
                                wt(t, f"p3_cev{ccg - 1}")
                        for ccl in range(4):
                            for tg in range(2):
                                ins = t.matmul(conv_bank(ccl * 2 + tg),
                                               cwb[j % 3][:, ccl * 128:(ccl + 1) * 128],
                                               ln2[:, dc * 1026 + tg * 512 + w_: dc * 1026 + tg * 512 + w_ + 512],
                                               start=first, stop=last)
                        ct.inc(ins)
                        mark(f"p3_cmm{j}", ct)
            @block.vector
            def _(v, ccg=ccg):
                v = AutoDrain(v)
                wt(v, f"p3_cmm{(ccg * C + C - 1) * 3 + 2}")
                for ccl in range(4):
                    for tg in range(2):
                        cc_ = ccg * 4 + ccl
                        ins = v.tensor_scalar(out=qkvT[:, cc_ * 1024 + tg * 512: cc_ * 1024 + (tg + 1) * 512],
                                              in0=conv_bank(ccl * 2 + tg), scalar1=cbcol[:, cc_:cc_ + 1],
                                              scalar2=None, op0=ALU.add)
                cv.inc(ins)
                mark(f"p3_cev{ccg}", cv)
        mark("conv_done", cv)
        c3.close()
        p3.close()

        q3 = ExitStack()
        qe3 = q3.enter_context
        qhT = qe3(nc.sbuf_tensor("qhT", [128, 8192], BF16, side="right"))
        khT = qe3(nc.sbuf_tensor("khT", [128, 8192], BF16, side="right"))
        vhT = qe3(nc.sbuf_tensor("vhT", [128, 8192], BF16, side="right"))
        wqb = [qe3(nc.sbuf_tensor(f"wqb{i}", [128, 1024], BF16, side="right")) for i in range(2)]
        avs = qe3(nc.sbuf_tensor("avs", [128, 8192], BF16, side="right"))
        for qk in range(2):
            src_ag = agq_out if qk == 0 else agk_out
            dst_t = qhT if qk == 0 else khT
            bias_c = bqcol if qk == 0 else bkcol
            agmark = "agq_done" if qk == 0 else "agk_done"
            for hec in range(8):
                hidx = qk * 8 + hec
                @block.sync
                def _(s, qk=qk, hec=hec, hidx=hidx, src_ag=src_ag, agmark=agmark):
                    s = AutoDrain(s)
                    if hidx % 8 == 0:
                        wt(s, agmark)
                        wt(s, "conv_done")
                    if hidx >= 2:
                        wt(s, f"p3_qmmlast{hidx - 2}")
                    srcap = AP(tensor=src_ag.tensor, offset=hec * 128,
                               ap=[[1024, 128], [131072, 8], [1, 128]])
                    QQ(f"qw{hidx % 2}").inc(s.dma_start(out=wqb[hidx % 2][:, :], in_=srcap), 16)
                    mark(f"p3_qw{hidx}", QQ(f"qw{hidx % 2}"))
                for g in range(2):
                    i = hidx * 2 + g
                    @block.tensor
                    def _(t, qk=qk, hec=hec, hidx=hidx, g=g, i=i):
                        if g == 0:
                            wt(t, f"p3_qw{hidx}")
                        if i >= 2:
                            wt(t, f"p3_qev{i - 2}")
                        for ccj in range(8):
                            ins = t.matmul(pf[2 + (i % 2)][:, 0:512], wqb[hidx % 2][:, ccj * 128:(ccj + 1) * 128],
                                           qkvT[:, (qk * 8 + ccj) * 1024 + g * 512:(qk * 8 + ccj) * 1024 + (g + 1) * 512],
                                           start=(ccj == 0), stop=(ccj == 7))
                        ct.inc(ins)
                        mark(f"p3_qmm{i}", ct)
                        if g == 1:
                            mark(f"p3_qmmlast{hidx}", ct)
                    @block.scalar
                    def _(s, qk=qk, i=i, hec=hec, g=g, dst_t=dst_t, bias_c=bias_c):
                        s = AutoDrain(s)
                        wt(s, f"p3_qmm{i}")
                        if qk == 0:
                            ins = s.activation(out=dst_t[:, hec * 1024 + g * 512:hec * 1024 + (g + 1) * 512],
                                               in_=pf[2 + (i % 2)][:, 0:512], func=AF.Identity,
                                               scale=consts[:, CN["eighth"]:CN["eighth"] + 1],
                                               bias=bias_c[:, hec:hec + 1])
                        else:
                            ins = s.activation(out=dst_t[:, hec * 1024 + g * 512:hec * 1024 + (g + 1) * 512],
                                               in_=pf[2 + (i % 2)][:, 0:512], func=AF.Identity,
                                               bias=bias_c[:, hec:hec + 1])
                        cs.inc(ins)
                        mark(f"p3_qev{i}", cs)
        mark("qk_done", cs)

        @block.sync
        def _(s):
            s = AutoDrain(s)
            wt(s, "agv_done")
            wt(s, "conv_done")
            for ccj in range(8):
                QQ("avs").inc(s.dma_start(out=avs[:, ccj * 1024:(ccj + 1) * 1024],
                                   in_=AP(tensor=agv_out.tensor, offset=ccj * 128 * 1024,
                                          ap=[[1024, 128], [1, 1024]])), 16)
        mark("avs_loaded", QQ("avs"))
        for tt in range(8):
            for hf in range(2):
                i = tt * 2 + hf
                @block.tensor
                def _(t, tt=tt, hf=hf, i=i):
                    if i == 0:
                        wt(t, "avs_loaded")
                        wt(t, "conv_done")
                    if i >= 2:
                        wt(t, f"p3_vev{i - 2}")
                    bank = pf[4 + (i % 2)]
                    for ccj in range(8):
                        t.matmul(bank[:, 0:512],
                                 qkvT[:, (16 + ccj) * 1024 + tt * 128:(16 + ccj) * 1024 + (tt + 1) * 128],
                                 avs[:, ccj * 1024 + hf * 512:ccj * 1024 + (hf + 1) * 512],
                                 start=(ccj == 0), stop=False)
                    ct.inc(t.matmul(bank[:, 0:512], ones1b[0:1, :], bvrow[0:1, hf * 512:(hf + 1) * 512],
                                    start=False, stop=True))
                    mark(f"p3_vmm{i}", ct)
                @block.vector
                def _(v, tt=tt, hf=hf, i=i):
                    v = AutoDrain(v)
                    wt(v, f"p3_vmm{i}")
                    cv.inc(v.tensor_copy(out=vhT[:, tt * 1024 + hf * 512:tt * 1024 + (hf + 1) * 512],
                                         in_=pf[4 + (i % 2)][:, 0:512]))
                    mark(f"p3_vev{i}", cv)
        mark("vh_done", cv)

        a3 = ExitStack()
        ae3 = a3.enter_context
        oT = ae3(nc.sbuf_tensor("oT", [128, 8192], BF16))
        pnb = ae3(nc.sbuf_tensor("pnb", [128, 2048], BF16))
        pne = ae3(nc.sbuf_tensor("pne", [128, 2048], BF16))
        pTb = ae3(nc.sbuf_tensor("pTb", [128, 512], BF16))
        prow = ae3(nc.sbuf_tensor("prow", [128, 8], F32))
        def att_post(sg):
            for b in range(16):
                i = sg * 16 + b
                tt, h = i // 16, i % 16
                hec, hp = h // 2, h % 2
                sl2 = i % 4
                g4 = i // 4
                @block.vector
                def _(v, i=i, b=b, sl2=sl2, g4=g4):
                    v = AutoDrain(v)
                    wt(v, f"p3_exp{g4}")
                    v.reduce_sum(out=prow[:, sl2 * 2 + 1:sl2 * 2 + 2],
                                 in_=pnb[:, b * 128:(b + 1) * 128], axis=AX)
                    v.reciprocal(out=prow[:, sl2 * 2 + 1:sl2 * 2 + 2], in_=prow[:, sl2 * 2 + 1:sl2 * 2 + 2])
                    cv.inc(v.tensor_scalar(out=pnb[:, b * 128:(b + 1) * 128],
                                           in0=pnb[:, b * 128:(b + 1) * 128],
                                           scalar1=prow[:, sl2 * 2 + 1:sl2 * 2 + 2],
                                           scalar2=None, op0=ALU.mult))
                    mark(f"p3_nrm{i}", cv)
                @block.tensor
                def _(t, i=i, b=b, sl2=sl2):
                    wt(t, f"p3_nrm{i}")
                    if i >= 2:
                        wt(t, f"p3_pT{i - 2}")
                    ct.inc(t.transpose(pbx[i % 2][:, 0:128], pnb[:, b * 128:(b + 1) * 128], identb[:, :]))
                    mark(f"p3_ptr{i}", ct)
                @block.vector
                def _(v, i=i, sl2=sl2):
                    v = AutoDrain(v)
                    wt(v, f"p3_ptr{i}")
                    if i >= 4:
                        wt(v, f"p3_omm{i - 4}")
                    cv.inc(v.tensor_copy(out=pTb[:, sl2 * 128:(sl2 + 1) * 128], in_=pbx[i % 2][:, 0:128]))
                    mark(f"p3_pT{i}", cv)
                @block.tensor
                def _(t, tt=tt, h=h, i=i, hp=hp, sl2=sl2):
                    wt(t, f"p3_pT{i}")
                    if hp == 0 and i >= 3:
                        wt(t, f"p3_oev{i - 3}")
                    ct.inc(t.matmul(pf[4 + ((i // 2) % 2)][hp * 64:hp * 64 + 64, 0:128],
                                    vhT[:, tt * 1024 + h * 64:tt * 1024 + h * 64 + 64],
                                    pTb[:, sl2 * 128:(sl2 + 1) * 128], start=True, stop=True))
                    mark(f"p3_omm{i}", ct)
                if hp == 1:
                    @block.vector
                    def _(v, tt=tt, hec=hec, i=i):
                        v = AutoDrain(v)
                        wt(v, f"p3_omm{i}")
                        cv.inc(v.tensor_copy(out=oT[:, hec * 1024 + tt * 128:hec * 1024 + (tt + 1) * 128],
                                             in_=pf[4 + ((i // 2) % 2)][:, 0:128]))
                        mark(f"p3_oev{i}", cv)

        for sg in range(8):
            if sg >= 1:
                att_post(sg - 1)
            for p16 in range(16):
                i = sg * 16 + p16
                tt, h = i // 16, i % 16
                hec, hp = h // 2, h % 2
                sl2 = i % 4
                @block.tensor
                def _(t, tt=tt, h=h, i=i, hec=hec, hp=hp):
                    if i == 0:
                        wt(t, "qk_done")
                        wt(t, "vh_done")
                    if i >= 4:
                        wt(t, f"p3_sub{i - 4}")
                    bank = pf[i % 4]
                    ct.inc(t.matmul(bank[:, 0:128],
                             qhT[hp * 64:hp * 64 + 64, hec * 1024 + tt * 128:hec * 1024 + (tt + 1) * 128],
                             khT[hp * 64:hp * 64 + 64, hec * 1024 + tt * 128:hec * 1024 + (tt + 1) * 128],
                             start=True, stop=True))
                    mark(f"p3_smm{i}", ct)
                @block.vector
                def _(v, i=i, p16=p16, sl2=sl2, sg=sg):
                    v = AutoDrain(v)
                    wt(v, f"p3_smm{i}")
                    if i == 0:
                        wt(v, "maskf_ready")
                    if p16 == 0 and sg >= 1:
                        wt(v, f"p3_exp{sg * 4 - 1}")
                    bank = pf[i % 4]
                    v.reduce_max(out=prow[:, sl2 * 2:sl2 * 2 + 1], in_=bank[:, 0:128],
                                 axis=AX, negate=True)
                    cv.inc(v.scalar_tensor_tensor(out=pne[:, p16 * 128:(p16 + 1) * 128],
                                                  in0=bank[:, 0:128],
                                                  scalar=prow[:, sl2 * 2:sl2 * 2 + 1],
                                                  in1=maskf[:, :], op0=ALU.add, op1=ALU.add))
                    mark(f"p3_sub{i}", cv)
            @block.scalar
            def _(s, sg=sg):
                s = AutoDrain(s)
                if sg >= 1:
                    wt(s, f"p3_ptr{sg * 16 - 1}")
                for gg in range(4):
                    g4 = sg * 4 + gg
                    wt(s, f"p3_sub{g4 * 4 + 3}")
                    cs.inc(s.activation(out=pnb[:, gg * 512:(gg + 1) * 512],
                                        in_=pne[:, gg * 512:(gg + 1) * 512], func=AF.Exp))
                    mark(f"p3_exp{g4}", cs)
        att_post(7)
        mark("att_done", cv)
        q3.close()
        qk_sc.close()

        o3 = ExitStack()
        oe3 = o3.enter_context
        o2T = oe3(nc.sbuf_tensor("o2T", [128, 8192], BF16, side="right"))
        wob = [oe3(nc.sbuf_tensor(f"wob{i}", [128, 1024], BF16, side="right")) for i in range(2)]
        for wc in range(8):
            @block.sync
            def _(s, wc=wc):
                s = AutoDrain(s)
                if wc == 0:
                    wt(s, "ago_done")
                    wt(s, "att_done")
                if wc >= 2:
                    wt(s, f"p3_o2mmlast{wc - 2}")
                srcap = AP(tensor=ago_out.tensor, offset=wc * 128,
                           ap=[[1024, 128], [131072, 8], [1, 128]])
                QQ(f"ow{wc % 2}").inc(s.dma_start(out=wob[wc % 2][:, :], in_=srcap), 16)
                mark(f"p3_o2w{wc}", QQ(f"ow{wc % 2}"))
            for g in range(2):
                i = wc * 2 + g
                @block.tensor
                def _(t, wc=wc, g=g, i=i):
                    if g == 0:
                        wt(t, f"p3_o2w{wc}")
                    if i >= 2:
                        wt(t, f"p3_o2ev{i - 2}")
                    for hec in range(8):
                        ins = t.matmul(pf[2 + (i % 2)][:, 0:512], wob[wc % 2][:, hec * 128:(hec + 1) * 128],
                                       oT[:, hec * 1024 + g * 512:hec * 1024 + (g + 1) * 512],
                                       start=(hec == 0), stop=(hec == 7))
                    ct.inc(ins)
                    mark(f"p3_o2mm{i}", ct)
                    if g == 1:
                        mark(f"p3_o2mmlast{wc}", ct)
                @block.vector
                def _(v, i=i, wc=wc, g=g):
                    v = AutoDrain(v)
                    wt(v, f"p3_o2mm{i}")
                    cv.inc(v.tensor_scalar(out=o2T[:, wc * 1024 + g * 512:wc * 1024 + (g + 1) * 512],
                                           in0=pf[2 + (i % 2)][:, 0:512],
                                           scalar1=bocol[:, wc:wc + 1], scalar2=None, op0=ALU.add))
                    mark(f"p3_o2ev{i}", cv)
        mark("o2_done", cv)

        a3.close()
        xt3_st = ExitStack()
        xt3 = xt3_st.enter_context(nc.sbuf_tensor("xt3", [128, C * T], F32))
        pr3 = ExitStack()
        pre3 = pr3.enter_context
        pwb = [pre3(nc.sbuf_tensor(f"pwb{i}", [128, 1024], BF16)) for i in range(2)]
        x2b = [pre3(nc.sbuf_tensor(f"x2b{i}", [128, 512], F32)) for i in range(2)]
        for dmc in range(C):
            @block.sync
            def _(s, dmc=dmc):
                s = AutoDrain(s)
                if dmc == 0:
                    wt(s, "agp_done")
                    wt(s, "o2_done")
                if dmc >= 2:
                    wt(s, f"p3_pmmlast{dmc - 2}")
                srcap = AP(tensor=agp_out.tensor, offset=dmc * 128,
                           ap=[[4096, 128], [524288, 8], [1, 128]])
                QQ(f"pw{dmc % 2}").inc(s.dma_start(out=pwb[dmc % 2][:, :], in_=srcap), 16)
                mark(f"p3_pw{dmc}", QQ(f"pw{dmc % 2}"))
            for g in range(2):
                i = dmc * 2 + g
                @block.sync
                def _(s, dmc=dmc, g=g, i=i):
                    s = AutoDrain(s)
                    if i < 2:
                        wt(s, "o2_done")
                    if i >= 2:
                        wt(s, f"p3_pev{i - 2}")
                    QQ(f"x2b{i % 2}").inc(s.dma_start(out=x2b[i % 2][:, :],
                                       in_=x2_h[dmc * 128:(dmc + 1) * 128, g * 512:(g + 1) * 512]), 16)
                    mark(f"p3_x2b{i}", QQ(f"x2b{i % 2}"))
                @block.tensor
                def _(t, dmc=dmc, g=g, i=i):
                    if g == 0:
                        wt(t, f"p3_pw{dmc}")
                    if i >= 2:
                        wt(t, f"p3_pev{i - 2}")
                    for wc in range(8):
                        ins = t.matmul(pf[2 + (i % 2)][:, 0:512], pwb[dmc % 2][:, wc * 128:(wc + 1) * 128],
                                       o2T[:, wc * 1024 + g * 512:wc * 1024 + (g + 1) * 512],
                                       start=(wc == 0), stop=(wc == 7))
                    ct.inc(ins)
                    mark(f"p3_pmm{i}", ct)
                    if g == 1:
                        mark(f"p3_pmmlast{dmc}", ct)
                @block.vector
                def _(v, i=i, dmc=dmc, g=g):
                    v = AutoDrain(v)
                    wt(v, f"p3_pmm{i}")
                    wt(v, f"p3_x2b{i}")
                    cv.inc(v.scalar_tensor_tensor(out=xt3[:, dmc * T + g * 512:dmc * T + (g + 1) * 512],
                                                  in0=x2b[i % 2][:, :], scalar=pbcol[:, dmc:dmc + 1],
                                                  in1=pf[2 + (i % 2)][:, 0:512], op0=ALU.add, op1=ALU.add))
                    mark(f"p3_pev{i}", cv)
        mark("p3_done", cv)
        pr3.close()
        o3.close()
        xt_final = xt3
    else:
        xt_final = xt

    if _PHASES >= 4:
        def out_hook(c, devidx):
            eng_sel3 = block.scalar if c % 2 == 1 else block.sync
            @eng_sel3
            def _(s, c=c, devidx=devidx):
                s = AutoDrain(s)
                wt(s, f"p4_dev{devidx}")
                QQ(f"out{c % 2}").inc(s.dma_start(out=OUT[c * 128:(c + 1) * 128, :],
                                      in_=xt_final[:, c * T:(c + 1) * T]), 16)
        sh2_st = ExitStack()
        shadow2 = sh2_st.enter_context(nc.sbuf_tensor("shadow2", [128, C * 512], BF16))
        ffn_phase(xt_final, shadow2, "p4", ar2_in, ar2_out, "p3_done", tail_hook=out_hook)
        final_mark = "p4_done"
        sh2_st.close()
    else:
        final_mark = {1: "p1_done", 2: "p2_done", 3: "p3_done"}[_PHASES]

    @block.sync
    def _(s):
        s = AutoDrain(s)
        wt(s, final_mark)
        s.wait_ge(cv.h, cv.n)
        if _PHASES < 4:
            for c in range(C):
                QQ("out0").inc(s.dma_start(out=OUT[c * 128:(c + 1) * 128, :], in_=xt_final[:, c * T:(c + 1) * T]), 16)
        for nm_, sm_ in sorted(_sems.items()):
            s.wait_ge(sm_.h, sm_.n)

    if _PHASES >= 3:
        xt3_st.close()
    else:
        xt_st.close()
    st.close()
    return nc


_tables_cache = None


def kernel(**ins):
    global _tables_cache
    if _tables_cache is None:
        _tables_cache = _host_tables()
    tables = _tables_cache
    x = np.asarray(ins["L0_attn_a_u"])
    fW = np.ascontiguousarray(ins["L0_ffn_W"], dtype=np.float32)
    cw = np.ascontiguousarray(ins["L0_attn_b_conv_w"], dtype=np.float32)
    wq = np.ascontiguousarray(ins["L0_attn_b_wq"], dtype=np.float32).reshape(1024, 1024)
    wk = np.ascontiguousarray(ins["L0_attn_b_wk"], dtype=np.float32).reshape(1024, 1024)
    wv = np.ascontiguousarray(ins["L0_attn_b_wv"], dtype=np.float32).reshape(1024, 1024)
    wo = np.ascontiguousarray(ins["L0_attn_b_wo"], dtype=np.float32).reshape(1024, 1024)
    pw = np.ascontiguousarray(ins["L0_attn_b_proj_w"], dtype=np.float32)
    Bp = np.asarray(ins["L0_attn_a_Bp"], dtype=np.float32)
    bpil = np.zeros((128, 2), np.float32)
    bpil[0::2, 0] = Bp[:, 0]
    bpil[1::2, 1] = Bp[:, 0]

    nc = build_nc()
    in_maps = []
    for cid in range(NCORE):
        m = {}
        m["x0T"] = np.ascontiguousarray(x[0, cid * T:(cid + 1) * T, :].T)
        m["aLam"] = np.asarray(ins["L0_attn_a_Lambda_ri"], np.float32)
        m["aP"] = np.asarray(ins["L0_attn_a_p_ri"], np.float32)
        m["aQ"] = np.asarray(ins["L0_attn_a_q_ri"], np.float32)
        m["aVc"] = np.asarray(ins["L0_attn_a_Vc_ri"], np.float32)
        m["aCt"] = np.asarray(ins["L0_attn_a_Ct_ri"], np.float32)
        m["aBpil"] = bpil
        m["aD"] = np.asarray(ins["L0_attn_a_D"], np.float32).reshape(1, 1)
        m["aLS"] = np.asarray(ins["L0_attn_a_log_step"], np.float32).reshape(1, 1)
        m["cwsh"] = np.ascontiguousarray(cw[:, cid * 512:(cid + 1) * 512, :])
        m["cb"] = np.asarray(ins["L0_attn_b_conv_b"], np.float32)
        m["fWsh"] = np.ascontiguousarray(
            np.concatenate([fW[oc * 128:(oc + 1) * 128, :] for oc in (cid, cid + 8, cid + 16, cid + 24)], axis=0))
        m["wqsh"] = np.ascontiguousarray(wq[cid * 128:(cid + 1) * 128, :])
        m["wksh"] = np.ascontiguousarray(wk[cid * 128:(cid + 1) * 128, :])
        m["wvsh"] = np.ascontiguousarray(wv[cid * 128:(cid + 1) * 128, :])
        m["wosh"] = np.ascontiguousarray(wo[cid * 128:(cid + 1) * 128, :])
        m["pwsh"] = np.ascontiguousarray(pw[cid * 128:(cid + 1) * 128, :])
        m["bq"] = np.asarray(ins["L0_attn_b_bq"], np.float32)
        m["bk"] = np.asarray(ins["L0_attn_b_bk"], np.float32)
        m["bv"] = np.asarray(ins["L0_attn_b_bv"], np.float32)
        m["bo"] = np.asarray(ins["L0_attn_b_bo"], np.float32)
        m["pb"] = np.asarray(ins["L0_attn_b_proj_b"], np.float32)
        m["fb"] = np.asarray(ins["L0_ffn_b"], np.float32)
        m["fA"] = np.asarray(ins["L0_ffn_A"], np.float32)
        m["fB"] = np.asarray(ins["L0_ffn_Bmat"], np.float32)
        m["fL"] = np.asarray(ins["L0_ffn_LiMEs"], np.float32)
        oneL = np.zeros((8, 1), np.float32)
        oneR = np.zeros((8, 1), np.float32)
        if cid > 0:
            oneL[cid - 1, 0] = 1.0
        if cid < NCORE - 1:
            oneR[cid + 1, 0] = 1.0
        m["oneL"] = oneL
        m["oneR"] = oneR
        for k, vv in tables.items():
            m[k] = vv
        in_maps.append(m)

    res = run_bass_kernel_spmd(nc, in_maps, core_ids=list(range(NCORE)))
    parts = [np.asarray(res.results[cid]["y"]).T for cid in range(NCORE)]
    out = np.concatenate(parts, axis=0).reshape(1, NCORE * T, D).astype(np.float32)
    return out
